# version 03
# baseline (speedup 1.0000x reference)
import numpy as np
import ml_dtypes
from contextlib import ExitStack
import concourse.bass as bass
import concourse.mybir as mybir
from concourse.bass_utils import run_bass_kernel_spmd
from concourse.bass_core_rust import IndirectOffsetOnAxis

F32 = mybir.dt.float32
BF16 = mybir.dt.bfloat16
AF = mybir.ActivationFunctionType
ALU = mybir.AluOpType
AX = mybir.AxisListType.X
BF16NP = ml_dtypes.bfloat16

NCORES = 8
B, T, D = 16, 2048, 1024
BC = B // NCORES
NT = BC * T
EPS_LN = 1e-5
EPS_ATTN = 1e-6

DEBUG = set()


class Prog:
    ENGS = ("sync", "tensor", "scalar", "vector", "gpsimd")

    def __init__(self, nc, ctx):
        self.nc = nc
        self.ctx = ctx
        self._scopes = []
        self.ops = {e: [] for e in self.ENGS}
        self.semh = {}
        self.cnt = {}
        self.last_inst = {}

    def sem(self, name):
        if name not in self.semh:
            self.semh[name] = self.ctx.enter_context(self.nc.semaphore(name))
            self.cnt[name] = 0
        return self.semh[name]

    def op(self, eng, fn):
        def w(e, fn=fn, eng=eng):
            r = fn(e)
            if r is not None:
                self.last_inst[eng] = r
        self.ops[eng].append(w)

    def wait(self, eng, name, val=None):
        s = self.sem(name)
        v = self.cnt[name] if val is None else val
        self.ops[eng].append(lambda e, s=s, v=v: e.wait_ge(s, v))

    def iop(self, eng, fn, sem, inc=1):
        s = self.sem(sem)
        self.cnt[sem] += inc
        def w(e, fn=fn, s=s, inc=inc, eng=eng):
            r = fn(e)
            r.then_inc(s, inc)
            self.last_inst[eng] = r
        self.ops[eng].append(w)
        return self.cnt[sem]

    def dma(self, q, out, in_, sem, **kw):
        s = self.sem(sem)
        self.cnt[sem] += 16
        def w(e, out=out, in_=in_, s=s, kw=kw):
            e.dma_start(out=out, in_=in_, **kw).then_inc(s, 16)
        self.ops[q].append(w)
        return self.cnt[sem]

    _sn = 0

    def sbuf(self, name, shape, dt):
        Prog._sn += 1
        cm = self.nc.sbuf_tensor(f"{name}_s{Prog._sn}", shape, dt)
        h = cm.__enter__()
        if self._scopes:
            self._scopes[-1].append(cm)
        else:
            self.ctx.callback(cm.__exit__, None, None, None)
        return h

    def push_scope(self):
        if not hasattr(self, "_scopes"):
            self._scopes = []
        self._scopes.append([])

    def pop_scope(self):
        for cm in reversed(self._scopes.pop()):
            cm.__exit__(None, None, None)

    def psum(self, name, shape, dt=F32):
        return self.ctx.enter_context(self.nc.psum_tensor(name, shape, dt))

    _pn = 0

    def palloc(self, name, shape, dt=F32):
        Prog._pn += 1
        cm = self.nc.psum_tensor(f"{name}_{Prog._pn}", shape, dt)
        h = cm.__enter__()
        if not hasattr(self, "_pcms"):
            self._pcms = {}
        self._pcms[id(h)] = cm
        return h

    def flush_dve(self):
        self.op("vector", lambda e: e.drain())

    flush_dve_big = flush_dve

    def pfree(self, *hs):
        for h in reversed(hs):
            self._pcms.pop(id(h)).__exit__(None, None, None)

    def finalize(self):
        block = self.ctx.enter_context(self.nc.Block())
        for eng in self.ENGS:
            getattr(block, eng)(lambda e, eng=eng: [fn(e) for fn in self.ops[eng]])



def const_inputs(nc):
    c = {}
    c["ones128_f32"] = nc.dram_tensor("ones128_f32", [128, 1], F32, kind="ExternalInput").ap()
    c["ones1x128_bf"] = nc.dram_tensor("ones1x128_bf", [1, 128], BF16, kind="ExternalInput").ap()
    c["ones128_bf"] = nc.dram_tensor("ones128_bf", [128, 1], BF16, kind="ExternalInput").ap()
    c["ident_bf"] = nc.dram_tensor("ident_bf", [128, 128], BF16, kind="ExternalInput").ap()
    c["triu_bf"] = nc.dram_tensor("triu_bf", [128, 128], BF16, kind="ExternalInput").ap()
    c["cvals"] = nc.dram_tensor("cvals", [128, 8], F32, kind="ExternalInput").ap()
    c["esel"] = nc.dram_tensor("esel", [8, 8 * 128], BF16, kind="ExternalInput").ap()
    c["strictU_bf"] = nc.dram_tensor("strictU_bf", [128, 128], BF16, kind="ExternalInput").ap()
    c["iota_mat"] = nc.dram_tensor("iota_mat", [128, 32], F32, kind="ExternalInput").ap()
    c["ident_f32"] = nc.dram_tensor("ident_f32", [128, 128], F32, kind="ExternalInput").ap()
    c["ones1x128_f32"] = nc.dram_tensor("ones1x128_f32", [1, 128], F32, kind="ExternalInput").ap()
    return c


def const_arrays():
    a = {}
    a["ones128_f32"] = np.ones((128, 1), np.float32)
    a["ones1x128_bf"] = np.ones((1, 128), BF16NP)
    a["ones128_bf"] = np.ones((128, 1), BF16NP)
    a["ident_bf"] = np.eye(128).astype(BF16NP)
    a["triu_bf"] = np.triu(np.ones((128, 128))).astype(BF16NP)
    cv = np.zeros((128, 8), np.float32)
    cv[:, 0] = 1e-5
    cv[:, 1] = 1e-4
    cv[:, 2] = 1e-6
    cv[:, 3] = 1.0
    a["cvals"] = cv
    es = np.zeros((8, 8 * 128), np.float32)
    for e in range(8):
        es[e, e * 128:(e + 1) * 128] = 1.0
    a["esel"] = es.astype(BF16NP)
    a["strictU_bf"] = np.triu(np.ones((128, 128)), 1).astype(BF16NP)
    a["iota_mat"] = (np.arange(128)[:, None] + 128 * np.arange(32)[None, :]).astype(np.float32)
    a["ident_f32"] = np.eye(128, dtype=np.float32)
    a["ones1x128_f32"] = np.ones((1, 128), np.float32)
    return a


def load_consts(P, c):
    s = {}
    s["ones_f32"] = P.sbuf("c_ones_f32", [128, 1], F32)
    s["ones1x128"] = P.sbuf("c_ones1x128", [1, 128], BF16)
    s["ones_bf"] = P.sbuf("c_ones_bf", [128, 1], BF16)
    s["ident"] = P.sbuf("c_ident", [128, 128], BF16)
    s["triu"] = P.sbuf("c_triu", [128, 128], BF16)
    s["cvals"] = P.sbuf("c_cvals", [128, 8], F32)
    s["esel"] = P.sbuf("c_esel", [8, 8 * 128], BF16)
    s["strictU"] = P.sbuf("c_strictU", [128, 128], BF16)
    s["iota"] = P.sbuf("c_iota", [128, 32], F32)
    s["ident_f32"] = P.sbuf("c_ident_f32", [128, 128], F32)
    s["ones1x128_f32"] = P.sbuf("c_ones1x128_f32", [1, 128], F32)
    P.dma("gpsimd", s["ones_f32"][:], c["ones128_f32"], "cld")
    P.dma("gpsimd", s["ones1x128"][:], c["ones1x128_bf"], "cld")
    P.dma("gpsimd", s["ones_bf"][:], c["ones128_bf"], "cld")
    P.dma("gpsimd", s["ident"][:], c["ident_bf"], "cld")
    P.dma("gpsimd", s["triu"][:], c["triu_bf"], "cld")
    P.dma("gpsimd", s["cvals"][:], c["cvals"], "cld")
    P.dma("gpsimd", s["esel"][:], c["esel"], "cld")
    P.dma("gpsimd", s["strictU"][:], c["strictU_bf"], "cld")
    P.dma("gpsimd", s["iota"][:], c["iota_mat"], "cld")
    P.dma("gpsimd", s["ident_f32"][:], c["ident_f32"], "cld")
    P.dma("gpsimd", s["ones1x128_f32"][:], c["ones1x128_f32"], "cld")
    for e in Prog.ENGS:
        P.wait(e, "cld")
    return s



class LNState:
    def __init__(self, P):
        self.xt = P.sbuf("ln_xt", [128, 8, 512], BF16)
        self.xsq = P.sbuf("ln_xsq", [128, 8, 512], BF16)
        self.mu = P.sbuf("ln_mu", [1, 512], F32)
        self.e2 = P.sbuf("ln_e2", [1, 512], F32)
        self.musq = P.sbuf("ln_musq", [1, 512], F32)
        self.var = P.sbuf("ln_var", [1, 512], F32)
        self.srt = P.sbuf("ln_srt", [1, 512], F32)
        self.rstd = P.sbuf("ln_rstd", [1, 512], F32)
        self.nb = P.sbuf("ln_nb", [1, 512], F32)
        self.rp = P.sbuf("ln_rp", [1, 1024], BF16)
        self.tmp = P.sbuf("ln_tmp", [128, 512], F32)
        self.ps_sum = None


def emit_ln(P, L, cs, X, xln, x_cols, xln_base=0, gcol=None, bcol=None):
    start, n = x_cols
    assert n % 512 == 0
    L.ps_sum = P.palloc("ln_ps_sum", [1, 512], F32)
    L.ps_sumsq = P.palloc("ln_ps_sumsq", [1, 512], F32)
    L.ps_bc0 = P.palloc("ln_ps_bc0", [128, 512], F32)
    L.ps_bc1 = P.palloc("ln_ps_bc1", [128, 512], F32)
    for t0 in range(start, start + n, 512):
        cols = slice(t0, t0 + 512)
        P.dma("gpsimd", L.xt[:], X.rearrange("(c p) n -> p c n", p=128)[:, :, cols], "xl")
        P.wait("scalar", "xl")
        P.wait("scalar", "pe")
        for c in range(8):
            P.iop("scalar", lambda e, c=c: e.activation(
                out=L.xsq[:, c, :], in_=L.xt[:, c, :], func=AF.Square), "act")
        P.wait("tensor", "act")
        P.wait("tensor", "dve")
        for c in range(8):
            P.op("tensor", lambda e, c=c: e.matmul(
                L.ps_sum[:], cs["ones_bf"][:], L.xt[:, c, :],
                start=(c == 0), stop=(c == 7)))
        for c in range(8):
            fn = lambda e, c=c: e.matmul(
                L.ps_sumsq[:], cs["ones_bf"][:], L.xsq[:, c, :],
                start=(c == 0), stop=(c == 7))
            if c == 7:
                P.iop("tensor", fn, "pe")
            else:
                P.op("tensor", fn)
        P.wait("vector", "pe")
        P.op("vector", lambda e: e.tensor_scalar(
            out=L.mu[:], in0=L.ps_sum[:], scalar1=1.0 / 1024, scalar2=None, op0=ALU.mult))
        P.op("vector", lambda e: e.tensor_scalar(
            out=L.e2[:], in0=L.ps_sumsq[:], scalar1=1.0 / 1024, scalar2=None, op0=ALU.mult))
        P.op("vector", lambda e: e.tensor_tensor(
            out=L.musq[:], in0=L.mu[:], in1=L.mu[:], op=ALU.mult))
        P.op("vector", lambda e: e.drain())
        P.iop("vector", lambda e: e.tensor_tensor(
            out=L.var[:], in0=L.e2[:], in1=L.musq[:], op=ALU.subtract), "dve")
        P.wait("scalar", "dve")
        P.iop("scalar", lambda e: e.activation(
            out=L.srt[:], in_=L.var[:], func=AF.Sqrt, bias=cs["cvals"][0:1, 0:1]), "act")
        P.wait("vector", "act")
        P.flush_dve()
        P.iop("vector", lambda e: e.reciprocal(out=L.rstd[:], in_=L.srt[:]), "dve")
        P.flush_dve()
        P.op("vector", lambda e: e.scalar_tensor_tensor(
            out=L.nb[:], in0=L.mu[:], scalar=-1.0, in1=L.rstd[:],
            op0=ALU.mult, op1=ALU.mult))
        P.op("vector", lambda e: e.drain())
        P.op("vector", lambda e: e.tensor_copy(out=L.rp[0:1, 0:512], in_=L.rstd[:]))
        P.iop("vector", lambda e: e.tensor_copy(out=L.rp[0:1, 512:1024], in_=L.nb[:]), "dve")
        P.wait("tensor", "dve")
        P.op("tensor", lambda e: e.matmul(
            L.ps_bc0[:], cs["ones1x128"][:], L.rp[0:1, 0:512], start=True, stop=True))
        P.iop("tensor", lambda e: e.matmul(
            L.ps_bc1[:], cs["ones1x128"][:], L.rp[0:1, 512:1024],
            start=True, stop=True), "pe")
        P.wait("vector", "pe")
        oc = slice(t0 - xln_base, t0 - xln_base + 512)
        for c in range(8):
            P.op("vector", lambda e, c=c: e.tensor_tensor(
                out=L.tmp[:], in0=L.xt[:, c, :], in1=L.ps_bc0[:], op=ALU.mult))
            if gcol is None:
                P.iop("vector", lambda e, c=c, oc=oc: e.tensor_tensor(
                    out=xln[:, c, oc], in0=L.tmp[:], in1=L.ps_bc1[:],
                    op=ALU.add), "dve")
            else:
                P.op("vector", lambda e, c=c: e.tensor_tensor(
                    out=L.tmp[:], in0=L.tmp[:], in1=L.ps_bc1[:], op=ALU.add))
                P.iop("vector", lambda e, c=c, oc=oc: e.tensor_scalar(
                    out=xln[:, c, oc], in0=L.tmp[:], scalar1=gcol[:, c:c + 1],
                    scalar2=bcol[:, c:c + 1], op0=ALU.mult, op1=ALU.add), "dve")
        P.wait("gpsimd", "dve")
        P.wait("gpsimd", "pe")
    P.pfree(L.ps_sum, L.ps_sumsq, L.ps_bc0, L.ps_bc1)



def phase1_dram(nc):
    io = {}
    io["wqT"] = nc.dram_tensor("p1_wqT", [D, D], F32, kind="ExternalInput").ap()
    io["wkT"] = nc.dram_tensor("p1_wkT", [D, D], F32, kind="ExternalInput").ap()
    io["wvT"] = nc.dram_tensor("p1_wvT", [D, D], F32, kind="ExternalInput").ap()
    io["wpT"] = nc.dram_tensor("p1_wpT", [D, D], F32, kind="ExternalInput").ap()
    io["bp"] = nc.dram_tensor("p1_bp", [128, 8], F32, kind="ExternalInput").ap()
    return io


def phase1_host(inputs):
    h = {}
    h["p1_wqT"] = np.ascontiguousarray(inputs["L0_attn_a_Wq"].T)
    h["p1_wkT"] = np.ascontiguousarray(inputs["L0_attn_a_Wk"].T)
    h["p1_wvT"] = np.ascontiguousarray(inputs["L0_attn_a_Wv"].T)
    h["p1_wpT"] = np.ascontiguousarray(inputs["L0_attn_a_Wp"].T)
    h["p1_bp"] = np.ascontiguousarray(inputs["L0_attn_a_bp"].reshape(8, 128).T)
    return h


def dbg_dump(P, tag, ap, hard=False):
    if tag not in DEBUG:
        return
    sh = list(ap.shape)
    t = P.nc.dram_tensor("dbg_" + tag, sh, ap.dtype, kind="ExternalOutput").ap()
    for q in ("gpsimd",):
        P.wait(q, "dve")
        P.wait(q, "pe")
        P.wait(q, "act")
    P.dma("gpsimd", t, ap, "st")
    if hard:
        for e in Prog.ENGS:
            P.wait(e, "st")


def emit_phase1(P, cs, io):
    nc = P.nc
    X, Xo = io["X"], io["Xo"]
    L = LNState(P)
    xln = P.sbuf("p1_xln", [128, 8, T], BF16)
    phiq = P.sbuf("p1_phiq", [128, 8, T], BF16)
    phik = P.sbuf("p1_phik", [128, 16, D], BF16)
    vtm = P.sbuf("p1_vtm", [128, 16, D], BF16)
    wblk = P.sbuf("p1_wblk", [128, 8, 128], BF16)
    wrhs = P.sbuf("p1_wrhs", [128, 8, 512], BF16)
    wrhs2 = P.sbuf("p1_wrhs2", [128, 8, 512], BF16)
    bp_sb = P.sbuf("p1_bp_sb", [128, 8], F32)
    t1 = P.sbuf("p1_t1", [128, 512], F32)
    t2 = P.sbuf("p1_t2", [128, 512], F32)
    t3 = P.sbuf("p1_t3", [128, 512], F32)
    S_sb = P.sbuf("p1_S", [64, 65], F32)
    S_bf = P.sbuf("p1_Sbf", [64, 65], BF16)
    kT_sb = P.sbuf("p1_kT", [64, 128], BF16)
    A_sb = P.sbuf("p1_A", [128, 128], BF16)
    den_sb = P.sbuf("p1_den", [128, 1], F32)
    rec_sb = P.sbuf("p1_rec", [128, 1], F32)
    ytm_sb = P.sbuf("p1_ytm", [128, 64], BF16)
    q_s = P.sbuf("p1_qs", [64, 128], BF16)
    res_sb = P.sbuf("p1_res", [128, 512], F32)
    out_sb = P.sbuf("p1_out", [128, 512], F32)

    P.dma("gpsimd", bp_sb[:], io["bp"], "xl")

    def phi_evac(psrc, dst):
        P.op("vector", lambda e: e.tensor_scalar(
            out=t1[:], in0=psrc, scalar1=0.0, scalar2=None, op0=ALU.min))
        P.iop("vector", lambda e: e.tensor_scalar(
            out=t3[:], in0=psrc, scalar1=0.0, scalar2=None, op0=ALU.max), "dve")
        P.wait("scalar", "dve")
        P.iop("scalar", lambda e: e.activation(out=t2[:], in_=t1[:], func=AF.Exp), "act")
        P.wait("vector", "act")
        P.op("vector", lambda e: e.drain())
        P.iop("vector", lambda e: e.tensor_tensor(
            out=dst, in0=t2[:], in1=t3[:], op=ALU.add), "dve")

    for s in range(BC):
        sbase = s * T
        emit_ln(P, L, cs, X, xln, (sbase, T), xln_base=sbase)
        P.wait("tensor", "dve")
        if s == 0:
            dbg_dump(P, "xln_s0", xln[:])

        ps_q = P.palloc("p1_ps_q", [128, 512], F32)
        for f in range(8):
            P.wait("gpsimd", "pe")
            P.dma("gpsimd", wblk[:],
                  io["wqT"].rearrange("(c p) f -> p c f", p=128)[:, :, f * 128:(f + 1) * 128],
                  "w")
            P.wait("tensor", "w")
            for g in range(4):
                gc = slice(g * 512, (g + 1) * 512)
                for k in range(8):
                    fn = lambda e, k=k, gc=gc: e.matmul(
                        ps_q[:], wblk[:, k, :], xln[:, k, gc],
                        start=(k == 0), stop=(k == 7))
                    if k == 7:
                        P.iop("tensor", fn, "pe")
                    else:
                        P.op("tensor", fn)
                P.wait("vector", "pe")
                phi_evac(ps_q[:], phiq[:, f, gc])
                P.wait("tensor", "dve")

        P.pfree(ps_q)
        ps_k = P.palloc("p1_ps_k", [128, 512], F32)
        ps_v = P.palloc("p1_ps_v", [128, 512], F32)
        for fg in range(2):
            fcols = slice(fg * 512, (fg + 1) * 512)
            P.wait("gpsimd", "pe")
            P.dma("gpsimd", wrhs[:],
                  io["wkT"].rearrange("(c p) f -> p c f", p=128)[:, :, fcols], "w")
            P.dma("gpsimd", wrhs2[:],
                  io["wvT"].rearrange("(c p) f -> p c f", p=128)[:, :, fcols], "w")
            P.wait("tensor", "w")
            for tc in range(16):
                tcs = slice(tc * 128, tc * 128 + 128)
                for k in range(8):
                    P.op("tensor", lambda e, k=k, tcs=tcs: e.matmul(
                        ps_k[:], xln[:, k, tcs], wrhs[:, k, :],
                        start=(k == 0), stop=(k == 7)))
                for k in range(8):
                    fn = lambda e, k=k, tcs=tcs: e.matmul(
                        ps_v[:], xln[:, k, tcs], wrhs2[:, k, :],
                        start=(k == 0), stop=(k == 7))
                    if k == 7:
                        P.iop("tensor", fn, "pe")
                    else:
                        P.op("tensor", fn)
                P.wait("vector", "pe")
                phi_evac(ps_k[:], phik[:, tc, fcols])
                P.iop("vector", lambda e, tc=tc, fcols=fcols: e.tensor_copy(
                    out=vtm[:, tc, fcols], in_=ps_v[:]), "dve")
                P.wait("tensor", "dve")

        P.pfree(ps_k, ps_v)
        if s == 0:
            dbg_dump(P, "phiq_s0", phiq[:])
            dbg_dump(P, "phik_s0", phik[:])
            dbg_dump(P, "vtm_s0", vtm[:])
        ps_kT = P.palloc("p1_ps_kT", [64, 128], BF16)
        ps_A = P.palloc("p1_ps_A", [128, 128], F32)
        ps_y = P.palloc("p1_ps_y", [128, 64], F32)
        ps_den = P.palloc("p1_ps_den", [128, 1], F32)
        ps_kv = P.palloc("p1_ps_kv", [64, 64], F32)
        ps_zk = P.palloc("p1_ps_zk", [64, 1], F32)
        ps_yT = P.palloc("p1_ps_yT", [64, 128], BF16)
        for h in range(16):
            hoff = (h % 2) * 64
            hch = h // 2
            hf = slice(h * 64, h * 64 + 64)
            P.op("vector", lambda e: e.memset(S_sb[:], 0.0))
            P.iop("vector", lambda e: e.memset(S_bf[:], 0.0), "dve")
            P.wait("tensor", "dve")
            for n in range(16):
                ncs = slice(n * 128, n * 128 + 128)
                qblk = phiq[hoff:hoff + 64, hch, ncs]
                kblk = phik[:, n, hf]
                vblk = vtm[:, n, hf]
                P.iop("tensor", lambda e, kblk=kblk: e.transpose(
                    ps_kT[:], kblk, cs["ident"][:]), "pe")
                P.wait("vector", "pe")
                P.op("vector", lambda e, qblk=qblk: e.tensor_copy(out=q_s[:], in_=qblk))
                P.iop("vector", lambda e: e.tensor_copy(out=kT_sb[:], in_=ps_kT[:]), "dve")
                P.wait("tensor", "dve")
                if s == 0 and h == 0 and n <= 1:
                    dbg_dump(P, f"kT_n{n}", kT_sb[:], hard=True)
                    dbg_dump(P, f"qs_n{n}", q_s[:], hard=True)
                P.iop("tensor", lambda e: e.matmul(
                    ps_A[:], kT_sb[:], q_s[:], start=True, stop=True), "pe")
                P.wait("vector", "pe")
                P.iop("vector", lambda e: e.tensor_tensor(
                    out=A_sb[:], in0=ps_A[:], in1=cs["triu"][:], op=ALU.mult), "dve")
                P.wait("tensor", "dve")
                if s == 0 and h == 0 and n <= 1:
                    dbg_dump(P, f"A_n{n}", A_sb[:], hard=True)
                P.op("tensor", lambda e: e.matmul(
                    ps_y[:], q_s[:], S_bf[:, 0:64], start=True, stop=False))
                P.op("tensor", lambda e, vblk=vblk: e.matmul(
                    ps_y[:], A_sb[:], vblk, start=False, stop=True))
                P.op("tensor", lambda e: e.matmul(
                    ps_den[:], q_s[:], S_bf[:, 64:65], start=True, stop=False))
                P.iop("tensor", lambda e: e.matmul(
                    ps_den[:], A_sb[:], cs["ones_bf"][:], start=False, stop=True), "pe")
                P.op("tensor", lambda e, kblk=kblk, vblk=vblk: e.matmul(
                    ps_kv[:], kblk, vblk, start=True, stop=True))
                P.iop("tensor", lambda e, kblk=kblk: e.matmul(
                    ps_zk[:], kblk, cs["ones_bf"][:], start=True, stop=True), "pe")
                P.wait("vector", "pe")
                P.op("vector", lambda e: e.tensor_scalar(
                    out=den_sb[:], in0=ps_den[:], scalar1=EPS_ATTN, scalar2=None,
                    op0=ALU.add))
                P.flush_dve()
                P.iop("vector", lambda e: e.reciprocal(out=rec_sb[:], in_=den_sb[:]), "dve")
                P.flush_dve()
                P.iop("vector", lambda e: e.tensor_tensor(
                    out=ytm_sb[:], in0=ps_y[:], in1=rec_sb[:].broadcast_to([128, 64]),
                    op=ALU.mult), "dve")
                if s == 0 and h == 0 and n <= 1:
                    dbg_dump(P, f"den_n{n}", den_sb[:], hard=True)
                    dbg_dump(P, f"ytm_n{n}", ytm_sb[:], hard=True)
                P.op("vector", lambda e: e.tensor_tensor(
                    out=S_sb[:, 0:64], in0=S_sb[:, 0:64], in1=ps_kv[:], op=ALU.add))
                P.op("vector", lambda e: e.tensor_tensor(
                    out=S_sb[:, 64:65], in0=S_sb[:, 64:65], in1=ps_zk[:], op=ALU.add))
                P.op("vector", lambda e: e.drain())
                P.iop("vector", lambda e: e.tensor_copy(out=S_bf[:], in_=S_sb[:]), "dve")
                P.wait("tensor", "dve")
                if s == 0 and h == 0 and n <= 1:
                    dbg_dump(P, f"S_n{n}", S_sb[:], hard=True)
                P.iop("tensor", lambda e: e.transpose(
                    ps_yT[:], ytm_sb[:], cs["ident"][:]), "pe")
                P.wait("vector", "pe")
                P.iop("vector", lambda e, hoff=hoff, hch=hch, ncs=ncs: e.tensor_copy(
                    out=xln[hoff:hoff + 64, hch, ncs], in_=ps_yT[:]), "dve")
                P.wait("tensor", "dve")

        P.pfree(ps_kT, ps_A, ps_y, ps_den, ps_kv, ps_zk, ps_yT)
        if s == 0:
            dbg_dump(P, "Y_s0", xln[:])
        ps_q = P.palloc("p1_ps_o", [128, 512], F32)
        for f in range(8):
            P.wait("gpsimd", "pe")
            P.dma("gpsimd", wblk[:],
                  io["wpT"].rearrange("(c p) f -> p c f", p=128)[:, :, f * 128:(f + 1) * 128],
                  "w")
            P.wait("tensor", "w")
            for g in range(4):
                gc = slice(g * 512, (g + 1) * 512)
                xcols = slice(sbase + g * 512, sbase + g * 512 + 512)
                P.wait("sync", "dve")
                P.dma("sync", res_sb[:], X[f * 128:(f + 1) * 128, xcols], "xr")
                for k in range(8):
                    fn = lambda e, k=k, gc=gc: e.matmul(
                        ps_q[:], wblk[:, k, :], xln[:, k, gc],
                        start=(k == 0), stop=(k == 7))
                    if k == 7:
                        P.iop("tensor", fn, "pe")
                    else:
                        P.op("tensor", fn)
                P.wait("scalar", "pe")
                P.iop("scalar", lambda e, f=f: e.activation(
                    out=t1[:], in_=ps_q[:], func=AF.Identity, bias=bp_sb[:, f:f + 1]), "act")
                P.wait("vector", "act")
                P.wait("vector", "xr")
                P.iop("vector", lambda e: e.tensor_tensor(
                    out=out_sb[:], in0=t1[:], in1=res_sb[:], op=ALU.add), "dve")
                P.wait("gpsimd", "dve")
                P.dma("gpsimd", Xo[f * 128:(f + 1) * 128, xcols], out_sb[:], "st")
                P.wait("vector", "st")
                P.wait("tensor", "dve")

        P.pfree(ps_q)
    P.wait("sync", "st")
    P.wait("gpsimd", "st")




def phase6_dram(nc):
    io = {}
    io["gateT"] = nc.dram_tensor("p6_gateT", [D, 8], F32, kind="ExternalInput").ap()
    io["wgT"] = nc.dram_tensor("p6_wgT", [8, D, 4096], F32, kind="ExternalInput").ap()
    io["wuT"] = nc.dram_tensor("p6_wuT", [8, D, 4096], F32, kind="ExternalInput").ap()
    io["wdT"] = nc.dram_tensor("p6_wdT", [8, 4096, D], F32, kind="ExternalInput").ap()
    return io


def phase6_host(inputs):
    h = {}
    h["p6_gateT"] = np.ascontiguousarray(inputs["L1_ffn_gate_w"].T)
    h["p6_wgT"] = np.ascontiguousarray(np.asarray(inputs["L1_ffn_wg"]).transpose(0, 2, 1))
    h["p6_wuT"] = np.ascontiguousarray(np.asarray(inputs["L1_ffn_wu"]).transpose(0, 2, 1))
    h["p6_wdT"] = np.ascontiguousarray(np.asarray(inputs["L1_ffn_wd"]).transpose(0, 2, 1))
    return h


def emit_phase6(P, cs, io):
    X, Xo = io["X"], io["Xo"]
    L = LNState(P)
    xln = P.sbuf("p6_xln", [128, 8, NT], BF16)
    h_e = P.sbuf("p6_h", [128, 32, 512], BF16)
    acc = P.sbuf("p6_acc", [128, 8, 512], F32)
    wgu = P.sbuf("p6_wgu", [128, 8, 1024], BF16)
    wdb = P.sbuf("p6_wdb", [128, 32, 128], BF16)
    gsb = P.sbuf("p6_gate", [128, 8, 8], BF16)
    logit = P.sbuf("p6_logit", [128, 32, 8], F32)
    m1 = P.sbuf("p6_m1", [128, 32], F32)
    m2 = P.sbuf("p6_m2", [128, 32], F32)
    dd = P.sbuf("p6_dd", [128, 32], F32)
    w1 = P.sbuf("p6_w1", [128, 32], F32)
    w2 = P.sbuf("p6_w2", [128, 32], F32)
    sel1 = P.sbuf("p6_sel1", [128, 8], F32)
    sel2 = P.sbuf("p6_sel2", [128, 8], F32)
    mskd = P.sbuf("p6_mskd", [128, 32, 8], F32)
    c1 = P.sbuf("p6_c1", [128, 8], F32)
    coefb = P.sbuf("p6_coefb", [128, 8], BF16)
    coef_fm = P.sbuf("p6_coef_fm", [8, NT], BF16)
    bc = P.sbuf("p6_bc", [128, 512], BF16)
    t1 = P.sbuf("p6_t1", [128, 512], F32)
    t2 = P.sbuf("p6_t2", [128, 512], F32)
    res = P.sbuf("p6_res", [128, 512], F32)
    osb = P.sbuf("p6_osb", [128, 512], F32)

    emit_ln(P, L, cs, X, xln, (0, NT))
    P.wait("tensor", "dve")

    P.dma("gpsimd", gsb[:], io["gateT"].rearrange("(c p) e -> p c e", p=128), "w")
    P.wait("tensor", "w")
    ps_l = P.palloc("p6_ps_l", [128, 8], F32)
    for tc in range(32):
        tcs = slice(tc * 128, tc * 128 + 128)
        for k in range(8):
            fn = lambda e, k=k, tcs=tcs: e.matmul(
                ps_l[:], xln[:, k, tcs], gsb[:, k, :], start=(k == 0), stop=(k == 7))
            if k == 7:
                P.iop("tensor", fn, "pe")
            else:
                P.op("tensor", fn)
        P.wait("vector", "pe")
        P.iop("vector", lambda e, tc=tc: e.tensor_copy(
            out=logit[:, tc, :], in_=ps_l[:]), "dve")
        P.wait("tensor", "dve")
    P.pfree(ps_l)

    dbg_dump(P, "p6_xln", xln[:], hard=True)
    dbg_dump(P, "p6_logit", logit[:], hard=True)
    P.iop("vector", lambda e: e.reduce_max(m1[:], logit[:], axis=AX), "dve")
    P.flush_dve_big()
    P.wait("vector", "dve")
    for tc in range(32):
        P.op("vector", lambda e, tc=tc: e.tensor_scalar(
            out=sel1[:], in0=logit[:, tc, :], scalar1=m1[:, tc:tc + 1], scalar2=None,
            op0=ALU.is_ge))
        P.op("vector", lambda e: e.drain())
        P.op("vector", lambda e, tc=tc: e.scalar_tensor_tensor(
            out=mskd[:, tc, :], in0=sel1[:], scalar=-1e9, in1=logit[:, tc, :],
            op0=ALU.mult, op1=ALU.add))
        P.iop("vector", lambda e, tc=tc: e.tensor_copy(
            out=logit[:, tc, :], in_=sel1[:]), "dve")
    dbg_dump(P, "p6_sel1", logit[:], hard=True)
    dbg_dump(P, "p6_mskd", mskd[:], hard=True)
    P.iop("vector", lambda e: e.reduce_max(m2[:], mskd[:], axis=AX), "dve")
    P.flush_dve_big()
    P.wait("vector", "dve")
    P.op("vector", lambda e: e.tensor_tensor(out=dd[:], in0=m1[:], in1=m2[:],
                                             op=ALU.subtract))
    P.op("vector", lambda e: e.drain())
    P.iop("vector", lambda e: e.tensor_scalar(
        out=w2[:], in0=dd[:], scalar1=-1.0, scalar2=None, op0=ALU.mult), "dve")
    P.wait("scalar", "dve")
    P.iop("scalar", lambda e: e.activation(out=w1[:], in_=dd[:], func=AF.Sigmoid), "act")
    P.iop("scalar", lambda e: e.activation(out=w2[:], in_=w2[:], func=AF.Sigmoid), "act")
    P.wait("vector", "act")
    ps_ct = P.palloc("p6_ps_ct", [8, 128], BF16)
    for tc in range(32):
        P.op("vector", lambda e, tc=tc: e.tensor_scalar(
            out=sel2[:], in0=mskd[:, tc, :], scalar1=m2[:, tc:tc + 1], scalar2=None,
            op0=ALU.is_ge))
        P.op("vector", lambda e, tc=tc: e.tensor_scalar(
            out=c1[:], in0=logit[:, tc, :], scalar1=w1[:, tc:tc + 1], scalar2=None,
            op0=ALU.mult))
        P.op("vector", lambda e: e.drain())
        P.iop("vector", lambda e, tc=tc: e.scalar_tensor_tensor(
            out=coefb[:], in0=sel2[:], scalar=w2[:, tc:tc + 1], in1=c1[:],
            op0=ALU.mult, op1=ALU.add), "dve")
        P.wait("tensor", "dve")
        P.iop("tensor", lambda e: e.transpose(ps_ct[:], coefb[:], cs["ident"][:]), "pe")
        P.wait("vector", "pe")
        P.iop("vector", lambda e, tc=tc: e.tensor_copy(
            out=coef_fm[:, tc * 128:tc * 128 + 128], in_=ps_ct[:]), "dve")
        P.wait("tensor", "dve")
    P.pfree(ps_ct)

    ps_g = [P.palloc("p6_ps_g0", [128, 512], F32), P.palloc("p6_ps_g1", [128, 512], F32)]
    ps_u = [P.palloc("p6_ps_u0", [128, 512], F32), P.palloc("p6_ps_u1", [128, 512], F32)]
    ps_d = [P.palloc("p6_ps_d0", [128, 512], F32), P.palloc("p6_ps_d1", [128, 512], F32)]
    dbg_dump(P, "p6_coef", coef_fm[:])
    dbg_dump(P, "p6_m1", m1[:])
    dbg_dump(P, "p6_m2", m2[:])
    dbg_dump(P, "p6_w1", w1[:])
    dbg_dump(P, "p6_w2", w2[:])
    ps_bc = P.palloc("p6_ps_bc", [128, 512], F32)
    for ti in range(8):
        tcols = slice(ti * 512, ti * 512 + 512)
        for e_i in range(8):
            P.wait("tensor", "act")
            P.iop("tensor", lambda e, e_i=e_i, tcols=tcols: e.matmul(
                ps_bc[:], cs["esel"][:, e_i * 128:(e_i + 1) * 128],
                coef_fm[:, tcols], start=True, stop=True), "pe")
            P.wait("scalar", "pe")
            P.wait("scalar", "dve")
            P.iop("scalar", lambda e: e.activation(
                out=bc[:], in_=ps_bc[:], func=AF.Copy), "act")
            for fb in range(8):
                P.wait("gpsimd", "pe")
                P.wait("gpsimd", "act")
                P.dma("gpsimd", wgu[:, :, 0:512],
                      io["wgT"][e_i].rearrange("(c p) f -> p c f", p=128)[
                          :, :, fb * 512:(fb + 1) * 512], "w")
                P.dma("gpsimd", wgu[:, :, 512:1024],
                      io["wuT"][e_i].rearrange("(c p) f -> p c f", p=128)[
                          :, :, fb * 512:(fb + 1) * 512], "w")
                P.wait("tensor", "w")
                for fc in range(4):
                    pg = ps_g[fc % 2]
                    pu = ps_u[fc % 2]
                    fpc = fb * 4 + fc
                    if fb + fc > 0:
                        P.wait("tensor", "dve")
                    for k in range(8):
                        P.op("tensor", lambda e, k=k, fc=fc, tcols=tcols, pg=pg: e.matmul(
                            pg[:], wgu[:, k, fc * 128:fc * 128 + 128], xln[:, k, tcols],
                            start=(k == 0), stop=(k == 7)))
                    for k in range(8):
                        fn = lambda e, k=k, fc=fc, tcols=tcols, pu=pu: e.matmul(
                            pu[:], wgu[:, k, 512 + fc * 128:512 + fc * 128 + 128],
                            xln[:, k, tcols], start=(k == 0), stop=(k == 7))
                        if k == 7:
                            P.iop("tensor", fn, "pe")
                        else:
                            P.op("tensor", fn)
                    P.wait("scalar", "pe")
                    P.wait("scalar", "dve")
                    P.iop("scalar", lambda e, pg=pg: e.activation(
                        out=t1[:], in_=pg[:], func=AF.Silu), "act")
                    P.wait("vector", "act")
                    P.op("vector", lambda e, pu=pu: e.tensor_tensor(
                        out=t2[:], in0=pu[:], in1=bc[:], op=ALU.mult))
                    P.op("vector", lambda e: e.drain())
                    P.iop("vector", lambda e, fpc=fpc: e.tensor_tensor(
                        out=h_e[:, fpc, :], in0=t1[:], in1=t2[:], op=ALU.mult), "dve")
            for fo in range(8):
                pd = ps_d[fo % 2]
                P.wait("gpsimd", "pe")
                P.dma("gpsimd", wdb[:],
                      io["wdT"][e_i].rearrange("(c p) f -> p c f", p=128)[
                          :, :, fo * 128:(fo + 1) * 128], "w")
                P.wait("tensor", "w")
                if fo > 1:
                    P.wait("tensor", "dve")
                for k in range(32):
                    fn = lambda e, k=k, pd=pd: e.matmul(
                        pd[:], wdb[:, k, :], h_e[:, k, :], start=(k == 0), stop=(k == 31))
                    if k == 31:
                        P.iop("tensor", fn, "pe")
                    else:
                        P.op("tensor", fn)
                P.wait("vector", "pe")
                if e_i == 0:
                    P.iop("vector", lambda e, fo=fo, pd=pd: e.tensor_copy(
                        out=acc[:, fo, :], in_=pd[:]), "dve")
                else:
                    P.iop("vector", lambda e, fo=fo, pd=pd: e.tensor_tensor(
                        out=acc[:, fo, :], in0=acc[:, fo, :], in1=pd[:], op=ALU.add), "dve")
        for fo in range(8):
            P.wait("sync", "dve")
            P.dma("sync", res[:], X[fo * 128:(fo + 1) * 128, tcols], "xr")
            P.wait("vector", "xr")
            P.wait("vector", "st")
            P.iop("vector", lambda e, fo=fo: e.tensor_tensor(
                out=osb[:], in0=acc[:, fo, :], in1=res[:], op=ALU.add), "dve")
            P.wait("gpsimd", "dve")
            P.dma("gpsimd", Xo[fo * 128:(fo + 1) * 128, tcols], osb[:], "st")
    P.pfree(ps_g[0], ps_g[1], ps_u[0], ps_u[1], ps_d[0], ps_d[1], ps_bc)
    P.wait("sync", "st")
    P.wait("gpsimd", "st")






def phase7_dram(nc):
    io = {}
    io["winT"] = nc.dram_tensor("p7_winT", [D, 4096], F32, kind="ExternalInput").ap()
    io["woT"] = nc.dram_tensor("p7_woT", [D, D], F32, kind="ExternalInput").ap()
    io["b_in"] = nc.dram_tensor("p7_b_in", [128, 32], F32, kind="ExternalInput").ap()
    io["b_out"] = nc.dram_tensor("p7_b_out", [128, 8], F32, kind="ExternalInput").ap()
    io["ones_row"] = nc.dram_tensor("p7_ones_row", [1, T], BF16, kind="ExternalInput").ap()
    return io


def phase7_host(inputs):
    h = {}
    h["p7_winT"] = np.ascontiguousarray(np.asarray(inputs["L1_attn_b_W_in"], np.float32).T)
    h["p7_woT"] = np.ascontiguousarray(np.asarray(inputs["L1_attn_b_W_out"], np.float32).T)
    h["p7_b_in"] = np.ascontiguousarray(
        np.asarray(inputs["L1_attn_b_b_in"], np.float32).reshape(32, 128).T)
    h["p7_b_out"] = np.ascontiguousarray(
        np.asarray(inputs["L1_attn_b_b_out"], np.float32).reshape(8, 128).T)
    h["p7_ones_row"] = np.ones((1, T), BF16NP)
    return h


def emit_phase7(P, cs, io):
    X, Xo = io["X"], io["Xo"]
    SHEP_EPS_COL = 1
    L = LNState(P)
    xln = P.sbuf("p7_xln", [128, 8, T], BF16)
    Xn = P.sbuf("p7_Xn", [128, 8, T], BF16)
    yfm = P.sbuf("p7_yfm", [128, 8, T], BF16)
    qk = P.sbuf("p7_qk", [128, T], BF16)
    vg = P.sbuf("p7_vg", [128, T], BF16)
    n2K = P.sbuf("p7_n2K", [64, T], BF16)
    Ksq = P.sbuf("p7_Ksq", [64, T], BF16)
    k2row = P.sbuf("p7_k2row", [1, T], BF16)
    q2row = P.sbuf("p7_q2row", [1, T], BF16)
    onesr = P.sbuf("p7_onesr", [1, T], BF16)
    Wk_sb = [P.sbuf("p7_Wk0", [128, T], BF16), P.sbuf("p7_Wk1", [128, T], BF16)]
    vtm = P.sbuf("p7_vtm", [128, 16, 65], BF16)
    s1 = P.sbuf("p7_s1", [128, 512], F32)
    s2 = P.sbuf("p7_s2", [128, 512], F32)
    lr = P.sbuf("p7_lr", [1, T], F32)
    recr = P.sbuf("p7_recr", [1, T], BF16)
    bc = P.sbuf("p7_bc", [64, 512], BF16)
    tsc = P.sbuf("p7_tsc", [128, 512], F32)
    wblk = P.sbuf("p7_wblk", [128, 8, 128], BF16)
    b_in = P.sbuf("p7_b_in_sb", [128, 32], F32)
    b_out = P.sbuf("p7_b_out_sb", [128, 8], F32)
    t1 = P.sbuf("p7_t1", [128, 512], F32)
    res = P.sbuf("p7_res", [128, 512], F32)
    osb = P.sbuf("p7_osb", [128, 512], F32)

    P.dma("gpsimd", b_in[:], io["b_in"], "w")
    P.dma("gpsimd", b_out[:], io["b_out"], "w")
    P.dma("gpsimd", onesr[:], io["ones_row"], "w")
    P.wait("vector", "w")
    P.iop("vector", lambda e: e.memset(vtm[:, :, 64:65], 1.0), "dve")

    for s in range(BC):
        sbase = s * T
        emit_ln(P, L, cs, X, xln, (sbase, T), xln_base=sbase)
        P.wait("tensor", "dve")
        for tt in range(4):
            emit_ln_sbuf(P, L, cs,
                         xln[:, :, tt * 512:(tt + 1) * 512],
                         Xn[:, :, tt * 512:(tt + 1) * 512])
        P.wait("tensor", "dve")

        for h in range(16):
            hb = (h % 2) * 64
            ps_p = [P.palloc("p7_ps_p0", [128, 512], F32),
                    P.palloc("p7_ps_p1", [128, 512], F32)]
            for half, dst in ((0, qk), (1, vg)):
                fch = 2 * h + half
                P.wait("gpsimd", "pe")
                P.dma("gpsimd", wblk[:],
                      io["winT"].rearrange("(c p) f -> p c f", p=128)[
                          :, :, fch * 128:(fch + 1) * 128], "w")
                P.wait("tensor", "w")
                for g in range(4):
                    gc = slice(g * 512, (g + 1) * 512)
                    pin = ps_p[g % 2]
                    if not (half == 0 and g == 0):
                        P.wait("tensor", "act")
                    for k in range(8):
                        fn = lambda e, k=k, gc=gc, pin=pin: e.matmul(
                            pin[:], wblk[:, k, :], Xn[:, k, gc],
                            start=(k == 0), stop=(k == 7))
                        if k == 7:
                            P.iop("tensor", fn, "pe")
                        else:
                            P.op("tensor", fn)
                    P.wait("scalar", "pe")
                    P.iop("scalar", lambda e, gc=gc, fch=fch, dst=dst, pin=pin: e.activation(
                        out=dst[:, gc], in_=pin[:], func=AF.Identity,
                        bias=b_in[:, fch:fch + 1]), "act")
            P.wait("vector", "act")
            P.pfree(ps_p[0], ps_p[1])

            P.iop("vector", lambda e: e.tensor_scalar(
                out=n2K[:], in0=qk[64:128, :], scalar1=-2.0, scalar2=None,
                op0=ALU.mult), "dve")
            P.wait("scalar", "dve")
            P.iop("scalar", lambda e: e.activation(
                out=Ksq[:], in_=qk[64:128, :], func=AF.Square), "act")
            ps_r = P.palloc("p7_ps_r", [1, 512], F32)
            P.wait("tensor", "act")
            for g in range(4):
                gc = slice(g * 512, (g + 1) * 512)
                P.iop("tensor", lambda e, gc=gc: e.matmul(
                    ps_r[:], cs["ones_bf"][0:64, :], Ksq[:, gc],
                    start=True, stop=True), "pe")
                P.wait("vector", "pe")
                P.iop("vector", lambda e, gc=gc: e.tensor_copy(
                    out=k2row[0:1, gc], in_=ps_r[:]), "dve")
                P.wait("tensor", "dve")
            P.wait("scalar", "dve")
            P.iop("scalar", lambda e: e.activation(
                out=Ksq[:], in_=qk[0:64, :], func=AF.Square), "act")
            P.wait("tensor", "act")
            for g in range(4):
                gc = slice(g * 512, (g + 1) * 512)
                P.iop("tensor", lambda e, gc=gc: e.matmul(
                    ps_r[:], cs["ones_bf"][0:64, :], Ksq[:, gc],
                    start=True, stop=True), "pe")
                P.wait("vector", "pe")
                P.iop("vector", lambda e, gc=gc: e.tensor_copy(
                    out=q2row[0:1, gc], in_=ps_r[:]), "dve")
                P.wait("tensor", "dve")
            P.pfree(ps_r)

            ps_t = P.palloc("p7_ps_t", [128, 64], BF16)
            for tb in range(16):
                P.iop("tensor", lambda e, tb=tb: e.transpose(
                    ps_t[:], vg[0:64, tb * 128:(tb + 1) * 128],
                    cs["ident"][0:64, 0:64]), "pe")
                P.wait("vector", "pe")
                P.iop("vector", lambda e, tb=tb: e.tensor_copy(
                    out=vtm[:, tb, 0:64], in_=ps_t[:]), "dve")
                P.wait("tensor", "dve")
            P.pfree(ps_t)

            ps_pv = [P.palloc(f"p7_ps_pv{g}", [65, 512], F32) for g in range(4)]
            ps_s = [P.palloc("p7_ps_s0", [128, 512], F32),
                    P.palloc("p7_ps_s1", [128, 512], F32)]
            for kcb in range(16):
                wcur = Wk_sb[kcb % 2]
                kcs = slice(kcb * 128, kcb * 128 + 128)
                for g in range(4):
                    gc = slice(g * 512, (g + 1) * 512)
                    pss = ps_s[g % 2]
                    if kcb + g > 0:
                        P.wait("tensor", "dve")
                    P.op("tensor", lambda e, kcs=kcs, gc=gc, pss=pss: e.matmul(
                        pss[:], n2K[:, kcs], qk[0:64, gc], start=True, stop=False))
                    P.op("tensor", lambda e, kcs=kcs, gc=gc, pss=pss: e.matmul(
                        pss[:], k2row[0:1, kcs], onesr[0:1, gc], start=False, stop=False))
                    P.iop("tensor", lambda e, kcs=kcs, gc=gc, pss=pss: e.matmul(
                        pss[:], cs["ones1x128"][:], q2row[0:1, gc],
                        start=False, stop=True), "pe")
                    P.wait("vector", "pe")
                    P.iop("vector", lambda e, pss=pss: e.tensor_scalar(
                        out=s1[:], in0=pss[:], scalar1=0.0, scalar2=None,
                        op0=ALU.max), "dve")
                    P.wait("scalar", "dve")
                    P.iop("scalar", lambda e: e.activation(
                        out=s2[:], in_=s1[:], func=AF.Sqrt), "act")
                    P.op("scalar", lambda e: e.drain())
                    P.iop("scalar", lambda e: e.activation(
                        out=s2[:], in_=s2[:], func=AF.Ln,
                        bias=cs["cvals"][:, 1:2]), "act")
                    P.op("scalar", lambda e: e.drain())
                    P.iop("scalar", lambda e, wcur=wcur, gc=gc: e.activation(
                        out=wcur[:, gc], in_=s2[:], func=AF.Exp, scale=-2.0), "act")
                P.wait("tensor", "act")
                for g in range(4):
                    gc = slice(g * 512, (g + 1) * 512)
                    fn = lambda e, kcb=kcb, g=g, gc=gc, wcur=wcur: e.matmul(
                        ps_pv[g][:], vtm[:, kcb, :], wcur[:, gc],
                        start=(kcb == 0), stop=(kcb == 15))
                    if g == 3:
                        P.iop("tensor", fn, "pe")
                    else:
                        P.op("tensor", fn)
            P.pfree(ps_s[0], ps_s[1])
            ps_bc = P.palloc("p7_ps_bc", [64, 512], F32)
            P.wait("vector", "pe")
            for g in range(4):
                gc = slice(g * 512, (g + 1) * 512)
                P.iop("vector", lambda e, g=g, gc=gc: e.tensor_scalar(
                    out=lr[0:1, gc], in0=ps_pv[g][64:65, :],
                    scalar1=cs["cvals"][0:1, 1:2], scalar2=None, op0=ALU.add), "dve")
            P.wait("scalar", "dve")
            P.iop("scalar", lambda e: e.activation(
                out=lr[:], in_=lr[:], func=AF.Ln), "act")
            P.op("scalar", lambda e: e.drain())
            P.iop("scalar", lambda e: e.activation(
                out=recr[:], in_=lr[:], func=AF.Exp, scale=-1.0), "act")
            P.wait("tensor", "act")
            for g in range(4):
                gc = slice(g * 512, (g + 1) * 512)
                P.iop("tensor", lambda e, gc=gc: e.matmul(
                    ps_bc[:], cs["ones1x128"][0:1, 0:64], recr[0:1, gc],
                    start=True, stop=True), "pe")
                P.wait("scalar", "pe")
                P.iop("scalar", lambda e: e.activation(
                    out=bc[:], in_=ps_bc[:], func=AF.Copy), "act")
                P.wait("vector", "act")
                P.op("vector", lambda e, g=g: e.tensor_tensor(
                    out=tsc[64:128, :], in0=ps_pv[g][0:64, :], in1=bc[:], op=ALU.mult))
                P.op("vector", lambda e: e.drain())
                P.iop("vector", lambda e, g=g, gc=gc, h=h, hb=hb: e.tensor_tensor(
                    out=yfm[hb:hb + 64, h // 2, gc], in0=tsc[64:128, :],
                    in1=vg[64:128, gc], op=ALU.mult), "dve")
                P.wait("tensor", "dve")
            P.pfree(ps_pv[0], ps_pv[1], ps_pv[2], ps_pv[3], ps_bc)

        ps_op = P.palloc("p7_ps_op", [128, 512], F32)
        for f in range(8):
            P.wait("gpsimd", "pe")
            P.dma("gpsimd", wblk[:],
                  io["woT"].rearrange("(c p) f -> p c f", p=128)[:, :, f * 128:(f + 1) * 128],
                  "w")
            P.wait("tensor", "w")
            for g in range(4):
                gc = slice(g * 512, (g + 1) * 512)
                xcols = slice(sbase + g * 512, sbase + g * 512 + 512)
                P.wait("sync", "dve")
                P.dma("sync", res[:], X[f * 128:(f + 1) * 128, xcols], "xr")
                for k in range(8):
                    fn = lambda e, k=k, gc=gc: e.matmul(
                        ps_op[:], wblk[:, k, :], yfm[:, k, gc], start=(k == 0), stop=(k == 7))
                    if k == 7:
                        P.iop("tensor", fn, "pe")
                    else:
                        P.op("tensor", fn)
                P.wait("scalar", "pe")
                P.iop("scalar", lambda e, f=f: e.activation(
                    out=t1[:], in_=ps_op[:], func=AF.Identity,
                    bias=b_out[:, f:f + 1]), "act")
                P.wait("vector", "act")
                P.wait("vector", "xr")
                P.wait("vector", "st")
                P.op("vector", lambda e, f=f, gc=gc: e.tensor_tensor(
                    out=osb[:], in0=t1[:], in1=xln[:, f, gc], op=ALU.add))
                P.op("vector", lambda e: e.drain())
                P.iop("vector", lambda e: e.tensor_tensor(
                    out=osb[:], in0=osb[:], in1=res[:], op=ALU.add), "dve")
                P.wait("gpsimd", "dve")
                P.dma("gpsimd", Xo[f * 128:(f + 1) * 128, xcols], osb[:], "st")
                P.wait("tensor", "dve")
        P.pfree(ps_op)
    P.wait("sync", "st")
    P.wait("gpsimd", "st")


def phase5_dram(nc):
    io = {}
    io["wq"] = nc.dram_tensor("p5_wq", [D, D], F32, kind="ExternalInput").ap()
    io["wk"] = nc.dram_tensor("p5_wk", [D, 192], F32, kind="ExternalInput").ap()
    io["wv"] = nc.dram_tensor("p5_wv", [D, 192], F32, kind="ExternalInput").ap()
    io["wproj"] = nc.dram_tensor("p5_wproj", [D, D], F32, kind="ExternalInput").ap()
    io["mscat"] = nc.dram_tensor("p5_mscat", [192, D], F32, kind="ExternalInput").ap()
    io["msign"] = nc.dram_tensor("p5_msign", [192, D], F32, kind="ExternalInput").ap()
    io["vdflat"] = nc.dram_tensor("p5_vdflat", [192, D], F32, kind="ExternalInput").ap()
    io["cos4"] = nc.dram_tensor("p5_cos4", [128, T], F32, kind="ExternalInput").ap()
    io["sinP"] = nc.dram_tensor("p5_sinP", [128, T], F32, kind="ExternalInput").ap()
    io["sinN"] = nc.dram_tensor("p5_sinN", [128, T], F32, kind="ExternalInput").ap()
    io["causal"] = nc.dram_tensor("p5_causal", [128, 128], F32, kind="ExternalInput").ap()
    return io


def phase5_host(inputs):
    h = {}
    perm = np.zeros(D, np.int64)
    for hh in range(16):
        base = hh * 64
        perm[base:base + 32] = base + np.arange(32) * 2
        perm[base + 32:base + 64] = base + np.arange(32) * 2 + 1
    h["p5_wq"] = np.ascontiguousarray(np.asarray(inputs["L1_attn_a_Wq"], np.float32)[:, perm])
    h["p5_wk"] = np.ascontiguousarray(np.asarray(inputs["L1_attn_a_Wk"], np.float32))
    h["p5_wv"] = np.ascontiguousarray(np.asarray(inputs["L1_attn_a_Wv"], np.float32))
    h["p5_wproj"] = np.ascontiguousarray(np.asarray(inputs["L1_attn_a_Wproj"], np.float32))
    kd = np.asarray(inputs["L1_attn_a_key_decoder"], np.float32)
    mscat = np.zeros((192, D), np.float32)
    msign = np.zeros((192, D), np.float32)
    for hh in range(16):
        for f in range(32):
            for r in range(3):
                zr_row = f * 6 + 2 * r
                zi_row = f * 6 + 2 * r + 1
                col_e = hh * 64 + f
                col_o = hh * 64 + 32 + f
                mscat[zr_row, col_e] = kd[f, hh, r, 0]; msign[zr_row, col_e] = 1.0
                mscat[zi_row, col_e] = kd[f, hh, r, 1]; msign[zi_row, col_e] = -1.0
                mscat[zr_row, col_o] = kd[f, hh, r, 1]; msign[zr_row, col_o] = 1.0
                mscat[zi_row, col_o] = kd[f, hh, r, 0]; msign[zi_row, col_o] = 1.0
    h["p5_mscat"] = mscat
    h["p5_msign"] = msign
    vd = np.asarray(inputs["L1_attn_a_value_decoder"], np.float32)
    h["p5_vdflat"] = np.ascontiguousarray(vd.transpose(1, 0, 2).reshape(192, D))
    half = 32
    freq = 1.0 / 10000.0 ** (np.arange(half, dtype=np.float32) / half)
    freqs = np.outer(np.arange(T, dtype=np.float32), freq)
    cosT_ = np.cos(freqs).T.astype(np.float32)
    sinT_ = np.sin(freqs).T.astype(np.float32)
    h["p5_cos4"] = np.ascontiguousarray(np.tile(cosT_, (4, 1)))
    h["p5_sinP"] = np.ascontiguousarray(np.tile(sinT_, (4, 1)))
    h["p5_sinN"] = np.ascontiguousarray(np.tile(-sinT_, (4, 1)))
    ca = np.zeros((128, 128), np.float32)
    ca[np.triu_indices(128, 1)] = -1e30
    h["p5_causal"] = ca
    return h


def emit_phase5(P, cs, io):
    X, Xo = io["X"], io["Xo"]
    L = LNState(P)
    xln = P.sbuf("p5_xln", [128, 8, T], BF16)
    yfm = P.sbuf("p5_yfm", [128, 8, T], BF16)
    z_hi = P.sbuf("p5_zhi", [128, T], BF16)
    z_lo = P.sbuf("p5_zlo", [64, T], BF16)
    v_hi = P.sbuf("p5_vhi", [128, T], BF16)
    v_lo = P.sbuf("p5_vlo", [64, T], BF16)
    mhi = P.sbuf("p5_mhi", [128, D], BF16)
    mlo = P.sbuf("p5_mlo", [64, D], BF16)
    vd_hi = P.sbuf("p5_vdhi", [128, D], BF16)
    vd_lo = P.sbuf("p5_vdlo", [64, D], BF16)
    mtmp = P.sbuf("p5_mtmp", [128, D], F32)
    mtmp2 = P.sbuf("p5_mtmp2", [128, D], F32)
    rope_s = P.sbuf("p5_rope_s", [128, T], F32)
    qc = P.sbuf("p5_qc", [128, T], BF16)
    kc = P.sbuf("p5_kc", [128, T], BF16)
    vc = P.sbuf("p5_vc", [128, 16, 128], BF16)
    cos4 = P.sbuf("p5_cos_sb", [128, T], F32)
    sinP = P.sbuf("p5_sinP_sb", [128, T], F32)
    sinN = P.sbuf("p5_sinN_sb", [128, T], F32)
    rs1 = P.sbuf("p5_rs1", [128, T], F32)
    wblk = P.sbuf("p5_wblk", [128, 8, 128], BF16)
    causal = P.sbuf("p5_causal_sb", [128, 128], F32)
    Psb = P.sbuf("p5_P", [128, T], BF16)
    PT = P.sbuf("p5_PT", [128, 128], BF16)
    mg = P.sbuf("p5_mg", [128, 4], F32)
    lparts = P.sbuf("p5_lp", [128, 4], F32)
    mrow = P.sbuf("p5_m", [128, 1], F32)
    negm8 = P.sbuf("p5_negm8", [128, 1], F32)
    lsum = P.sbuf("p5_l", [128, 1], F32)
    linv = P.sbuf("p5_linv", [128, 1], F32)
    ytm = P.sbuf("p5_ytm", [128, 64], BF16)
    t1 = P.sbuf("p5_t1", [128, 512], F32)
    res = P.sbuf("p5_res", [128, 512], F32)
    osb = P.sbuf("p5_osb", [128, 512], F32)

    P.dma("gpsimd", cos4[:], io["cos4"], "w")
    P.dma("gpsimd", sinP[:], io["sinP"], "w")
    P.dma("gpsimd", sinN[:], io["sinN"], "w")
    P.dma("gpsimd", causal[:], io["causal"], "w")
    P.dma("gpsimd", mtmp[:], io["mscat"][0:128, :], "w")
    P.dma("gpsimd", mtmp2[:], io["msign"][0:128, :], "w")
    P.dma("gpsimd", vd_hi[:], io["vdflat"][0:128, :], "w")
    P.dma("gpsimd", vd_lo[:], io["vdflat"][128:192, :], "w")
    P.wait("vector", "w")
    P.iop("vector", lambda e: e.tensor_tensor(
        out=mhi[:], in0=mtmp[:], in1=mtmp2[:], op=ALU.mult), "dve")
    P.wait("gpsimd", "dve")
    P.dma("gpsimd", mtmp[0:64, :], io["mscat"][128:192, :], "w")
    P.dma("gpsimd", mtmp2[0:64, :], io["msign"][128:192, :], "w")
    P.wait("vector", "w")
    P.iop("vector", lambda e: e.tensor_tensor(
        out=mlo[:], in0=mtmp[0:64, :], in1=mtmp2[0:64, :], op=ALU.mult), "dve")

    def rope(buf, scaled):
        P.op("vector", lambda e, buf=buf: e.tensor_tensor(
            out=rs1[:], in0=buf[:], in1=cos4[:], op=ALU.mult))
        for blk in range(4):
            srcb = blk ^ 1
            tbl = sinN if blk % 2 == 0 else sinP
            P.op("vector", lambda e, buf=buf, blk=blk, srcb=srcb, tbl=tbl: e.tensor_tensor(
                out=rope_s[blk * 32:blk * 32 + 32, :],
                in0=buf[srcb * 32:srcb * 32 + 32, :],
                in1=tbl[srcb * 32:srcb * 32 + 32, :], op=ALU.mult))
        P.op("vector", lambda e: e.drain())
        P.iop("vector", lambda e, buf=buf: e.tensor_tensor(
            out=buf[:], in0=rs1[:], in1=rope_s[:], op=ALU.add), "dve")

    for s in range(BC):
        sbase = s * T
        emit_ln(P, L, cs, X, xln, (sbase, T), xln_base=sbase)
        P.wait("tensor", "dve")
        ps_p = [P.palloc("p5_ps_p0", [128, 512], F32), P.palloc("p5_ps_p1", [128, 512], F32)]
        for (wname, dsth, dstl) in (("wk", z_hi, z_lo), ("wv", v_hi, v_lo)):
            for fg, (f0, fw) in enumerate(((0, 128), (128, 64))):
                P.wait("gpsimd", "pe")
                P.dma("gpsimd", wblk[:, :, 0:fw],
                      io[wname].rearrange("(c p) f -> p c f", p=128)[:, :, f0:f0 + fw], "w")
                P.wait("tensor", "w")
                for g in range(4):
                    gc = slice(g * 512, (g + 1) * 512)
                    pin = ps_p[g % 2]
                    if not (fg == 0 and g == 0):
                        P.wait("tensor", "dve")
                    for k in range(8):
                        fn = lambda e, k=k, fw=fw, gc=gc, pin=pin: e.matmul(
                            pin[0:fw, :], wblk[:, k, 0:fw], xln[:, k, gc],
                            start=(k == 0), stop=(k == 7))
                        if k == 7:
                            P.iop("tensor", fn, "pe")
                        else:
                            P.op("tensor", fn)
                    P.wait("vector", "pe")
                    dst = dsth if fw == 128 else dstl
                    P.iop("vector", lambda e, dst=dst, fw=fw, gc=gc, pin=pin: e.tensor_copy(
                        out=dst[0:fw, gc], in_=pin[0:fw, :]), "dve")
        P.wait("tensor", "dve")
        P.pfree(ps_p[0], ps_p[1])

        ps_q = P.palloc("p5_ps_q", [128, 512], F32)
        ps_s = [P.palloc(f"p5_ps_s{i}", [128, 512], F32) for i in range(4)]
        ps_pt = P.palloc("p5_ps_pt", [128, 128], BF16)
        ps_o = P.palloc("p5_ps_o", [128, 64], F32)
        ps_yt = P.palloc("p5_ps_yt", [64, 128], BF16)
        for c in range(8):
            P.wait("gpsimd", "pe")
            P.dma("gpsimd", wblk[:],
                  io["wq"].rearrange("(c p) f -> p c f", p=128)[:, :, c * 128:(c + 1) * 128],
                  "w")
            P.wait("tensor", "w")
            for g in range(4):
                gc = slice(g * 512, (g + 1) * 512)
                if not (c == 0 and g == 0):
                    P.wait("tensor", "dve")
                for k in range(8):
                    fn = lambda e, k=k, gc=gc: e.matmul(
                        ps_q[:], wblk[:, k, :], xln[:, k, gc], start=(k == 0), stop=(k == 7))
                    if k == 7:
                        P.iop("tensor", fn, "pe")
                    else:
                        P.op("tensor", fn)
                P.wait("vector", "pe")
                P.iop("vector", lambda e, gc=gc: e.tensor_copy(
                    out=qc[:, gc], in_=ps_q[:]), "dve")
            for g in range(4):
                gc = slice(g * 512, (g + 1) * 512)
                P.wait("tensor", "dve")
                P.op("tensor", lambda e, c=c, gc=gc: e.matmul(
                    ps_q[:], mhi[:, c * 128:(c + 1) * 128], z_hi[:, gc],
                    start=True, stop=False))
                P.iop("tensor", lambda e, c=c, gc=gc: e.matmul(
                    ps_q[:], mlo[:, c * 128:(c + 1) * 128], z_lo[:, gc],
                    start=False, stop=True), "pe")
                P.wait("vector", "pe")
                P.iop("vector", lambda e, gc=gc: e.tensor_copy(
                    out=kc[:, gc], in_=ps_q[:]), "dve")
            for tb in range(16):
                tbs = slice(tb * 128, tb * 128 + 128)
                for half in range(2):
                    f0 = c * 128 + half * 64
                    P.wait("tensor", "dve")
                    P.op("tensor", lambda e, tbs=tbs, f0=f0: e.matmul(
                        ps_o[:], v_hi[:, tbs], vd_hi[:, f0:f0 + 64],
                        start=True, stop=False))
                    P.iop("tensor", lambda e, tbs=tbs, f0=f0: e.matmul(
                        ps_o[:], v_lo[:, tbs], vd_lo[:, f0:f0 + 64],
                        start=False, stop=True), "pe")
                    P.wait("vector", "pe")
                    P.iop("vector", lambda e, tb=tb, half=half: e.tensor_copy(
                        out=vc[:, tb, half * 64:half * 64 + 64], in_=ps_o[:]), "dve")
            P.wait("tensor", "dve")
            rope(qc, True)
            rope(kc, False)
            P.iop("vector", lambda e: e.drain(), "dve")
            P.wait("tensor", "dve")
            for hh in range(2):
                b = hh * 64
                for qi in range(16):
                    ngroups = qi // 4 + 1
                    qs = slice(qi * 128, qi * 128 + 128)
                    P.wait("tensor", "act")
                    for g in range(ngroups):
                        fn = lambda e, g=g, b=b, qs=qs: e.matmul(
                            ps_s[g][:], qc[b:b + 64, qs],
                            kc[b:b + 64, g * 512:(g + 1) * 512], start=True, stop=True)
                        if g == ngroups - 1:
                            P.iop("tensor", fn, "pe")
                        else:
                            P.op("tensor", fn)
                    P.wait("vector", "pe")
                    dcol = (qi % 4) * 128
                    P.op("vector", lambda e, ngroups=ngroups, dcol=dcol: e.tensor_tensor(
                        out=ps_s[ngroups - 1][:, dcol:dcol + 128],
                        in0=ps_s[ngroups - 1][:, dcol:dcol + 128],
                        in1=causal[:], op=ALU.add))
                    if dcol + 128 < 512:
                        P.op("vector", lambda e, ngroups=ngroups, dcol=dcol: e.memset(
                            ps_s[ngroups - 1][:, dcol + 128:512], -1e30))
                    P.op("vector", lambda e: e.drain())
                    for g in range(ngroups):
                        P.op("vector", lambda e, g=g: e.reduce_max(
                            mg[:, g:g + 1], ps_s[g][:], axis=AX))
                    P.op("vector", lambda e: e.drain())
                    P.op("vector", lambda e, ngroups=ngroups: e.reduce_max(
                        mrow[:], mg[:, 0:ngroups], axis=AX))
                    P.op("vector", lambda e: e.drain())
                    P.op("vector", lambda e: e.tensor_scalar(
                        out=negm8[:], in0=mrow[:], scalar1=-0.125, scalar2=None,
                        op0=ALU.mult))
                    P.iop("vector", lambda e: e.drain(), "dve")
                    P.wait("scalar", "dve")
                    for g in range(ngroups):
                        P.iop("scalar", lambda e, g=g: e.activation(
                            out=Psb[:, g * 512:(g + 1) * 512], in_=ps_s[g][:],
                            func=AF.Exp, bias=negm8[:], scale=0.125,
                            accum_out=lparts[:, g:g + 1]), "act")
                    P.wait("vector", "act")
                    P.op("vector", lambda e, ngroups=ngroups: e.reduce_sum(
                        lsum[:], lparts[:, 0:ngroups], axis=AX))
                    P.op("vector", lambda e: e.drain())
                    P.op("vector", lambda e: e.reciprocal(out=linv[:], in_=lsum[:]))
                    P.iop("vector", lambda e: e.drain(), "dve")
                    P.wait("tensor", "act")
                    for kb in range(qi + 1):
                        P.iop("tensor", lambda e, kb=kb: e.transpose(
                            ps_pt[:], Psb[:, kb * 128:kb * 128 + 128], cs["ident"][:]), "pe")
                        P.wait("scalar", "pe")
                        P.iop("scalar", lambda e: e.activation(
                            out=PT[:], in_=ps_pt[:], func=AF.Copy), "act")
                        P.wait("tensor", "act")
                        fn = lambda e, kb=kb, b=b, qi=qi: e.matmul(
                            ps_o[:], PT[:], vc[:, kb, b:b + 64],
                            start=(kb == 0), stop=(kb == qi))
                        if kb == qi:
                            P.iop("tensor", fn, "pe")
                        else:
                            P.op("tensor", fn)
                    P.wait("vector", "pe")
                    P.wait("vector", "dve")
                    P.iop("vector", lambda e: e.tensor_tensor(
                        out=ytm[:], in0=ps_o[:], in1=linv[:].broadcast_to([128, 64]),
                        op=ALU.mult), "dve")
                    P.wait("tensor", "dve")
                    P.iop("tensor", lambda e: e.transpose(
                        ps_yt[:], ytm[:], cs["ident"][:]), "pe")
                    P.wait("vector", "pe")
                    P.iop("vector", lambda e, c=c, hh=hh, qs=qs: e.tensor_copy(
                        out=yfm[hh * 64:hh * 64 + 64, c, qs],
                        in_=ps_yt[:]), "dve")
                    P.wait("tensor", "dve")
        P.pfree(ps_q, ps_s[0], ps_s[1], ps_s[2], ps_s[3], ps_pt, ps_o, ps_yt)

        ps_op = P.palloc("p5_ps_op", [128, 512], F32)
        for f in range(8):
            P.wait("gpsimd", "pe")
            P.dma("gpsimd", wblk[:],
                  io["wproj"].rearrange("(c p) f -> p c f", p=128)[:, :, f * 128:(f + 1) * 128],
                  "w")
            P.wait("tensor", "w")
            for g in range(4):
                gc = slice(g * 512, (g + 1) * 512)
                xcols = slice(sbase + g * 512, sbase + g * 512 + 512)
                P.wait("sync", "dve")
                P.dma("sync", res[:], X[f * 128:(f + 1) * 128, xcols], "xr")
                for k in range(8):
                    fn = lambda e, k=k, gc=gc: e.matmul(
                        ps_op[:], wblk[:, k, :], yfm[:, k, gc], start=(k == 0), stop=(k == 7))
                    if k == 7:
                        P.iop("tensor", fn, "pe")
                    else:
                        P.op("tensor", fn)
                P.wait("vector", "pe")
                P.wait("vector", "xr")
                P.wait("vector", "st")
                P.iop("vector", lambda e: e.tensor_tensor(
                    out=osb[:], in0=ps_op[:], in1=res[:], op=ALU.add), "dve")
                P.wait("gpsimd", "dve")
                P.dma("gpsimd", Xo[f * 128:(f + 1) * 128, xcols], osb[:], "st")
                P.wait("tensor", "dve")
        P.pfree(ps_op)
    P.wait("sync", "st")
    P.wait("gpsimd", "st")


def emit_ln_sbuf(P, L, cs, src, out, gcol=None, bcol=None, ncols=512):
    ps_sum = P.palloc("lns_sum", [1, ncols], F32)
    ps_sumsq = P.palloc("lns_sumsq", [1, ncols], F32)
    ps_bc0 = P.palloc("lns_bc0", [128, ncols], F32)
    ps_bc1 = P.palloc("lns_bc1", [128, ncols], F32)
    P.wait("scalar", "pe")
    P.wait("scalar", "dve")
    for c in range(8):
        P.iop("scalar", lambda e, c=c: e.activation(
            out=L.xsq[:, c, :ncols], in_=src[:, c, :], func=AF.Square), "act")
    P.wait("tensor", "act")
    P.wait("tensor", "dve")
    for c in range(8):
        P.op("tensor", lambda e, c=c: e.matmul(
            ps_sum[:], cs["ones_bf"][:], src[:, c, :], start=(c == 0), stop=(c == 7)))
    for c in range(8):
        fn = lambda e, c=c: e.matmul(
            ps_sumsq[:], cs["ones_bf"][:], L.xsq[:, c, :ncols],
            start=(c == 0), stop=(c == 7))
        if c == 7:
            P.iop("tensor", fn, "pe")
        else:
            P.op("tensor", fn)
    P.wait("vector", "pe")
    P.op("vector", lambda e: e.tensor_scalar(
        out=L.mu[0:1, :ncols], in0=ps_sum[:], scalar1=1.0 / 1024, scalar2=None, op0=ALU.mult))
    P.op("vector", lambda e: e.tensor_scalar(
        out=L.e2[0:1, :ncols], in0=ps_sumsq[:], scalar1=1.0 / 1024, scalar2=None, op0=ALU.mult))
    P.op("vector", lambda e: e.tensor_tensor(
        out=L.musq[0:1, :ncols], in0=L.mu[0:1, :ncols], in1=L.mu[0:1, :ncols], op=ALU.mult))
    P.op("vector", lambda e: e.drain())
    P.iop("vector", lambda e: e.tensor_tensor(
        out=L.var[0:1, :ncols], in0=L.e2[0:1, :ncols], in1=L.musq[0:1, :ncols],
        op=ALU.subtract), "dve")
    P.wait("scalar", "dve")
    P.iop("scalar", lambda e: e.activation(
        out=L.srt[0:1, :ncols], in_=L.var[0:1, :ncols], func=AF.Sqrt,
        bias=cs["cvals"][0:1, 0:1]), "act")
    P.wait("vector", "act")
    P.op("vector", lambda e: e.drain())
    P.iop("vector", lambda e: e.reciprocal(out=L.rstd[0:1, :ncols], in_=L.srt[0:1, :ncols]), "dve")
    P.op("vector", lambda e: e.drain())
    P.op("vector", lambda e: e.scalar_tensor_tensor(
        out=L.nb[0:1, :ncols], in0=L.mu[0:1, :ncols], scalar=-1.0, in1=L.rstd[0:1, :ncols],
        op0=ALU.mult, op1=ALU.mult))
    P.op("vector", lambda e: e.drain())
    P.op("vector", lambda e: e.tensor_copy(out=L.rp[0:1, 0:ncols], in_=L.rstd[0:1, :ncols]))
    P.iop("vector", lambda e: e.tensor_copy(out=L.rp[0:1, 512:512 + ncols], in_=L.nb[0:1, :ncols]), "dve")
    P.wait("tensor", "dve")
    P.op("tensor", lambda e: e.matmul(
        ps_bc0[:], cs["ones1x128"][:], L.rp[0:1, 0:ncols], start=True, stop=True))
    P.iop("tensor", lambda e: e.matmul(
        ps_bc1[:], cs["ones1x128"][:], L.rp[0:1, 512:512 + ncols],
        start=True, stop=True), "pe")
    P.wait("vector", "pe")
    for c in range(8):
        P.op("vector", lambda e, c=c: e.tensor_tensor(
            out=L.tmp[:, :ncols], in0=src[:, c, :], in1=ps_bc0[:], op=ALU.mult))
        if gcol is None:
            P.iop("vector", lambda e, c=c: e.tensor_tensor(
                out=out[:, c, :], in0=L.tmp[:, :ncols], in1=ps_bc1[:], op=ALU.add), "dve")
        else:
            P.op("vector", lambda e, c=c: e.tensor_tensor(
                out=L.tmp[:, :ncols], in0=L.tmp[:, :ncols], in1=ps_bc1[:], op=ALU.add))
            P.iop("vector", lambda e, c=c: e.tensor_scalar(
                out=out[:, c, :], in0=L.tmp[:, :ncols], scalar1=gcol[:, c:c + 1],
                scalar2=bcol[:, c:c + 1], op0=ALU.mult, op1=ALU.add), "dve")
    P.wait("tensor", "dve")
    P.wait("scalar", "dve")
    P.pfree(ps_sum, ps_sumsq, ps_bc0, ps_bc1)


def phase3_dram(nc):
    io = {}
    io["winT"] = nc.dram_tensor("p3_winT", [D, 4096], F32, kind="ExternalInput").ap()
    io["wcT"] = nc.dram_tensor("p3_wcT", [2048, 2048], F32, kind="ExternalInput").ap()
    io["wdT"] = nc.dram_tensor("p3_wdT", [2048, 2048], F32, kind="ExternalInput").ap()
    io["woT"] = nc.dram_tensor("p3_woT", [2048, D], F32, kind="ExternalInput").ap()
    io["b_in"] = nc.dram_tensor("p3_b_in", [128, 32], F32, kind="ExternalInput").ap()
    io["b_c"] = nc.dram_tensor("p3_b_c", [128, 16], F32, kind="ExternalInput").ap()
    io["b_d"] = nc.dram_tensor("p3_b_d", [128, 16], F32, kind="ExternalInput").ap()
    io["b_out"] = nc.dram_tensor("p3_b_out", [128, 8], F32, kind="ExternalInput").ap()
    io["convw"] = nc.dram_tensor("p3_convw", [128, 16, 4], F32, kind="ExternalInput").ap()
    io["convb"] = nc.dram_tensor("p3_convb", [128, 16], F32, kind="ExternalInput").ap()
    io["logdelta"] = nc.dram_tensor("p3_logdelta", [128, 16], F32, kind="ExternalInput").ap()
    io["Amat"] = nc.dram_tensor("p3_Amat", [128, 16, 16], F32, kind="ExternalInput").ap()
    io["Bmat"] = nc.dram_tensor("p3_Bmat", [128, 16, 16], F32, kind="ExternalInput").ap()
    io["ln_g"] = nc.dram_tensor("p3_ln_g", [128, 8], F32, kind="ExternalInput").ap()
    io["ln_b"] = nc.dram_tensor("p3_ln_b", [128, 8], F32, kind="ExternalInput").ap()
    return io


def phase3_host(inputs):
    def cols(v, nch):
        return np.ascontiguousarray(np.asarray(v, np.float32).reshape(nch, 128).T)
    h = {}
    h["p3_winT"] = np.ascontiguousarray(inputs["L0_attn_b_W_in"].T)
    h["p3_wcT"] = np.ascontiguousarray(inputs["L0_attn_b_W_c"].T)
    h["p3_wdT"] = np.ascontiguousarray(inputs["L0_attn_b_W_d"].T)
    h["p3_woT"] = np.ascontiguousarray(inputs["L0_attn_b_W_out"].T)
    h["p3_b_in"] = cols(inputs["L0_attn_b_b_in"], 32)
    h["p3_b_c"] = cols(inputs["L0_attn_b_b_c"], 16)
    h["p3_b_d"] = cols(inputs["L0_attn_b_b_d"], 16)
    h["p3_b_out"] = cols(inputs["L0_attn_b_b_out"], 8)
    h["p3_convw"] = np.ascontiguousarray(
        np.asarray(inputs["L0_attn_b_conv_w"], np.float32)[:, 0, :].reshape(16, 128, 4)
        .transpose(1, 0, 2))
    h["p3_convb"] = cols(inputs["L0_attn_b_conv_b"], 16)
    h["p3_logdelta"] = cols(inputs["L0_attn_b_log_delta"], 16)
    h["p3_Amat"] = np.ascontiguousarray(
        np.asarray(inputs["L0_attn_b_A"], np.float32).reshape(16, 128, 16).transpose(1, 0, 2))
    h["p3_Bmat"] = np.ascontiguousarray(
        np.asarray(inputs["L0_attn_b_Bm"], np.float32).reshape(16, 128, 16).transpose(1, 0, 2))
    h["p3_ln_g"] = cols(inputs["L0_attn_b_ln_g"], 8)
    h["p3_ln_b"] = cols(inputs["L0_attn_b_ln_b"], 8)
    return h


def emit_phase3(P, cs, io):
    X, Xo = io["X"], io["Xo"]
    L = LNState(P)
    xln = P.sbuf("p3_xln", [128, 8, 512], BF16)
    xla = P.sbuf("p3_xla", [128, 8, 512], BF16)
    xp = P.sbuf("p3_xp", [128, 16, 515], BF16)
    halo = P.sbuf("p3_halo", [128, 16, 3], BF16)
    gsig = P.sbuf("p3_gsig", [128, 16, 512], BF16)
    xc = P.sbuf("p3_xc", [128, 16, 512], BF16)
    u = P.sbuf("p3_u", [128, 16, 512], BF16)
    g_all = P.sbuf("p3_g", [128, 16, 512], F32)
    xg = P.sbuf("p3_xg", [128, 16, 512], BF16)
    carry = P.sbuf("p3_carry", [128, 16], F32)
    adec = P.sbuf("p3_adec", [128, 16], F32)
    Cvec = P.sbuf("p3_C", [128, 16], F32)
    wblk = P.sbuf("p3_wblk", [128, 8, 128], BF16)
    wcb = P.sbuf("p3_wcb", [128, 16, 128], BF16)
    wdb = P.sbuf("p3_wdb", [128, 16, 128], BF16)
    wob = P.sbuf("p3_wob", [128, 16, 128], BF16)
    biasb = {}
    for nm, shape in (("b_in", [128, 32]), ("b_c", [128, 16]), ("b_d", [128, 16]),
                      ("b_out", [128, 8]), ("convw", [128, 16, 4]), ("convb", [128, 16]),
                      ("logdelta", [128, 16]), ("ln_g", [128, 8]), ("ln_b", [128, 8])):
        biasb[nm] = P.sbuf("p3s_" + nm, shape, F32)
        P.dma("gpsimd", biasb[nm][:], io[nm], "w")
    Abuf = P.sbuf("p3s_Amat", [128, 16, 16], F32)
    Bbuf = P.sbuf("p3s_Bmat", [128, 16, 16], F32)
    P.dma("gpsimd", Abuf[:], io["Amat"], "w")
    P.dma("gpsimd", Bbuf[:], io["Bmat"], "w")
    ca = P.sbuf("p3_ca", [128, 512], F32)
    cb = P.sbuf("p3_cb", [128, 512], F32)
    t1 = P.sbuf("p3_t1", [128, 512], F32)
    sg = P.sbuf("p3_sg", [128, 512], F32)
    AB = P.sbuf("p3_AB", [128, 16, 16], F32)
    res = P.sbuf("p3_res", [128, 512], F32)
    osb = P.sbuf("p3_osb", [128, 512], F32)

    P.wait("scalar", "w")
    P.wait("vector", "w")
    ld_ = biasb["logdelta"]
    P.iop("scalar", lambda e: e.activation(out=adec[:], in_=ld_[:], func=AF.Abs), "act")
    P.iop("scalar", lambda e: e.activation(out=adec[:], in_=adec[:], func=AF.Exp,
                                           scale=-1.0), "act")
    P.iop("scalar", lambda e: e.activation(out=adec[:], in_=adec[:], func=AF.Ln,
                                           bias=cs["cvals"][:, 3:4]), "act")
    P.wait("vector", "act")
    P.op("vector", lambda e: e.tensor_scalar(
        out=Cvec[:], in0=ld_[:], scalar1=0.0, scalar2=None, op0=ALU.max))
    P.op("vector", lambda e: e.drain())
    P.iop("vector", lambda e: e.tensor_tensor(
        out=adec[:], in0=adec[:], in1=Cvec[:], op=ALU.add), "dve")
    P.wait("scalar", "dve")
    P.iop("scalar", lambda e: e.activation(out=adec[:], in_=adec[:], func=AF.Exp,
                                           scale=-1.0), "act")
    P.wait("vector", "act")
    P.op("vector", lambda e: e.tensor_tensor(
        out=AB[:], in0=Abuf[:], in1=Bbuf[:], op=ALU.mult))
    P.op("vector", lambda e: e.drain())
    P.op("vector", lambda e: e.reduce_sum(Cvec[:], AB[:], axis=AX))
    P.iop("vector", lambda e: e.drain(), "dve")

    ps_in = [P.palloc("p3_ps_in0", [128, 512], F32), P.palloc("p3_ps_in1", [128, 512], F32)]
    ps_c = P.palloc("p3_ps_c", [128, 512], F32)
    ps_dd = P.palloc("p3_ps_dd", [128, 512], F32)

    for tile in range(8):
        s = tile // 4
        j = tile % 4
        x0 = s * T + j * 512
        tcol = slice(x0, x0 + 512)
        emit_ln(P, L, cs, X, xln, (x0, 512), xln_base=x0)
        P.wait("tensor", "dve")
        emit_ln_sbuf(P, L, cs, xln, xla, biasb["ln_g"], biasb["ln_b"])
        for f in range(32):
            P.wait("gpsimd", "pe")
            P.wait("gpsimd", "act")
            P.dma("gpsimd", wblk[:],
                  io["winT"].rearrange("(c p) f -> p c f", p=128)[:, :, f * 128:(f + 1) * 128],
                  "w")
            P.wait("tensor", "w")
            pin = ps_in[f % 2]
            if f > 1:
                P.wait("tensor", "act")
            for k in range(8):
                fn = lambda e, k=k, pin=pin: e.matmul(
                    pin[:], wblk[:, k, :], xla[:, k, :], start=(k == 0), stop=(k == 7))
                if k == 7:
                    P.iop("tensor", fn, "pe")
                else:
                    P.op("tensor", fn)
            P.wait("scalar", "pe")
            if f < 16:
                P.iop("scalar", lambda e, f=f, pin=pin: e.activation(
                    out=xp[:, f, 3:515], in_=pin[:], func=AF.Identity,
                    bias=io and biasb["b_in"][:, f:f + 1]), "act")
            else:
                P.iop("scalar", lambda e, f=f, pin=pin: e.activation(
                    out=gsig[:, f - 16, :], in_=pin[:], func=AF.Sigmoid,
                    bias=biasb["b_in"][:, f:f + 1]), "act")
        P.wait("vector", "act")
        if j == 0:
            P.iop("vector", lambda e: e.memset(xp[:, :, 0:3], 0.0), "dve")
        else:
            P.iop("vector", lambda e: e.tensor_copy(out=xp[:, :, 0:3], in_=halo[:]), "dve")
        P.op("vector", lambda e: e.drain())
        P.iop("vector", lambda e: e.tensor_copy(out=halo[:], in_=xp[:, :, 512:515]), "dve")
        P.op("vector", lambda e: e.drain())
        cw = biasb["convw"]
        for fc in range(16):
            P.op("vector", lambda e, fc=fc: e.tensor_scalar(
                out=ca[:], in0=xp[:, fc, 0:512], scalar1=cw[:, fc, 0:1], scalar2=None,
                op0=ALU.mult))
            P.op("vector", lambda e, fc=fc: e.tensor_scalar(
                out=cb[:], in0=xp[:, fc, 1:513], scalar1=cw[:, fc, 1:2], scalar2=None,
                op0=ALU.mult))
            P.op("vector", lambda e: e.drain())
            P.op("vector", lambda e, fc=fc: e.scalar_tensor_tensor(
                out=ca[:], in0=xp[:, fc, 2:514], scalar=cw[:, fc, 2:3], in1=ca[:],
                op0=ALU.mult, op1=ALU.add))
            P.op("vector", lambda e, fc=fc: e.scalar_tensor_tensor(
                out=cb[:], in0=xp[:, fc, 3:515], scalar=cw[:, fc, 3:4], in1=cb[:],
                op0=ALU.mult, op1=ALU.add))
            P.op("vector", lambda e: e.drain())
            P.iop("vector", lambda e: e.tensor_tensor(
                out=ca[:], in0=ca[:], in1=cb[:], op=ALU.add), "dve")
            P.wait("scalar", "dve")
            P.iop("scalar", lambda e, fc=fc: e.activation(
                out=xc[:, fc, :], in_=ca[:], func=AF.Silu,
                bias=biasb["convb"][:, fc:fc + 1]), "act")
            P.wait("vector", "act")
        for f in range(16):
            P.wait("gpsimd", "pe")
            P.dma("gpsimd", wcb[:],
                  io["wcT"].rearrange("(c p) f -> p c f", p=128)[:, :, f * 128:(f + 1) * 128],
                  "w")
            P.dma("gpsimd", wdb[:],
                  io["wdT"].rearrange("(c p) f -> p c f", p=128)[:, :, f * 128:(f + 1) * 128],
                  "w")
            P.wait("tensor", "w")
            P.wait("tensor", "dve")
            P.wait("tensor", "act")
            for k in range(16):
                P.op("tensor", lambda e, k=k: e.matmul(
                    ps_c[:], wcb[:, k, :], xc[:, k, :], start=(k == 0), stop=(k == 15)))
            for k in range(16):
                fn = lambda e, k=k: e.matmul(
                    ps_dd[:], wdb[:, k, :], xc[:, k, :], start=(k == 0), stop=(k == 15))
                if k == 15:
                    P.iop("tensor", fn, "pe")
                else:
                    P.op("tensor", fn)
            P.wait("scalar", "pe")
            P.iop("scalar", lambda e, f=f: e.activation(
                out=sg[:], in_=ps_dd[:], func=AF.Sigmoid,
                bias=biasb["b_d"][:, f:f + 1]), "act")
            P.wait("vector", "pe")
            P.op("vector", lambda e, f=f: e.tensor_scalar(
                out=t1[:], in0=ps_c[:], scalar1=biasb["b_c"][:, f:f + 1], scalar2=None,
                op0=ALU.add))
            P.wait("vector", "act")
            P.op("vector", lambda e: e.drain())
            P.iop("vector", lambda e, f=f: e.tensor_tensor(
                out=u[:, f, :], in0=t1[:], in1=sg[:], op=ALU.mult), "dve")
        if j == 0:
            P.iop("vector", lambda e: e.memset(carry[:], 0.0), "dve")
            P.op("vector", lambda e: e.drain())
        for fc in range(16):
            P.op("vector", lambda e, fc=fc: e.tensor_tensor_scan(
                out=g_all[:, fc, :], data0=adec[:, fc:fc + 1].broadcast_to([128, 512]),
                data1=u[:, fc, :], initial=carry[:, fc:fc + 1],
                op0=ALU.mult, op1=ALU.add))
        P.op("vector", lambda e: e.drain())
        P.iop("vector", lambda e: e.tensor_copy(out=carry[:], in_=g_all[:, :, 511]), "dve")
        P.op("vector", lambda e: e.drain())
        for fc in range(16):
            P.iop("vector", lambda e, fc=fc: e.scalar_tensor_tensor(
                out=xg[:, fc, :], in0=g_all[:, fc, :], scalar=Cvec[:, fc:fc + 1],
                in1=gsig[:, fc, :], op0=ALU.mult, op1=ALU.mult), "dve")
        for fo in range(8):
            P.wait("gpsimd", "pe")
            P.dma("gpsimd", wob[:],
                  io["woT"].rearrange("(c p) f -> p c f", p=128)[:, :, fo * 128:(fo + 1) * 128],
                  "w")
            P.wait("sync", "dve")
            P.dma("sync", res[:], X[fo * 128:(fo + 1) * 128, tcol], "xr")
            P.wait("tensor", "w")
            P.wait("tensor", "dve")
            for k in range(16):
                fn = lambda e, k=k: e.matmul(
                    ps_c[:], wob[:, k, :], xg[:, k, :], start=(k == 0), stop=(k == 15))
                if k == 15:
                    P.iop("tensor", fn, "pe")
                else:
                    P.op("tensor", fn)
            P.wait("scalar", "pe")
            P.iop("scalar", lambda e, fo=fo: e.activation(
                out=t1[:], in_=ps_c[:], func=AF.Identity,
                bias=biasb["b_out"][:, fo:fo + 1]), "act")
            P.wait("vector", "act")
            P.wait("vector", "xr")
            P.wait("vector", "st")
            P.op("vector", lambda e, fo=fo: e.tensor_tensor(
                out=osb[:], in0=t1[:], in1=xln[:, fo, :], op=ALU.add))
            P.op("vector", lambda e: e.drain())
            P.iop("vector", lambda e: e.tensor_tensor(
                out=osb[:], in0=osb[:], in1=res[:], op=ALU.add), "dve")
            P.wait("gpsimd", "dve")
            P.dma("gpsimd", Xo[fo * 128:(fo + 1) * 128, tcol], osb[:], "st")
    P.pfree(ps_in[0], ps_in[1], ps_c, ps_dd)
    P.wait("sync", "st")
    P.wait("gpsimd", "st")


def phase2_dram(nc):
    io = {}
    io["wgate"] = nc.dram_tensor("p2_wgate", [D, 8], F32, kind="ExternalInput").ap()
    io["cfc"] = nc.dram_tensor("p2_cfc", [8, D, 3072], F32, kind="ExternalInput").ap()
    io["cproj"] = nc.dram_tensor("p2_cproj", [8, 1536, D], F32, kind="ExternalInput").ap()
    io["prevmask"] = nc.dram_tensor("p2_prevmask", [8, 1], F32, kind="ExternalInput").ap()
    io["cc_in"] = nc.dram_tensor("p2_cc_in", [8, 1], F32).ap()
    io["cc_out"] = nc.dram_tensor("p2_cc_out", [64, 1], F32).ap()
    return io


def phase2_host(inputs):
    h = {}
    h["p2_wgate"] = np.ascontiguousarray(np.asarray(inputs["L0_ffn_w_gate"], np.float32))
    h["p2_cfc"] = np.ascontiguousarray(np.asarray(inputs["L0_ffn_c_fc"], np.float32))
    h["p2_cproj"] = np.ascontiguousarray(np.asarray(inputs["L0_ffn_c_proj"], np.float32))
    return h


def emit_phase2(P, cs, io):
    X, Xo = io["X"], io["Xo"]
    CAP = 4096.0
    L = LNState(P)
    xln = P.sbuf("p2_xln", [128, 8, NT], BF16)
    h_e = P.sbuf("p2_h", [128, 12, 512], BF16)
    acc = P.sbuf("p2_acc", [128, 8, 512], F32)
    wfa = P.sbuf("p2_wfa", [128, 8, 512], BF16)
    wfg = P.sbuf("p2_wfg", [128, 8, 512], BF16)
    wpr = P.sbuf("p2_wpr", [128, 12, 128], BF16)
    gsb = P.sbuf("p2_gate", [128, 8, 8], BF16)
    logit = P.sbuf("p2_logit", [128, 32, 8], F32)
    shifted = P.sbuf("p2_shift", [128, 32, 8], F32)
    EX = P.sbuf("p2_EX", [128, 32, 8], F32)
    mskd = P.sbuf("p2_mskd", [128, 32, 8], F32)
    m1 = P.sbuf("p2_m1", [128, 32], F32)
    m2 = P.sbuf("p2_m2", [128, 32], F32)
    Z = P.sbuf("p2_Z", [128, 32], F32)
    w1 = P.sbuf("p2_w1", [128, 32], F32)
    w2 = P.sbuf("p2_w2", [128, 32], F32)
    dd2 = P.sbuf("p2_dd2", [128, 32], F32)
    e2w = P.sbuf("p2_e2w", [128, 32], F32)
    sel1 = P.sbuf("p2_sel1", [128, 8], F32)
    sel2 = P.sbuf("p2_sel2", [128, 8], F32)
    c1 = P.sbuf("p2_c1", [128, 8], F32)
    sboth = P.sbuf("p2_sboth", [128, 32, 8], BF16)
    coefb = P.sbuf("p2_coefb", [128, 32, 8], BF16)
    cum_bf = P.sbuf("p2_cumbf", [128, 8], BF16)
    tot = P.sbuf("p2_tot", [8, 32], F32)
    off = P.sbuf("p2_off", [8, 33], F32)
    gath = P.sbuf("p2_gath", [8, 8], F32)
    pmsk = P.sbuf("p2_pmsk", [8, 1], F32)
    thrb = P.sbuf("p2_thrb", [8, 1], F32)
    thr = P.sbuf("p2_thr", [8, 32], F32)
    vT = P.sbuf("p2_vT", [8, 128], F32)
    coef_fm = P.sbuf("p2_coef_fm", [8, NT], BF16)
    bc = P.sbuf("p2_bc", [128, 512], BF16)
    t1 = P.sbuf("p2_t1", [128, 512], F32)
    t2 = P.sbuf("p2_t2", [128, 512], F32)
    res = P.sbuf("p2_res", [128, 512], F32)
    osb = P.sbuf("p2_osb", [128, 512], F32)

    emit_ln(P, L, cs, X, xln, (0, NT))
    P.wait("tensor", "dve")

    P.dma("gpsimd", gsb[:], io["wgate"].rearrange("(c p) e -> p c e", p=128), "w")
    P.dma("gpsimd", pmsk[:], io["prevmask"], "w")
    P.wait("tensor", "w")
    ps_l = P.palloc("p2_ps_l", [128, 8], F32)
    for tc in range(32):
        tcs = slice(tc * 128, tc * 128 + 128)
        for k in range(8):
            fn = lambda e, k=k, tcs=tcs: e.matmul(
                ps_l[:], xln[:, k, tcs], gsb[:, k, :], start=(k == 0), stop=(k == 7))
            if k == 7:
                P.iop("tensor", fn, "pe")
            else:
                P.op("tensor", fn)
        P.wait("vector", "pe")
        P.iop("vector", lambda e, tc=tc: e.tensor_copy(
            out=logit[:, tc, :], in_=ps_l[:]), "dve")
        P.wait("tensor", "dve")
    P.pfree(ps_l)

    P.iop("vector", lambda e: e.reduce_max(m1[:], logit[:], axis=AX), "dve")
    P.op("vector", lambda e: e.drain())
    for tc in range(32):
        P.op("vector", lambda e, tc=tc: e.tensor_scalar(
            out=sel1[:], in0=logit[:, tc, :], scalar1=m1[:, tc:tc + 1], scalar2=None,
            op0=ALU.is_ge))
        P.op("vector", lambda e: e.drain())
        P.op("vector", lambda e, tc=tc: e.scalar_tensor_tensor(
            out=mskd[:, tc, :], in0=sel1[:], scalar=-1e9, in1=logit[:, tc, :],
            op0=ALU.mult, op1=ALU.add))
        P.op("vector", lambda e, tc=tc: e.tensor_scalar(
            out=shifted[:, tc, :], in0=logit[:, tc, :], scalar1=m1[:, tc:tc + 1],
            scalar2=None, op0=ALU.subtract))
        P.op("vector", lambda e: e.drain())
        P.iop("vector", lambda e, tc=tc: e.tensor_copy(
            out=logit[:, tc, :], in_=sel1[:]), "dve")
    P.iop("vector", lambda e: e.reduce_max(m2[:], mskd[:], axis=AX), "dve")
    P.op("vector", lambda e: e.drain())
    P.op("vector", lambda e: e.tensor_tensor(
        out=dd2[:], in0=m2[:], in1=m1[:], op=ALU.subtract))
    P.iop("vector", lambda e: e.drain(), "dve")
    P.wait("scalar", "dve")
    P.iop("scalar", lambda e: e.activation(out=EX[:], in_=shifted[:], func=AF.Exp), "act")
    P.iop("scalar", lambda e: e.activation(out=e2w[:], in_=dd2[:], func=AF.Exp), "act")
    P.wait("vector", "act")
    P.iop("vector", lambda e: e.reduce_sum(Z[:], EX[:], axis=AX), "dve")
    P.op("vector", lambda e: e.drain())
    P.op("vector", lambda e: e.reciprocal(out=w1[:], in_=Z[:]))
    P.op("vector", lambda e: e.drain())
    P.op("vector", lambda e: e.tensor_tensor(out=w2[:], in0=e2w[:], in1=w1[:],
                                             op=ALU.mult))
    P.iop("vector", lambda e: e.drain(), "dve")

    for tc in range(32):
        P.op("vector", lambda e, tc=tc: e.tensor_scalar(
            out=sel2[:], in0=mskd[:, tc, :], scalar1=m2[:, tc:tc + 1], scalar2=None,
            op0=ALU.is_ge))
        P.op("vector", lambda e, tc=tc: e.tensor_scalar(
            out=c1[:], in0=logit[:, tc, :], scalar1=w1[:, tc:tc + 1], scalar2=None,
            op0=ALU.mult))
        P.op("vector", lambda e: e.drain())
        P.op("vector", lambda e, tc=tc: e.tensor_tensor(
            out=sboth[:, tc, :], in0=logit[:, tc, :], in1=sel2[:], op=ALU.add))
        P.iop("vector", lambda e, tc=tc: e.scalar_tensor_tensor(
            out=coefb[:, tc, :], in0=sel2[:], scalar=w2[:, tc:tc + 1], in1=c1[:],
            op0=ALU.mult, op1=ALU.add), "dve")
    ps_t = P.palloc("p2_ps_t", [8, 1], F32)
    P.wait("tensor", "dve")
    for tc in range(32):
        P.iop("tensor", lambda e, tc=tc: e.matmul(
            ps_t[:], sboth[:, tc, :], cs["ones_bf"][:], start=True, stop=True), "pe")
        P.wait("vector", "pe")
        P.iop("vector", lambda e, tc=tc: e.tensor_copy(
            out=tot[:, tc:tc + 1], in_=ps_t[:]), "dve")
        P.wait("tensor", "dve")
    P.pfree(ps_t)
    P.op("vector", lambda e: e.memset(off[:, 0:1], 0.0))
    P.op("vector", lambda e: e.drain())
    for n in range(32):
        P.op("vector", lambda e, n=n: e.tensor_tensor(
            out=off[:, n + 1:n + 2], in0=off[:, n:n + 1], in1=tot[:, n:n + 1],
            op=ALU.add))
        P.iop("vector", lambda e: e.drain(), "dve")
    P.wait("gpsimd", "dve")
    P.dma("gpsimd", io["cc_in"], off[:, 32:33], "cc")
    P.wait("gpsimd", "cc")
    P.iop("gpsimd", lambda e: e.collective_compute(
        "AllGather", ALU.bypass, replica_groups=[list(range(NCORES))],
        ins=[io["cc_in"].opt()], outs=[io["cc_out"].opt()]), "ccd")
    P.wait("gpsimd", "ccd")
    P.dma("gpsimd", gath[:], io["cc_out"].rearrange("(c e) one -> c (e one)", e=8), "cc")
    P.wait("tensor", "cc")
    ps_og = P.palloc("p2_ps_og", [8, 1], F32)
    P.iop("tensor", lambda e: e.matmul(
        ps_og[:], gath[:], pmsk[:], start=True, stop=True), "pe")
    P.wait("vector", "pe")
    P.op("vector", lambda e: e.tensor_scalar(
        out=thrb[:], in0=ps_og[:], scalar1=-1.0, scalar2=CAP, op0=ALU.mult, op1=ALU.add))
    P.op("vector", lambda e: e.drain())
    for n in range(32):
        P.op("vector", lambda e, n=n: e.tensor_tensor(
            out=thr[:, n:n + 1], in0=thrb[:], in1=off[:, n:n + 1], op=ALU.subtract))
    P.iop("vector", lambda e: e.drain(), "dve")
    P.pfree(ps_og)

    ps_cum = P.palloc("p2_ps_cum", [128, 8], F32)
    ps_cT = P.palloc("p2_ps_cT", [8, 128], BF16)
    ps_fT = P.palloc("p2_ps_fT", [8, 128], BF16)
    P.wait("tensor", "dve")
    for tc in range(32):
        P.iop("tensor", lambda e, tc=tc: e.matmul(
            ps_cum[:], cs["strictU"][:], sboth[:, tc, :], start=True, stop=True), "pe")
        P.wait("vector", "pe")
        P.iop("vector", lambda e: e.tensor_copy(out=cum_bf[:], in_=ps_cum[:]), "dve")
        P.wait("tensor", "dve")
        P.op("tensor", lambda e: e.transpose(ps_cT[:], cum_bf[:], cs["ident"][:]))
        P.iop("tensor", lambda e, tc=tc: e.transpose(
            ps_fT[:], coefb[:, tc, :], cs["ident"][:]), "pe")
        P.wait("vector", "pe")
        P.op("vector", lambda e, tc=tc: e.tensor_scalar(
            out=vT[:], in0=ps_cT[:], scalar1=thr[:, tc:tc + 1], scalar2=None,
            op0=ALU.is_lt))
        P.op("vector", lambda e: e.drain())
        P.iop("vector", lambda e, tc=tc: e.tensor_tensor(
            out=coef_fm[:, tc * 128:tc * 128 + 128], in0=ps_fT[:], in1=vT[:],
            op=ALU.mult), "dve")
        P.wait("tensor", "dve")
    P.pfree(ps_cum, ps_cT, ps_fT)

    ps_a = [P.palloc("p2_ps_a0", [128, 512], F32), P.palloc("p2_ps_a1", [128, 512], F32)]
    ps_g = [P.palloc("p2_ps_g0", [128, 512], F32), P.palloc("p2_ps_g1", [128, 512], F32)]
    ps_d = [P.palloc("p2_ps_d0", [128, 512], F32), P.palloc("p2_ps_d1", [128, 512], F32)]
    ps_bc = P.palloc("p2_ps_bc", [128, 512], F32)
    for ti in range(8):
        tcols = slice(ti * 512, ti * 512 + 512)
        for e_i in range(8):
            P.wait("tensor", "act")
            P.iop("tensor", lambda e, e_i=e_i, tcols=tcols: e.matmul(
                ps_bc[:], cs["esel"][:, e_i * 128:(e_i + 1) * 128],
                coef_fm[:, tcols], start=True, stop=True), "pe")
            P.wait("scalar", "pe")
            P.wait("scalar", "dve")
            P.iop("scalar", lambda e: e.activation(
                out=bc[:], in_=ps_bc[:], func=AF.Copy), "act")
            for fb in range(3):
                P.wait("gpsimd", "pe")
                P.dma("gpsimd", wfa[:],
                      io["cfc"][e_i].rearrange("(c p) f -> p c f", p=128)[
                          :, :, fb * 512:(fb + 1) * 512], "w")
                P.dma("gpsimd", wfg[:],
                      io["cfc"][e_i].rearrange("(c p) f -> p c f", p=128)[
                          :, :, 1536 + fb * 512:1536 + (fb + 1) * 512], "w")
                P.wait("tensor", "w")
                for fc in range(4):
                    pa = ps_a[fc % 2]
                    pg = ps_g[fc % 2]
                    j = fb * 4 + fc
                    if fb + fc > 0:
                        P.wait("tensor", "dve")
                    for k in range(8):
                        P.op("tensor", lambda e, k=k, fc=fc, tcols=tcols, pa=pa: e.matmul(
                            pa[:], wfa[:, k, fc * 128:fc * 128 + 128], xln[:, k, tcols],
                            start=(k == 0), stop=(k == 7)))
                    for k in range(8):
                        fn = lambda e, k=k, fc=fc, tcols=tcols, pg=pg: e.matmul(
                            pg[:], wfg[:, k, fc * 128:fc * 128 + 128], xln[:, k, tcols],
                            start=(k == 0), stop=(k == 7))
                        if k == 7:
                            P.iop("tensor", fn, "pe")
                        else:
                            P.op("tensor", fn)
                    P.wait("scalar", "pe")
                    P.wait("scalar", "dve")
                    P.iop("scalar", lambda e, pa=pa: e.activation(
                        out=t1[:], in_=pa[:], func=AF.Silu), "act")
                    P.wait("vector", "act")
                    P.op("vector", lambda e, pg=pg: e.tensor_tensor(
                        out=t2[:], in0=pg[:], in1=bc[:], op=ALU.mult))
                    P.op("vector", lambda e: e.drain())
                    P.iop("vector", lambda e, j=j: e.tensor_tensor(
                        out=h_e[:, j, :], in0=t1[:], in1=t2[:], op=ALU.mult), "dve")
            for fo in range(8):
                pd = ps_d[fo % 2]
                P.wait("gpsimd", "pe")
                P.dma("gpsimd", wpr[:],
                      io["cproj"][e_i].rearrange("(c p) f -> p c f", p=128)[
                          :, :, fo * 128:(fo + 1) * 128], "w")
                P.wait("tensor", "w")
                if fo > 1:
                    P.wait("tensor", "dve")
                for k in range(12):
                    fn = lambda e, k=k, pd=pd: e.matmul(
                        pd[:], wpr[:, k, :], h_e[:, k, :], start=(k == 0), stop=(k == 11))
                    if k == 11:
                        P.iop("tensor", fn, "pe")
                    else:
                        P.op("tensor", fn)
                P.wait("vector", "pe")
                if e_i == 0:
                    P.iop("vector", lambda e, fo=fo, pd=pd: e.tensor_copy(
                        out=acc[:, fo, :], in_=pd[:]), "dve")
                else:
                    P.iop("vector", lambda e, fo=fo, pd=pd: e.tensor_tensor(
                        out=acc[:, fo, :], in0=acc[:, fo, :], in1=pd[:], op=ALU.add), "dve")
        for fo in range(8):
            P.wait("sync", "dve")
            P.dma("sync", res[:], X[fo * 128:(fo + 1) * 128, tcols], "xr")
            P.wait("vector", "xr")
            P.wait("vector", "st")
            P.iop("vector", lambda e, fo=fo: e.tensor_tensor(
                out=osb[:], in0=acc[:, fo, :], in1=res[:], op=ALU.add), "dve")
            P.wait("gpsimd", "dve")
            P.dma("gpsimd", Xo[fo * 128:(fo + 1) * 128, tcols], osb[:], "st")
    P.pfree(ps_a[0], ps_a[1], ps_g[0], ps_g[1], ps_d[0], ps_d[1], ps_bc)
    P.wait("sync", "st")
    P.wait("gpsimd", "st")



N_TILES = 24
SLOTS = N_TILES * 512
I32 = mybir.dt.int32


def phase12_dram(nc):
    io = phase2_dram(nc)
    io["cfcB"] = nc.dram_tensor("p12_cfcB", [8 * D, 3072], BF16).ap()
    io["cprojB"] = nc.dram_tensor("p12_cprojB", [8 * 1536, D], BF16).ap()
    io["gidx"] = nc.dram_tensor("p12_gidx", [SLOTS, 1], I32).ap()
    io["wslot"] = nc.dram_tensor("p12_wslot", [SLOTS, 1], F32).ap()
    io["accm"] = nc.dram_tensor("p12_accm", [NT, D], F32).ap()
    io["xlntm"] = nc.dram_tensor("p12_xlntm", [NT, D], BF16).ap()
    io["cc_in2"] = nc.dram_tensor("p12_cc_in", [1, 8], F32).ap()
    io["cc_out2"] = nc.dram_tensor("p12_cc_out", [8, 8], F32).ap()
    io["_conv_done"] = [False]
    return io


def phase16_dram(nc):
    io = phase6_dram(nc)
    io["wgB"] = nc.dram_tensor("p16_wgB", [8 * D, 4096], BF16).ap()
    io["wuB"] = nc.dram_tensor("p16_wuB", [8 * D, 4096], BF16).ap()
    io["wdB"] = nc.dram_tensor("p16_wdB", [8 * 4096, D], BF16).ap()
    io["gidx"] = nc.dram_tensor("p16_gidx", [SLOTS, 1], I32).ap()
    io["wslot"] = nc.dram_tensor("p16_wslot", [SLOTS, 1], F32).ap()
    io["accm"] = nc.dram_tensor("p16_accm", [NT, D], F32).ap()
    io["xlntm"] = nc.dram_tensor("p16_xlntm", [NT, D], BF16).ap()
    io["_conv_done"] = [False]
    return io


def _emit_cast_cache(P, dst_flat, src_ap, rows, width):
    buf = P.sbuf("cvt", [128, width], BF16)
    sap = src_ap.rearrange("e r w -> (e r) w") if len(src_ap.shape) == 3 else src_ap
    for r0 in range(0, rows, 128):
        P.wait("gpsimd", "cvo")
        P.dma("gpsimd", buf[:], sap[r0:r0 + 128, :], "cvi")
        P.wait("gpsimd", "cvi")
        P.dma("gpsimd", dst_flat[r0:r0 + 128, :], buf[:], "cvo")


def emit_moe_sparse(P, cs, io, variant):
    X, Xo = io["X"], io["Xo"]
    l0 = (variant == 0)
    IOTA = cs["iota"]

    if not io["_conv_done"][0]:
        io["_conv_done"][0] = True
        P.push_scope()
        if l0:
            _emit_cast_cache(P, io["cfcB"], io["cfc"], 8 * D, 3072)
            _emit_cast_cache(P, io["cprojB"], io["cproj"], 8 * 1536, D)
        else:
            _emit_cast_cache(P, io["wgB"], io["wgT"], 8 * D, 4096)
            _emit_cast_cache(P, io["wuB"], io["wuT"], 8 * D, 4096)
            _emit_cast_cache(P, io["wdB"], io["wdT"], 8 * 4096, D)
        P.pop_scope()

    P.push_scope()
    L = LNState(P)
    xln = P.sbuf("sm_xln", [128, 8, NT], BF16)
    logit = P.sbuf("sm_logit", [128, 32, 8], F32)
    mskd = P.sbuf("sm_mskd", [128, 32, 8], F32)
    shifted = P.sbuf("sm_shift", [128, 32, 8], F32)
    EX = P.sbuf("sm_EX", [128, 32, 8], F32)
    m1 = P.sbuf("sm_m1", [128, 32], F32)
    m2 = P.sbuf("sm_m2", [128, 32], F32)
    Z = P.sbuf("sm_Z", [128, 32], F32)
    w1 = P.sbuf("sm_w1", [128, 32], F32)
    w2 = P.sbuf("sm_w2", [128, 32], F32)
    dd2 = P.sbuf("sm_dd2", [128, 32], F32)
    e2w = P.sbuf("sm_e2w", [128, 32], F32)
    sel1 = P.sbuf("sm_sel1", [128, 8], F32)
    sel2 = P.sbuf("sm_sel2", [128, 8], F32)
    sboth = P.sbuf("sm_sboth", [128, 32, 8], BF16)
    svalid = P.sbuf("sm_svalid", [128, 32, 8], BF16)
    vtmb = P.sbuf("sm_vtm", [128, 8], F32)
    tots = P.sbuf("sm_tots", [1, 32 * 8], F32)
    offr = P.sbuf("sm_offr", [1, 33 * 8], F32)
    obf = P.sbuf("sm_obf", [1, 32 * 8], F32)
    alr = P.sbuf("sm_alr", [1, 8], F32)
    bser = P.sbuf("sm_bser", [1, 8], F32)
    endr = P.sbuf("sm_endr", [1, 8], F32)
    ter = P.sbuf("sm_ter", [1, N_TILES], F32)
    tmp8 = P.sbuf("sm_tmp8", [1, 8], F32)
    gath = P.sbuf("sm_gath", [8, 8], F32)
    pmsk = P.sbuf("sm_pmsk", [8, 1], F32)
    offgr = P.sbuf("sm_offgr", [1, 8], F32)
    tokf = P.sbuf("sm_tokf", [128, 1], F32)
    toki = P.sbuf("sm_toki", [128, 1], I32)
    posf = P.sbuf("sm_posf", [128, 1], F32)
    posi = P.sbuf("sm_posi", [128, 1], I32)
    coefc = P.sbuf("sm_coefc", [128, 1], F32)
    vk = P.sbuf("sm_vk", [128, 1], F32)
    t8 = P.sbuf("sm_t8", [128, 8], F32)
    cumv = P.sbuf("sm_cumv", [128, 8], F32)
    zrow = P.sbuf("sm_zrow", [128, 2], F32)
    tpT = P.sbuf("sm_tpT", [128, 128], BF16)
    tecol = P.sbuf("sm_tecol", [128, N_TILES], F32)
    gsb = P.sbuf("sm_gate", [128, 8, 8], BF16)

    emit_ln(P, L, cs, X, xln, (0, NT))
    P.wait("tensor", "dve")

    ps_tp = P.palloc("sm_ps_tp", [128, 128], BF16)
    for tc in range(32):
        for fc in range(8):
            P.iop("tensor", lambda e, tc=tc, fc=fc: e.transpose(
                ps_tp[:], xln[:, fc, tc * 128:(tc + 1) * 128], cs["ident"][:]), "pe")
            P.wait("scalar", "pe")
            P.iop("scalar", lambda e: e.activation(
                out=tpT[:], in_=ps_tp[:], func=AF.Copy), "act")
            P.wait("gpsimd", "act")
            P.dma("gpsimd", io["xlntm"][tc * 128:(tc + 1) * 128, fc * 128:(fc + 1) * 128],
                  tpT[:], "xtm")
            P.wait("tensor", "xtm")
    P.pfree(ps_tp)

    gname = "wgate" if l0 else "gateT"
    P.dma("gpsimd", gsb[:], io[gname].rearrange("(c p) e -> p c e", p=128), "w")
    if l0:
        P.dma("gpsimd", pmsk[:], io["prevmask"], "w")
    P.wait("tensor", "w")
    ps_l = P.palloc("sm_ps_l", [128, 8], F32)
    for tc in range(32):
        tcs = slice(tc * 128, tc * 128 + 128)
        for k in range(8):
            fn = lambda e, k=k, tcs=tcs: e.matmul(
                ps_l[:], xln[:, k, tcs], gsb[:, k, :], start=(k == 0), stop=(k == 7))
            if k == 7:
                P.iop("tensor", fn, "pe")
            else:
                P.op("tensor", fn)
        P.wait("vector", "pe")
        P.iop("vector", lambda e, tc=tc: e.tensor_copy(
            out=logit[:, tc, :], in_=ps_l[:]), "dve")
        P.wait("tensor", "dve")
    P.pfree(ps_l)

    P.iop("vector", lambda e: e.reduce_max(m1[:], logit[:], axis=AX), "dve")
    P.op("vector", lambda e: e.drain())
    for tc in range(32):
        P.op("vector", lambda e, tc=tc: e.tensor_scalar(
            out=sel1[:], in0=logit[:, tc, :], scalar1=m1[:, tc:tc + 1], scalar2=None,
            op0=ALU.is_ge))
        P.op("vector", lambda e: e.drain())
        P.op("vector", lambda e, tc=tc: e.scalar_tensor_tensor(
            out=mskd[:, tc, :], in0=sel1[:], scalar=-1e9, in1=logit[:, tc, :],
            op0=ALU.mult, op1=ALU.add))
        if l0:
            P.op("vector", lambda e, tc=tc: e.tensor_scalar(
                out=shifted[:, tc, :], in0=logit[:, tc, :], scalar1=m1[:, tc:tc + 1],
                scalar2=None, op0=ALU.subtract))
        P.op("vector", lambda e: e.drain())
        P.iop("vector", lambda e, tc=tc: e.tensor_copy(
            out=logit[:, tc, :], in_=sel1[:]), "dve")
    P.iop("vector", lambda e: e.reduce_max(m2[:], mskd[:], axis=AX), "dve")
    P.op("vector", lambda e: e.drain())
    P.op("vector", lambda e: e.tensor_tensor(
        out=dd2[:], in0=m2[:], in1=m1[:], op=ALU.subtract))
    P.iop("vector", lambda e: e.drain(), "dve")
    P.wait("scalar", "dve")
    if l0:
        P.iop("scalar", lambda e: e.activation(out=EX[:], in_=shifted[:], func=AF.Exp), "act")
        P.iop("scalar", lambda e: e.activation(out=e2w[:], in_=dd2[:], func=AF.Exp), "act")
        P.wait("vector", "act")
        P.iop("vector", lambda e: e.reduce_sum(Z[:], EX[:], axis=AX), "dve")
        P.op("vector", lambda e: e.drain())
        P.op("vector", lambda e: e.reciprocal(out=w1[:], in_=Z[:]))
        P.op("vector", lambda e: e.drain())
        P.op("vector", lambda e: e.tensor_tensor(out=w2[:], in0=e2w[:], in1=w1[:],
                                                 op=ALU.mult))
        P.iop("vector", lambda e: e.drain(), "dve")
    else:
        P.iop("vector", lambda e: e.tensor_scalar(
            out=e2w[:], in0=dd2[:], scalar1=-1.0, scalar2=None, op0=ALU.mult), "dve")
        P.wait("scalar", "dve")
        P.iop("scalar", lambda e: e.activation(out=w1[:], in_=e2w[:], func=AF.Sigmoid), "act")
        P.iop("scalar", lambda e: e.activation(out=w2[:], in_=dd2[:], func=AF.Sigmoid), "act")
        P.wait("vector", "act")

    for tc in range(32):
        P.op("vector", lambda e, tc=tc: e.tensor_scalar(
            out=sel2[:], in0=mskd[:, tc, :], scalar1=m2[:, tc:tc + 1], scalar2=None,
            op0=ALU.is_ge))
        P.op("vector", lambda e: e.drain())
        P.iop("vector", lambda e, tc=tc: e.tensor_tensor(
            out=sboth[:, tc, :], in0=logit[:, tc, :], in1=sel2[:], op=ALU.add), "dve")

    ps_c = P.palloc("sm_ps_c", [1, 8], F32)
    ps_cum = P.palloc("sm_ps_cum", [128, 8], F32)
    if l0:
        P.wait("tensor", "dve")
        for tc in range(32):
            P.iop("tensor", lambda e, tc=tc: e.matmul(
                ps_c[:], cs["ones_bf"][:], sboth[:, tc, :], start=True, stop=True), "pe")
            P.wait("vector", "pe")
            P.iop("vector", lambda e, tc=tc: e.tensor_copy(
                out=tots[0:1, tc * 8:(tc + 1) * 8], in_=ps_c[:]), "dve")
            P.wait("tensor", "dve")
        P.op("vector", lambda e: e.memset(offr[0:1, 0:8], 0.0))
        P.op("vector", lambda e: e.drain())
        for n in range(32):
            P.op("vector", lambda e, n=n: e.tensor_tensor(
                out=offr[0:1, (n + 1) * 8:(n + 2) * 8], in0=offr[0:1, n * 8:(n + 1) * 8],
                in1=tots[0:1, n * 8:(n + 1) * 8], op=ALU.add))
            P.op("vector", lambda e: e.drain())
        P.iop("vector", lambda e: e.drain(), "dve")
        P.wait("gpsimd", "dve")
        P.dma("gpsimd", io["cc_in2"], offr[0:1, 32 * 8:33 * 8], "cc")
        P.wait("gpsimd", "cc")
        P.iop("gpsimd", lambda e: e.collective_compute(
            "AllGather", ALU.bypass, replica_groups=[list(range(NCORES))],
            ins=[io["cc_in2"].opt()], outs=[io["cc_out2"].opt()]), "ccd")
        P.wait("gpsimd", "ccd")
        P.dma("gpsimd", gath[:], io["cc_out2"], "cc")
        P.wait("tensor", "cc")
        ps_og = P.palloc("sm_ps_og", [1, 8], F32)
        P.iop("tensor", lambda e: e.matmul(
            ps_og[:], pmsk[:], gath[:], start=True, stop=True), "pe")
        P.wait("vector", "pe")
        P.iop("vector", lambda e: e.tensor_copy(out=offgr[:], in_=ps_og[:]), "dve")
        P.wait("tensor", "dve")
        P.pfree(ps_og)
        for tc in range(32):
            P.op("vector", lambda e, tc=tc: e.tensor_tensor(
                out=tmp8[:], in0=offr[0:1, tc * 8:(tc + 1) * 8], in1=offgr[:],
                op=ALU.add))
            P.iop("vector", lambda e: e.drain(), "dve")
            P.wait("tensor", "dve")
            P.op("tensor", lambda e, tc=tc: e.matmul(
                ps_cum[:], cs["strictU"][:], sboth[:, tc, :], start=True, stop=False))
            P.iop("tensor", lambda e: e.matmul(
                ps_cum[:], cs["ones1x128_f32"][:], tmp8[:], start=False, stop=True), "pe")
            P.wait("vector", "pe")
            P.iop("vector", lambda e: e.tensor_scalar(
                out=vtmb[:], in0=ps_cum[:], scalar1=4096.0, scalar2=None,
                op0=ALU.is_lt), "dve")
            P.op("vector", lambda e: e.drain())
            P.op("vector", lambda e, tc=tc: e.tensor_tensor(
                out=svalid[:, tc, :], in0=sboth[:, tc, :], in1=vtmb[:], op=ALU.mult))
            P.iop("vector", lambda e, tc=tc: e.tensor_copy(
                out=EX[:, tc, :], in_=vtmb[:]), "dve")
            P.wait("tensor", "dve")
    else:
        for tc in range(32):
            P.op("vector", lambda e, tc=tc: e.tensor_copy(
                out=svalid[:, tc, :], in_=sboth[:, tc, :]))
        P.iop("vector", lambda e: e.memset(EX[:], 1.0), "dve")
        P.wait("tensor", "dve")

    for tc in range(32):
        P.iop("tensor", lambda e, tc=tc: e.matmul(
            ps_c[:], cs["ones_bf"][:], svalid[:, tc, :], start=True, stop=True), "pe")
        P.wait("vector", "pe")
        P.iop("vector", lambda e, tc=tc: e.tensor_copy(
            out=tots[0:1, tc * 8:(tc + 1) * 8], in_=ps_c[:]), "dve")
        P.wait("tensor", "dve")
    P.op("vector", lambda e: e.memset(offr[0:1, 0:8], 0.0))
    P.op("vector", lambda e: e.drain())
    for n in range(32):
        P.op("vector", lambda e, n=n: e.tensor_tensor(
            out=offr[0:1, (n + 1) * 8:(n + 2) * 8], in0=offr[0:1, n * 8:(n + 1) * 8],
            in1=tots[0:1, n * 8:(n + 1) * 8], op=ALU.add))
        P.op("vector", lambda e: e.drain())
    P.op("vector", lambda e: e.memset(alr[:], 0.0))
    P.op("vector", lambda e: e.drain())
    for jj in range(8):
        P.op("vector", lambda e, jj=jj: e.tensor_scalar(
            out=tmp8[:], in0=offr[0:1, 32 * 8:33 * 8], scalar1=float(512 * jj),
            scalar2=None, op0=ALU.is_gt))
        P.op("vector", lambda e: e.drain())
        P.op("vector", lambda e: e.scalar_tensor_tensor(
            out=alr[:], in0=tmp8[:], scalar=512.0, in1=alr[:],
            op0=ALU.mult, op1=ALU.add))
        P.op("vector", lambda e: e.drain())
    P.op("vector", lambda e: e.memset(bser[0:1, 0:1], 0.0))
    P.op("vector", lambda e: e.drain())
    for e_i in range(7):
        P.op("vector", lambda e, e_i=e_i: e.tensor_tensor(
            out=bser[0:1, e_i + 1:e_i + 2], in0=bser[0:1, e_i:e_i + 1],
            in1=alr[0:1, e_i:e_i + 1], op=ALU.add))
        P.op("vector", lambda e: e.drain())
    P.op("vector", lambda e: e.tensor_tensor(
        out=endr[:], in0=bser[:], in1=alr[:], op=ALU.add))
    P.op("vector", lambda e: e.drain())
    for jj in range(N_TILES):
        P.op("vector", lambda e, jj=jj: e.tensor_scalar(
            out=tmp8[:], in0=endr[:], scalar1=float(512 * jj), scalar2=None,
            op0=ALU.is_le))
        P.op("vector", lambda e: e.drain())
        P.op("vector", lambda e, jj=jj: e.reduce_sum(
            ter[0:1, jj:jj + 1], tmp8[:], axis=AX))
        P.op("vector", lambda e: e.drain())
        P.op("vector", lambda e, jj=jj: e.tensor_scalar(
            out=ter[0:1, jj:jj + 1], in0=ter[0:1, jj:jj + 1], scalar1=7.0,
            scalar2=None, op0=ALU.min))
        P.op("vector", lambda e: e.drain())
    for tc in range(32):
        P.op("vector", lambda e, tc=tc: e.tensor_tensor(
            out=obf[0:1, tc * 8:(tc + 1) * 8], in0=offr[0:1, tc * 8:(tc + 1) * 8],
            in1=bser[:], op=ALU.add))
    P.iop("vector", lambda e: e.drain(), "dve")

    P.op("vector", lambda e: e.memset(zrow[:], 0.0))
    P.iop("vector", lambda e: e.drain(), "dve")
    P.wait("gpsimd", "dve")
    for b in range(N_TILES * 4):
        P.dma("gpsimd", io["gidx"][b * 128:(b + 1) * 128, :], zrow[:, 0:1], "gz")
        P.dma("gpsimd", io["wslot"][b * 128:(b + 1) * 128, :], zrow[:, 1:2], "gz")
    P.wait("gpsimd", "gz")
    for tc in range(32):
        P.op("vector", lambda e, tc=tc: e.tensor_scalar(
            out=tokf[:], in0=IOTA[:, 0:1], scalar1=float(tc * 128), scalar2=None,
            op0=ALU.add))
        P.op("vector", lambda e: e.drain())
        P.iop("vector", lambda e: e.tensor_copy(out=toki[:], in_=tokf[:]), "dve")
        P.wait("tensor", "dve")
        P.op("tensor", lambda e, tc=tc: e.matmul(
            ps_cum[:], cs["strictU"][:], svalid[:, tc, :], start=True, stop=False))
        P.iop("tensor", lambda e, tc=tc: e.matmul(
            ps_cum[:], cs["ones1x128_f32"][:], obf[0:1, tc * 8:(tc + 1) * 8],
            start=False, stop=True), "pe")
        P.wait("vector", "pe")
        P.iop("vector", lambda e: e.tensor_copy(out=cumv[:], in_=ps_cum[:]), "dve")
        P.op("vector", lambda e: e.drain())
        P.wait("tensor", "dve")
        for kk in range(2):
            wk = w1 if kk == 0 else w2
            sap = (lambda tc=tc: logit[:, tc, :]) if kk == 0 else (lambda: sel2[:])
            if kk == 1:
                P.op("vector", lambda e, tc=tc: e.tensor_scalar(
                    out=sel2[:], in0=mskd[:, tc, :], scalar1=m2[:, tc:tc + 1],
                    scalar2=None, op0=ALU.is_ge))
                P.op("vector", lambda e: e.drain())
            P.op("vector", lambda e, tc=tc, sap=sap: e.tensor_tensor(
                out=t8[:], in0=sap(), in1=EX[:, tc, :], op=ALU.mult))
            P.op("vector", lambda e: e.drain())
            P.op("vector", lambda e: e.reduce_sum(vk[:], t8[:], axis=AX))
            P.op("vector", lambda e, sap=sap: e.tensor_tensor(
                out=t8[:], in0=sap(), in1=cumv[:], op=ALU.mult))
            P.op("vector", lambda e: e.drain())
            P.op("vector", lambda e: e.reduce_sum(posf[:], t8[:], axis=AX))
            P.op("vector", lambda e: e.drain())
            P.op("vector", lambda e, tc=tc, wk=wk: e.tensor_scalar(
                out=coefc[:], in0=vk[:], scalar1=wk[:, tc:tc + 1], scalar2=None,
                op0=ALU.mult))
            P.op("vector", lambda e: e.tensor_scalar(
                out=vk[:], in0=vk[:], scalar1=-1.0, scalar2=1.0,
                op0=ALU.mult, op1=ALU.add))
            P.op("vector", lambda e: e.drain())
            P.op("vector", lambda e: e.scalar_tensor_tensor(
                out=posf[:], in0=vk[:], scalar=1e6, in1=posf[:],
                op0=ALU.mult, op1=ALU.add))
            P.op("vector", lambda e: e.drain())
            P.iop("vector", lambda e: e.tensor_copy(out=posi[:], in_=posf[:]), "dve")
            P.op("vector", lambda e: e.drain())
            P.wait("gpsimd", "dve")
            P.iop("gpsimd", lambda e: e.indirect_dma_start(
                out=io["gidx"], out_offset=IndirectOffsetOnAxis(posi[:], 0),
                in_=toki[:], in_offset=None), "gsc", 16)
            P.iop("gpsimd", lambda e: e.indirect_dma_start(
                out=io["wslot"], out_offset=IndirectOffsetOnAxis(posi[:], 0),
                in_=coefc[:], in_offset=None), "gsc", 16)
            P.wait("vector", "gsc")
    if DEBUG:
        dbg_dump(P, "smx_xln", xln[:], hard=True)
        dbg_dump(P, "smx_alr", alr[:], hard=True)
        dbg_dump(P, "smx_bser", bser[:], hard=True)
        dbg_dump(P, "smx_endr", endr[:], hard=True)
        dbg_dump(P, "smx_ter", ter[:], hard=True)
        dbg_dump(P, "smx_offr", offr[:], hard=True)
        dbg_dump(P, "smx_w1", w1[:], hard=True)
        dbg_dump(P, "smx_w2", w2[:], hard=True)
        gdump = P.sbuf("smx_gd", [128, SLOTS // 128], F32)
        wdump = P.sbuf("smx_wd", [128, SLOTS // 128], F32)
        gdi = P.sbuf("smx_gdi", [128, 1], I32)
        P.wait("gpsimd", "gsc")
        for b in range(SLOTS // 128):
            P.dma("gpsimd", gdi[:], io["gidx"][b * 128:(b + 1) * 128, :], "gdd")
            P.wait("vector", "gdd")
            P.iop("vector", lambda e, b=b: e.tensor_copy(
                out=gdump[:, b:b + 1], in_=gdi[:]), "dve")
            P.wait("gpsimd", "dve")
            P.dma("gpsimd", wdump[:, b:b + 1].rearrange("p one -> p one") if False else wdump[:, b:b + 1],
                  io["wslot"][b * 128:(b + 1) * 128, :], "gdd")
        P.wait("gpsimd", "gdd")
        dbg_dump(P, "smx_gidx", gdump[:], hard=True)
        dbg_dump(P, "smx_wslot", wdump[:], hard=True)
    ps_te = P.palloc("sm_ps_te", [128, N_TILES], F32)
    P.wait("tensor", "dve")
    P.iop("tensor", lambda e: e.matmul(
        ps_te[:], cs["ones1x128_f32"][:], ter[:], start=True, stop=True), "pe")
    P.wait("vector", "pe")
    P.iop("vector", lambda e: e.tensor_copy(out=tecol[:], in_=ps_te[:]), "dve")
    P.wait("tensor", "dve")
    P.pfree(ps_c, ps_cum, ps_te)
    P.pop_scope()

    P.push_scope()
    NKD = 12 if l0 else 32
    HF = 12 if l0 else 32
    upB = io["cfcB"] if l0 else io["wgB"]
    upB2 = io["cfcB"] if l0 else io["wuB"]
    up2_off = 1536 if l0 else 0
    dnB = io["cprojB"] if l0 else io["wdB"]
    DNROWS = 1536 if l0 else 4096
    gidx_sb = P.sbuf("sm_gidx_sb", [128, 4], I32)
    idxW = P.sbuf("sm_idxW", [128, 8], I32)
    idxD = P.sbuf("sm_idxD", [128, NKD], I32)
    idxf = P.sbuf("sm_idxf", [128, NKD], F32)
    ebc = P.sbuf("sm_ebc", [128, 1], F32)
    xgt = P.sbuf("sm_xgt", [128, D], BF16)
    xgf = P.sbuf("sm_xgf", [128, 8, 512], BF16)
    wgu = P.sbuf("sm_wgu", [128, 8, 1024], BF16)
    wdb = P.sbuf("sm_wdb", [128, D], BF16)
    h_e = P.sbuf("sm_h", [128, HF, 512], BF16)
    wsrow = P.sbuf("sm_wsrow", [1, 512], F32)
    wsbf = P.sbuf("sm_wsbf", [1, 512], BF16)
    bc = P.sbuf("sm_bc", [128, 512], BF16)
    t1 = P.sbuf("sm_t1", [128, 512], F32)
    t2 = P.sbuf("sm_t2", [128, 512], F32)
    osb_f = P.sbuf("sm_osbf", [128, 8, 512], F32)
    otm = P.sbuf("sm_otm", [128, D], F32)
    zb = P.sbuf("sm_zb", [128, D], F32)
    res = P.sbuf("sm_res", [128, 512], F32)
    osb = P.sbuf("sm_osb", [128, 128], F32)

    P.iop("vector", lambda e: e.memset(zb[:], 0.0), "dve")
    P.wait("gpsimd", "dve")
    for b in range(32):
        P.dma("gpsimd", io["accm"][b * 128:(b + 1) * 128, :], zb[:], "az")
    P.wait("gpsimd", "az")

    for j in range(N_TILES):
        ps_eb = P.palloc("sm_ps_eb", [128, 1], F32)
        P.wait("tensor", "dve")
        P.iop("tensor", lambda e, j=j: e.matmul(
            ps_eb[:], cs["ones1x128_f32"][:], tecol[0:1, j:j + 1], start=True, stop=True), "pe")
        P.wait("vector", "pe")
        P.iop("vector", lambda e: e.tensor_copy(out=ebc[:], in_=ps_eb[:]), "dve")
        P.op("vector", lambda e: e.drain())
        P.op("vector", lambda e: e.scalar_tensor_tensor(
            out=idxf[:, 0:8], in0=ebc[:].broadcast_to([128, 8]), scalar=float(D),
            in1=IOTA[:, 0:8], op0=ALU.mult, op1=ALU.add))
        P.op("vector", lambda e: e.drain())
        P.iop("vector", lambda e: e.tensor_copy(out=idxW[:], in_=idxf[:, 0:8]), "dve")
        P.op("vector", lambda e: e.scalar_tensor_tensor(
            out=idxf[:, 0:NKD], in0=ebc[:].broadcast_to([128, NKD]),
            scalar=float(DNROWS), in1=IOTA[:, 0:NKD], op0=ALU.mult, op1=ALU.add))
        P.op("vector", lambda e: e.drain())
        P.iop("vector", lambda e: e.tensor_copy(out=idxD[:], in_=idxf[:, 0:NKD]), "dve")
        P.op("vector", lambda e: e.drain())
        P.pfree(ps_eb)
        ps_tp2 = [P.palloc("sm_ps_t2a", [128, 128], BF16),
                  P.palloc("sm_ps_t2b", [128, 128], BF16)]
        P.wait("gpsimd", "dve")
        P.wait("gpsimd", "gsc")
        P.wait("gpsimd", "xtm")
        for a in range(4):
            P.dma("gpsimd", gidx_sb[:, a:a + 1],
                  io["gidx"][j * 512 + a * 128:j * 512 + (a + 1) * 128, :], "gi")
        P.wait("gpsimd", "gi")
        for sub in range(4):
            P.iop("gpsimd", lambda e, sub=sub: e.indirect_dma_start(
                out=xgt[:], out_offset=None,
                in_=io["xlntm"], in_offset=IndirectOffsetOnAxis(gidx_sb[:, sub:sub + 1], 0)),
                "gi", 16)
            P.wait("tensor", "gi")
            for fc in range(8):
                pt = ps_tp2[fc % 2]
                if sub + fc > 0:
                    P.wait("tensor", "act")
                P.iop("tensor", lambda e, fc=fc, pt=pt: e.transpose(
                    pt[:], xgt[:, fc * 128:(fc + 1) * 128], cs["ident"][:]), "pe")
                P.wait("scalar", "pe")
                P.iop("scalar", lambda e, fc=fc, sub=sub, pt=pt: e.activation(
                    out=xgf[:, fc, sub * 128:sub * 128 + 128], in_=pt[:], func=AF.Copy),
                    "act")
            P.wait("gpsimd", "act")
        P.pfree(ps_tp2[0], ps_tp2[1])
        ps_bc2 = P.palloc("sm_ps_bc2", [128, 512], F32)
        P.dma("gpsimd", wsrow[:],
              io["wslot"].rearrange("s one -> (one) (s)")[0:1, j * 512:(j + 1) * 512], "gi")
        P.wait("vector", "gi")
        P.wait("vector", "act")
        P.iop("vector", lambda e: e.tensor_copy(out=wsbf[:], in_=wsrow[:]), "dve")
        P.wait("tensor", "dve")
        P.wait("tensor", "act")
        P.iop("tensor", lambda e: e.matmul(
            ps_bc2[:], cs["ones1x128"][:], wsbf[:], start=True, stop=True), "pe")
        P.wait("scalar", "pe")
        P.iop("scalar", lambda e: e.activation(out=bc[:], in_=ps_bc2[:], func=AF.Copy), "act")
        ps_g = [P.palloc("sm_ps_g0", [128, 512], F32), P.palloc("sm_ps_g1", [128, 512], F32)]
        ps_u = [P.palloc("sm_ps_u0", [128, 512], F32), P.palloc("sm_ps_u1", [128, 512], F32)]
        for fb in range(HF // 4):
            P.wait("gpsimd", "pe")
            P.wait("gpsimd", "act")
            for kb in range(8):
                P.iop("gpsimd", lambda e, kb=kb, fb=fb: e.indirect_dma_start(
                    out=wgu[:, kb, 0:512], out_offset=None,
                    in_=upB.rearrange("r (b w) -> r b w", w=512)[:, fb, :],
                    in_offset=IndirectOffsetOnAxis(idxW[:, kb:kb + 1], 0),
                    element_offset=fb * 512), "w", 16) if False else \
                P.iop("gpsimd", lambda e, kb=kb, fb=fb: e.indirect_dma_start(
                    out=wgu[:, kb, 0:512], out_offset=None,
                    in_=upB,
                    in_offset=IndirectOffsetOnAxis(idxW[:, kb:kb + 1], 0),
                    element_offset=fb * 512), "w", 16)
                P.iop("gpsimd", lambda e, kb=kb, fb=fb: e.indirect_dma_start(
                    out=wgu[:, kb, 512:1024], out_offset=None,
                    in_=upB2,
                    in_offset=IndirectOffsetOnAxis(idxW[:, kb:kb + 1], 0),
                    element_offset=up2_off + fb * 512), "w", 16)
            P.wait("tensor", "w")
            if j == 0 and fb == 1:
                dbg_dump(P, "smx_wgu0", wgu[:], hard=True)
                dbg_dump(P, "smx_xgf0", xgf[:], hard=True)
                dbg_dump(P, "smx_bc0", bc[:], hard=True)
            for fc in range(4):
                pg = ps_g[fc % 2]
                pu = ps_u[fc % 2]
                fpc = fb * 4 + fc
                if fb + fc > 0:
                    P.wait("tensor", "dve")
                for k in range(8):
                    P.op("tensor", lambda e, k=k, fc=fc, pg=pg: e.matmul(
                        pg[:], wgu[:, k, fc * 128:fc * 128 + 128], xgf[:, k, :],
                        start=(k == 0), stop=(k == 7)))
                for k in range(8):
                    fn = lambda e, k=k, fc=fc, pu=pu: e.matmul(
                        pu[:], wgu[:, k, 512 + fc * 128:512 + fc * 128 + 128],
                        xgf[:, k, :], start=(k == 0), stop=(k == 7))
                    if k == 7:
                        P.iop("tensor", fn, "pe")
                    else:
                        P.op("tensor", fn)
                P.wait("scalar", "pe")
                P.wait("scalar", "dve")
                P.iop("scalar", lambda e, pg=pg: e.activation(
                    out=t1[:], in_=pg[:], func=AF.Silu), "act")
                P.wait("vector", "act")
                P.op("vector", lambda e, pu=pu: e.tensor_tensor(
                    out=t2[:], in0=pu[:], in1=bc[:], op=ALU.mult))
                P.op("vector", lambda e: e.drain())
                P.iop("vector", lambda e, fpc=fpc: e.tensor_tensor(
                    out=h_e[:, fpc, :], in0=t1[:], in1=t2[:], op=ALU.mult), "dve")
        P.pfree(ps_g[0], ps_g[1], ps_u[0], ps_u[1])
        P.pfree(ps_bc2)
        if j == 0:
            dbg_dump(P, "smx_h0", h_e[:], hard=True)
        ps_d = [P.palloc(f"sm_ps_d{i}", [128, 512], F32) for i in range(8)]
        P.wait("tensor", "dve")
        for kb in range(NKD):
            P.wait("gpsimd", "pe")
            P.iop("gpsimd", lambda e, kb=kb: e.indirect_dma_start(
                out=wdb[:], out_offset=None,
                in_=dnB, in_offset=IndirectOffsetOnAxis(idxD[:, kb:kb + 1], 0)), "w", 16)
            P.wait("tensor", "w")
            for fo in range(8):
                fn = lambda e, kb=kb, fo=fo: e.matmul(
                    ps_d[fo][:], wdb[:, fo * 128:(fo + 1) * 128], h_e[:, kb, :],
                    start=(kb == 0), stop=(kb == NKD - 1))
                if fo == 7:
                    P.iop("tensor", fn, "pe")
                else:
                    P.op("tensor", fn)
        P.wait("vector", "pe")
        for fo in range(8):
            P.iop("vector", lambda e, fo=fo: e.tensor_copy(
                out=osb_f[:, fo, :], in_=ps_d[fo][:]), "dve")
        P.pfree(*ps_d)
        if j == 0:
            dbg_dump(P, "smx_osbf0", osb_f[:], hard=True)
        ps_ot = [P.palloc("sm_ps_ot0", [128, 128], F32),
                 P.palloc("sm_ps_ot1", [128, 128], F32)]
        P.wait("tensor", "dve")
        for sub in range(4):
            P.wait("gpsimd", "sca")
            P.wait("vector", "sca")
            for fc in range(8):
                po = ps_ot[fc % 2]
                if sub + fc > 0:
                    P.wait("tensor", "dve")
                P.iop("tensor", lambda e, fc=fc, sub=sub, po=po: e.transpose(
                    po[:], osb_f[:, fc, sub * 128:sub * 128 + 128],
                    cs["ident_f32"][:]), "pe")
                P.wait("vector", "pe")
                P.iop("vector", lambda e, fc=fc, po=po: e.tensor_copy(
                    out=otm[:, fc * 128:(fc + 1) * 128], in_=po[:]), "dve")
            P.wait("gpsimd", "dve")
            P.iop("gpsimd", lambda e, sub=sub: e.indirect_dma_start(
                out=io["accm"], out_offset=IndirectOffsetOnAxis(gidx_sb[:, sub:sub + 1], 0),
                in_=otm[:], in_offset=None, compute_op=ALU.add), "sca", 16)
        P.pfree(ps_ot[0], ps_ot[1])

    if DEBUG:
        accd = P.sbuf("smx_accd", [128, 32 * D // 128], F32)
        P.wait("gpsimd", "sca")
        for b in range(4):
            P.dma("gpsimd", accd[:, b * 8 * 8:(b + 1) * 8 * 8].rearrange("p x -> p x")
                  if False else accd[:, b * 64:(b + 1) * 64],
                  io["accm"][b * 128:(b + 1) * 128, 0:64], "fd")
        P.wait("gpsimd", "fd")
        dbg_dump(P, "smx_accm", accd[:, 0:256], hard=True)
    ps_f = [P.palloc("sm_ps_f0", [128, 128], F32), P.palloc("sm_ps_f1", [128, 128], F32)]
    acct = P.sbuf("sm_acct", [128, D], F32)
    P.wait("gpsimd", "sca")
    for tc in range(32):
        P.wait("gpsimd", "dve")
        P.dma("gpsimd", acct[:], io["accm"][tc * 128:(tc + 1) * 128, :], "fi")
        P.wait("tensor", "fi")
        for fc in range(8):
            pf = ps_f[fc % 2]
            P.wait("sync", "dve")
            P.dma("sync", res[:, 0:128], X[fc * 128:(fc + 1) * 128, tc * 128:(tc + 1) * 128],
                  "xr")
            if tc + fc > 0:
                P.wait("tensor", "dve")
            P.iop("tensor", lambda e, fc=fc, pf=pf: e.transpose(
                pf[:], acct[:, fc * 128:(fc + 1) * 128], cs["ident_f32"][:]), "pe")
            P.wait("vector", "pe")
            P.wait("vector", "xr")
            P.wait("vector", "st")
            P.iop("vector", lambda e, pf=pf: e.tensor_tensor(
                out=osb[:], in0=pf[:], in1=res[:, 0:128], op=ALU.add), "dve")
            P.wait("gpsimd", "dve")
            P.dma("gpsimd", Xo[fc * 128:(fc + 1) * 128, tc * 128:(tc + 1) * 128],
                  osb[:], "st")
    P.pfree(ps_f[0], ps_f[1])
    P.pop_scope()
    P.wait("sync", "st")
    P.wait("gpsimd", "st")


def emit_phase12(P, cs, io):
    emit_moe_sparse(P, cs, io, 0)


def emit_phase16(P, cs, io):
    emit_moe_sparse(P, cs, io, 1)


_PHASES = {
    1: ("phase1", phase1_dram, phase1_host, emit_phase1),
    3: ("phase3", phase3_dram, phase3_host, emit_phase3),
    5: ("phase5", phase5_dram, phase5_host, emit_phase5),
    7: ("phase7", phase7_dram, phase7_host, emit_phase7),
    2: ("phase2", phase2_dram, phase2_host, emit_phase2),
    6: ("phase6", phase6_dram, phase6_host, emit_phase6),
    12: ("phase12", phase12_dram, phase2_host, emit_phase12),
    16: ("phase16", phase16_dram, phase6_host, emit_phase16),
}


def build_phase(idx):
    nc = bass.Bass("TRN2", target_bir_lowering=False, debug=False)
    ctx = ExitStack()
    c = const_inputs(nc)
    P = Prog(nc, ctx)
    _, iofn, _, emitfn = _PHASES[idx]
    io = iofn(nc)
    io["X"] = nc.dram_tensor("X_in", [D, NT], F32, kind="ExternalInput").ap()
    io["Xo"] = nc.dram_tensor("X_out", [D, NT], F32, kind="ExternalOutput").ap()
    cs = load_consts(P, c)
    P.push_scope()
    emitfn(P, cs, io)
    P.pop_scope()
    P.finalize()
    ctx.close()
    return nc


def build_fused():
    nc = bass.Bass("TRN2", target_bir_lowering=False, debug=False)
    ctx = ExitStack()
    c = const_inputs(nc)
    P = Prog(nc, ctx)
    ios = {}
    for idx in sorted(set(PHASE_SEQ)):
        ios[idx] = _PHASES[idx][1](nc)
    Xin = nc.dram_tensor("X_in", [D, NT], F32, kind="ExternalInput").ap()
    Xout = nc.dram_tensor("X_out", [D, NT], F32, kind="ExternalOutput").ap()
    XA = nc.dram_tensor("X_scratchA", [D, NT], F32).ap()
    XB = nc.dram_tensor("X_scratchB", [D, NT], F32).ap()
    cs = load_consts(P, c)
    n = len(PHASE_SEQ)
    for i, idx in enumerate(PHASE_SEQ):
        src_ap = Xin if i == 0 else (XA if i % 2 == 1 else XB)
        dst_ap = Xout if i == n - 1 else (XA if i % 2 == 0 else XB)
        io = dict(ios[idx])
        io["X"] = src_ap
        io["Xo"] = dst_ap
        P.wait("gpsimd", "st")
        P.wait("sync", "st")
        P.push_scope()
        _PHASES[idx][3](P, cs, io)
        P.pop_scope()
    P.finalize()
    ctx.close()
    return nc


def shard_stream(x):
    out = []
    for cid in range(NCORES):
        xs = x[cid * BC:(cid + 1) * BC].reshape(NT, D)
        out.append(np.ascontiguousarray(xs.T))
    return out


def unshard_stream(Xs):
    return np.concatenate(
        [Xc.T.reshape(BC, T, D) for Xc in Xs], axis=0).astype(np.float32)


PHASE_SEQ = [1, 12, 3, 12, 5, 16, 7, 16]
FUSED = True


def kernel(**inputs):
    consts = const_arrays()
    Xs = shard_stream(np.asarray(inputs["L0_attn_a_x"], np.float32))
    if FUSED:
        nc = build_fused()
        hosts = {}
        for idx in set(PHASE_SEQ):
            hosts[idx] = _PHASES[idx][2](inputs)
        in_maps = []
        for cid in range(NCORES):
            m = dict(consts)
            for idx in set(PHASE_SEQ):
                m.update(hosts[idx])
            m["X_in"] = Xs[cid]
            if 2 in PHASE_SEQ or 12 in PHASE_SEQ:
                m["p2_prevmask"] = (np.arange(8) < cid).astype(np.float32)[:, None]
            in_maps.append(m)
        res = run_bass_kernel_spmd(nc, in_maps, core_ids=list(range(NCORES)))
        return unshard_stream([r["X_out"] for r in res.results])
    progs = {}
    hosts = {}
    for idx in PHASE_SEQ:
        if idx not in progs:
            progs[idx] = build_phase(idx)
            hosts[idx] = _PHASES[idx][2](inputs)
        in_maps = []
        for cid in range(NCORES):
            m = dict(consts)
            m.update(hosts[idx])
            m["X_in"] = Xs[cid]
            if idx in (2, 12):
                m["p2_prevmask"] = (np.arange(8) < cid).astype(np.float32)[:, None]
            in_maps.append(m)
        res = run_bass_kernel_spmd(progs[idx], in_maps, core_ids=list(range(NCORES)))
        Xs = [r["X_out"] for r in res.results]
    return unshard_stream(Xs)


# version 05 (final)
# speedup vs baseline: 1.0199x; 1.0199x over previous
import numpy as np
import ml_dtypes
from contextlib import ExitStack
import concourse.bass as bass
import concourse.mybir as mybir
from concourse.bass_utils import run_bass_kernel_spmd
from concourse.bass_core_rust import IndirectOffsetOnAxis

F32 = mybir.dt.float32
BF16 = mybir.dt.bfloat16
AF = mybir.ActivationFunctionType
ALU = mybir.AluOpType
AX = mybir.AxisListType.X
BF16NP = ml_dtypes.bfloat16

NCORES = 8
B, T, D = 16, 2048, 1024
BC = B // NCORES
NT = BC * T
EPS_LN = 1e-5
EPS_ATTN = 1e-6

DEBUG = set()


class Prog:
    ENGS = ("sync", "tensor", "scalar", "vector", "gpsimd")

    def __init__(self, nc, ctx):
        self.nc = nc
        self.ctx = ctx
        self._scopes = []
        self.ops = {e: [] for e in self.ENGS}
        self.semh = {}
        self.cnt = {}
        self.last_inst = {}

    def sem(self, name):
        if name not in self.semh:
            self.semh[name] = self.ctx.enter_context(self.nc.semaphore(name))
            self.cnt[name] = 0
        return self.semh[name]

    def op(self, eng, fn):
        def w(e, fn=fn, eng=eng):
            r = fn(e)
            if r is not None:
                self.last_inst[eng] = r
        self.ops[eng].append(w)

    def wait(self, eng, name, val=None):
        s = self.sem(name)
        v = self.cnt[name] if val is None else val
        self.ops[eng].append(lambda e, s=s, v=v: e.wait_ge(s, v))

    def iop(self, eng, fn, sem, inc=1):
        s = self.sem(sem)
        self.cnt[sem] += inc
        def w(e, fn=fn, s=s, inc=inc, eng=eng):
            r = fn(e)
            r.then_inc(s, inc)
            self.last_inst[eng] = r
        self.ops[eng].append(w)
        return self.cnt[sem]

    def dma(self, q, out, in_, sem, **kw):
        s = self.sem(sem)
        self.cnt[sem] += 16
        def w(e, out=out, in_=in_, s=s, kw=kw):
            e.dma_start(out=out, in_=in_, **kw).then_inc(s, 16)
        self.ops[q].append(w)
        return self.cnt[sem]

    _sn = 0

    def sbuf(self, name, shape, dt):
        Prog._sn += 1
        cm = self.nc.sbuf_tensor(f"{name}_s{Prog._sn}", shape, dt)
        h = cm.__enter__()
        if self._scopes:
            self._scopes[-1].append(cm)
        else:
            self.ctx.callback(cm.__exit__, None, None, None)
        return h

    def push_scope(self):
        if not hasattr(self, "_scopes"):
            self._scopes = []
        self._scopes.append([])

    def pop_scope(self):
        for cm in reversed(self._scopes.pop()):
            cm.__exit__(None, None, None)

    def psum(self, name, shape, dt=F32):
        return self.ctx.enter_context(self.nc.psum_tensor(name, shape, dt))

    _pn = 0

    def palloc(self, name, shape, dt=F32):
        Prog._pn += 1
        cm = self.nc.psum_tensor(f"{name}_{Prog._pn}", shape, dt)
        h = cm.__enter__()
        if not hasattr(self, "_pcms"):
            self._pcms = {}
        self._pcms[id(h)] = cm
        return h

    def flush_dve(self):
        self.op("vector", lambda e: e.drain())

    flush_dve_big = flush_dve

    def pfree(self, *hs):
        for h in reversed(hs):
            self._pcms.pop(id(h)).__exit__(None, None, None)

    def finalize(self):
        block = self.ctx.enter_context(self.nc.Block())
        for eng in self.ENGS:
            getattr(block, eng)(lambda e, eng=eng: [fn(e) for fn in self.ops[eng]])



def const_inputs(nc):
    c = {}
    c["ones128_f32"] = nc.dram_tensor("ones128_f32", [128, 1], F32, kind="ExternalInput").ap()
    c["ones1x128_bf"] = nc.dram_tensor("ones1x128_bf", [1, 128], BF16, kind="ExternalInput").ap()
    c["ones128_bf"] = nc.dram_tensor("ones128_bf", [128, 1], BF16, kind="ExternalInput").ap()
    c["ident_bf"] = nc.dram_tensor("ident_bf", [128, 128], BF16, kind="ExternalInput").ap()
    c["triu_bf"] = nc.dram_tensor("triu_bf", [128, 128], BF16, kind="ExternalInput").ap()
    c["cvals"] = nc.dram_tensor("cvals", [128, 8], F32, kind="ExternalInput").ap()
    c["esel"] = nc.dram_tensor("esel", [8, 8 * 128], BF16, kind="ExternalInput").ap()
    c["strictU_bf"] = nc.dram_tensor("strictU_bf", [128, 128], BF16, kind="ExternalInput").ap()
    c["iota_mat"] = nc.dram_tensor("iota_mat", [128, 32], F32, kind="ExternalInput").ap()
    c["ident_f32"] = nc.dram_tensor("ident_f32", [128, 128], F32, kind="ExternalInput").ap()
    c["ones1x128_f32"] = nc.dram_tensor("ones1x128_f32", [1, 128], F32, kind="ExternalInput").ap()
    return c


def const_arrays():
    a = {}
    a["ones128_f32"] = np.ones((128, 1), np.float32)
    a["ones1x128_bf"] = np.ones((1, 128), BF16NP)
    a["ones128_bf"] = np.ones((128, 1), BF16NP)
    a["ident_bf"] = np.eye(128).astype(BF16NP)
    a["triu_bf"] = np.triu(np.ones((128, 128))).astype(BF16NP)
    cv = np.zeros((128, 8), np.float32)
    cv[:, 0] = 1e-5
    cv[:, 1] = 1e-4
    cv[:, 2] = 1e-6
    cv[:, 3] = 1.0
    a["cvals"] = cv
    es = np.zeros((8, 8 * 128), np.float32)
    for e in range(8):
        es[e, e * 128:(e + 1) * 128] = 1.0
    a["esel"] = es.astype(BF16NP)
    a["strictU_bf"] = np.triu(np.ones((128, 128)), 1).astype(BF16NP)
    a["iota_mat"] = (np.arange(128)[:, None] + 128 * np.arange(32)[None, :]).astype(np.float32)
    a["ident_f32"] = np.eye(128, dtype=np.float32)
    a["ones1x128_f32"] = np.ones((1, 128), np.float32)
    return a


def load_consts(P, c):
    s = {}
    s["ones_f32"] = P.sbuf("c_ones_f32", [128, 1], F32)
    s["ones1x128"] = P.sbuf("c_ones1x128", [1, 128], BF16)
    s["ones_bf"] = P.sbuf("c_ones_bf", [128, 1], BF16)
    s["ident"] = P.sbuf("c_ident", [128, 128], BF16)
    s["triu"] = P.sbuf("c_triu", [128, 128], BF16)
    s["cvals"] = P.sbuf("c_cvals", [128, 8], F32)
    s["esel"] = P.sbuf("c_esel", [8, 8 * 128], BF16)
    s["strictU"] = P.sbuf("c_strictU", [128, 128], BF16)
    s["iota"] = P.sbuf("c_iota", [128, 32], F32)
    s["ident_f32"] = P.sbuf("c_ident_f32", [128, 128], F32)
    s["ones1x128_f32"] = P.sbuf("c_ones1x128_f32", [1, 128], F32)
    P.dma("gpsimd", s["ones_f32"][:], c["ones128_f32"], "cld")
    P.dma("gpsimd", s["ones1x128"][:], c["ones1x128_bf"], "cld")
    P.dma("gpsimd", s["ones_bf"][:], c["ones128_bf"], "cld")
    P.dma("gpsimd", s["ident"][:], c["ident_bf"], "cld")
    P.dma("gpsimd", s["triu"][:], c["triu_bf"], "cld")
    P.dma("gpsimd", s["cvals"][:], c["cvals"], "cld")
    P.dma("gpsimd", s["esel"][:], c["esel"], "cld")
    P.dma("gpsimd", s["strictU"][:], c["strictU_bf"], "cld")
    P.dma("gpsimd", s["iota"][:], c["iota_mat"], "cld")
    P.dma("gpsimd", s["ident_f32"][:], c["ident_f32"], "cld")
    P.dma("gpsimd", s["ones1x128_f32"][:], c["ones1x128_f32"], "cld")
    for e in Prog.ENGS:
        P.wait(e, "cld")
    return s



class LNState:
    def __init__(self, P):
        self.xt = P.sbuf("ln_xt", [128, 8, 512], BF16)
        self.xsq = P.sbuf("ln_xsq", [128, 8, 512], BF16)
        self.mu = P.sbuf("ln_mu", [1, 512], F32)
        self.e2 = P.sbuf("ln_e2", [1, 512], F32)
        self.musq = P.sbuf("ln_musq", [1, 512], F32)
        self.var = P.sbuf("ln_var", [1, 512], F32)
        self.srt = P.sbuf("ln_srt", [1, 512], F32)
        self.rstd = P.sbuf("ln_rstd", [1, 512], F32)
        self.nb = P.sbuf("ln_nb", [1, 512], F32)
        self.rp = P.sbuf("ln_rp", [1, 1024], BF16)
        self.tmp = P.sbuf("ln_tmp", [128, 512], F32)
        self.ps_sum = None


def emit_ln(P, L, cs, X, xln, x_cols, xln_base=0, gcol=None, bcol=None):
    start, n = x_cols
    assert n % 512 == 0
    L.ps_sum = P.palloc("ln_ps_sum", [1, 512], F32)
    L.ps_sumsq = P.palloc("ln_ps_sumsq", [1, 512], F32)
    L.ps_bc0 = P.palloc("ln_ps_bc0", [128, 512], F32)
    L.ps_bc1 = P.palloc("ln_ps_bc1", [128, 512], F32)
    for t0 in range(start, start + n, 512):
        cols = slice(t0, t0 + 512)
        P.dma("gpsimd", L.xt[:], X.rearrange("(c p) n -> p c n", p=128)[:, :, cols], "xl")
        P.wait("scalar", "xl")
        P.wait("scalar", "pe")
        for c in range(8):
            P.iop("scalar", lambda e, c=c: e.activation(
                out=L.xsq[:, c, :], in_=L.xt[:, c, :], func=AF.Square), "act")
        P.wait("tensor", "act")
        P.wait("tensor", "dve")
        for c in range(8):
            P.op("tensor", lambda e, c=c: e.matmul(
                L.ps_sum[:], cs["ones_bf"][:], L.xt[:, c, :],
                start=(c == 0), stop=(c == 7)))
        for c in range(8):
            fn = lambda e, c=c: e.matmul(
                L.ps_sumsq[:], cs["ones_bf"][:], L.xsq[:, c, :],
                start=(c == 0), stop=(c == 7))
            if c == 7:
                P.iop("tensor", fn, "pe")
            else:
                P.op("tensor", fn)
        P.wait("vector", "pe")
        P.op("vector", lambda e: e.tensor_scalar(
            out=L.mu[:], in0=L.ps_sum[:], scalar1=1.0 / 1024, scalar2=None, op0=ALU.mult))
        P.op("vector", lambda e: e.tensor_scalar(
            out=L.e2[:], in0=L.ps_sumsq[:], scalar1=1.0 / 1024, scalar2=None, op0=ALU.mult))
        P.op("vector", lambda e: e.tensor_tensor(
            out=L.musq[:], in0=L.mu[:], in1=L.mu[:], op=ALU.mult))
        P.op("vector", lambda e: e.drain())
        P.iop("vector", lambda e: e.tensor_tensor(
            out=L.var[:], in0=L.e2[:], in1=L.musq[:], op=ALU.subtract), "dve")
        P.wait("scalar", "dve")
        P.iop("scalar", lambda e: e.activation(
            out=L.srt[:], in_=L.var[:], func=AF.Sqrt, bias=cs["cvals"][0:1, 0:1]), "act")
        P.wait("vector", "act")
        P.flush_dve()
        P.iop("vector", lambda e: e.reciprocal(out=L.rstd[:], in_=L.srt[:]), "dve")
        P.flush_dve()
        P.op("vector", lambda e: e.scalar_tensor_tensor(
            out=L.nb[:], in0=L.mu[:], scalar=-1.0, in1=L.rstd[:],
            op0=ALU.mult, op1=ALU.mult))
        P.op("vector", lambda e: e.drain())
        P.op("vector", lambda e: e.tensor_copy(out=L.rp[0:1, 0:512], in_=L.rstd[:]))
        P.iop("vector", lambda e: e.tensor_copy(out=L.rp[0:1, 512:1024], in_=L.nb[:]), "dve")
        P.wait("tensor", "dve")
        P.op("tensor", lambda e: e.matmul(
            L.ps_bc0[:], cs["ones1x128"][:], L.rp[0:1, 0:512], start=True, stop=True))
        P.iop("tensor", lambda e: e.matmul(
            L.ps_bc1[:], cs["ones1x128"][:], L.rp[0:1, 512:1024],
            start=True, stop=True), "pe")
        P.wait("vector", "pe")
        oc = slice(t0 - xln_base, t0 - xln_base + 512)
        for c in range(8):
            P.op("vector", lambda e, c=c: e.tensor_tensor(
                out=L.tmp[:], in0=L.xt[:, c, :], in1=L.ps_bc0[:], op=ALU.mult))
            if gcol is None:
                P.iop("vector", lambda e, c=c, oc=oc: e.tensor_tensor(
                    out=xln[:, c, oc], in0=L.tmp[:], in1=L.ps_bc1[:],
                    op=ALU.add), "dve")
            else:
                P.op("vector", lambda e, c=c: e.tensor_tensor(
                    out=L.tmp[:], in0=L.tmp[:], in1=L.ps_bc1[:], op=ALU.add))
                P.iop("vector", lambda e, c=c, oc=oc: e.tensor_scalar(
                    out=xln[:, c, oc], in0=L.tmp[:], scalar1=gcol[:, c:c + 1],
                    scalar2=bcol[:, c:c + 1], op0=ALU.mult, op1=ALU.add), "dve")
        P.wait("gpsimd", "dve")
        P.wait("gpsimd", "pe")
    P.pfree(L.ps_sum, L.ps_sumsq, L.ps_bc0, L.ps_bc1)



def phase1_dram(nc):
    io = {}
    io["wqT"] = nc.dram_tensor("p1_wqT", [D, D], F32, kind="ExternalInput").ap()
    io["wkT"] = nc.dram_tensor("p1_wkT", [D, D], F32, kind="ExternalInput").ap()
    io["wvT"] = nc.dram_tensor("p1_wvT", [D, D], F32, kind="ExternalInput").ap()
    io["wpT"] = nc.dram_tensor("p1_wpT", [D, D], F32, kind="ExternalInput").ap()
    io["bp"] = nc.dram_tensor("p1_bp", [128, 8], F32, kind="ExternalInput").ap()
    return io


def phase1_host(inputs):
    h = {}
    h["p1_wqT"] = np.ascontiguousarray(inputs["L0_attn_a_Wq"].T)
    h["p1_wkT"] = np.ascontiguousarray(inputs["L0_attn_a_Wk"].T)
    h["p1_wvT"] = np.ascontiguousarray(inputs["L0_attn_a_Wv"].T)
    h["p1_wpT"] = np.ascontiguousarray(inputs["L0_attn_a_Wp"].T)
    h["p1_bp"] = np.ascontiguousarray(inputs["L0_attn_a_bp"].reshape(8, 128).T)
    return h


def dbg_dump(P, tag, ap, hard=False):
    if tag not in DEBUG:
        return
    sh = list(ap.shape)
    t = P.nc.dram_tensor("dbg_" + tag, sh, ap.dtype, kind="ExternalOutput").ap()
    for q in ("gpsimd",):
        P.wait(q, "dve")
        P.wait(q, "pe")
        P.wait(q, "act")
    P.dma("gpsimd", t, ap, "st")
    if hard:
        for e in Prog.ENGS:
            P.wait(e, "st")


def emit_phase1(P, cs, io):
    nc = P.nc
    X, Xo = io["X"], io["Xo"]
    L = LNState(P)
    xln = P.sbuf("p1_xln", [128, 8, T], BF16)
    phiq = P.sbuf("p1_phiq", [128, 8, T], BF16)
    phik = P.sbuf("p1_phik", [128, 16, D], BF16)
    vtm = P.sbuf("p1_vtm", [128, 16, D], BF16)
    wblk = P.sbuf("p1_wblk", [128, 8, 128], BF16)
    wrhs = P.sbuf("p1_wrhs", [128, 8, 512], BF16)
    wrhs2 = P.sbuf("p1_wrhs2", [128, 8, 512], BF16)
    bp_sb = P.sbuf("p1_bp_sb", [128, 8], F32)
    t1 = P.sbuf("p1_t1", [128, 512], F32)
    t2 = P.sbuf("p1_t2", [128, 512], F32)
    t3 = P.sbuf("p1_t3", [128, 512], F32)
    S_sb = P.sbuf("p1_S", [64, 65], F32)
    S_bf = P.sbuf("p1_Sbf", [64, 65], BF16)
    kT_sb = P.sbuf("p1_kT", [64, 128], BF16)
    A_sb = P.sbuf("p1_A", [128, 128], BF16)
    den_sb = P.sbuf("p1_den", [128, 1], F32)
    rec_sb = P.sbuf("p1_rec", [128, 1], F32)
    ytm_sb = P.sbuf("p1_ytm", [128, 64], BF16)
    q_s = P.sbuf("p1_qs", [64, 128], BF16)
    res_sb = P.sbuf("p1_res", [128, 512], F32)
    out_sb = P.sbuf("p1_out", [128, 512], F32)

    P.dma("gpsimd", bp_sb[:], io["bp"], "xl")

    def phi_evac(psrc, dst):
        P.op("vector", lambda e: e.tensor_scalar(
            out=t1[:], in0=psrc, scalar1=0.0, scalar2=None, op0=ALU.min))
        P.iop("vector", lambda e: e.tensor_scalar(
            out=t3[:], in0=psrc, scalar1=0.0, scalar2=None, op0=ALU.max), "dve")
        P.wait("scalar", "dve")
        P.iop("scalar", lambda e: e.activation(out=t2[:], in_=t1[:], func=AF.Exp), "act")
        P.wait("vector", "act")
        P.op("vector", lambda e: e.drain())
        P.iop("vector", lambda e: e.tensor_tensor(
            out=dst, in0=t2[:], in1=t3[:], op=ALU.add), "dve")

    for s in range(BC):
        sbase = s * T
        emit_ln(P, L, cs, X, xln, (sbase, T), xln_base=sbase)
        P.wait("tensor", "dve")
        if s == 0:
            dbg_dump(P, "xln_s0", xln[:])

        ps_q = P.palloc("p1_ps_q", [128, 512], F32)
        for f in range(8):
            P.wait("gpsimd", "pe")
            P.dma("gpsimd", wblk[:],
                  io["wqT"].rearrange("(c p) f -> p c f", p=128)[:, :, f * 128:(f + 1) * 128],
                  "w")
            P.wait("tensor", "w")
            for g in range(4):
                gc = slice(g * 512, (g + 1) * 512)
                for k in range(8):
                    fn = lambda e, k=k, gc=gc: e.matmul(
                        ps_q[:], wblk[:, k, :], xln[:, k, gc],
                        start=(k == 0), stop=(k == 7))
                    if k == 7:
                        P.iop("tensor", fn, "pe")
                    else:
                        P.op("tensor", fn)
                P.wait("vector", "pe")
                phi_evac(ps_q[:], phiq[:, f, gc])
                P.wait("tensor", "dve")

        P.pfree(ps_q)
        ps_k = P.palloc("p1_ps_k", [128, 512], F32)
        ps_v = P.palloc("p1_ps_v", [128, 512], F32)
        for fg in range(2):
            fcols = slice(fg * 512, (fg + 1) * 512)
            P.wait("gpsimd", "pe")
            P.dma("gpsimd", wrhs[:],
                  io["wkT"].rearrange("(c p) f -> p c f", p=128)[:, :, fcols], "w")
            P.dma("gpsimd", wrhs2[:],
                  io["wvT"].rearrange("(c p) f -> p c f", p=128)[:, :, fcols], "w")
            P.wait("tensor", "w")
            for tc in range(16):
                tcs = slice(tc * 128, tc * 128 + 128)
                for k in range(8):
                    P.op("tensor", lambda e, k=k, tcs=tcs: e.matmul(
                        ps_k[:], xln[:, k, tcs], wrhs[:, k, :],
                        start=(k == 0), stop=(k == 7)))
                for k in range(8):
                    fn = lambda e, k=k, tcs=tcs: e.matmul(
                        ps_v[:], xln[:, k, tcs], wrhs2[:, k, :],
                        start=(k == 0), stop=(k == 7))
                    if k == 7:
                        P.iop("tensor", fn, "pe")
                    else:
                        P.op("tensor", fn)
                P.wait("vector", "pe")
                phi_evac(ps_k[:], phik[:, tc, fcols])
                P.iop("vector", lambda e, tc=tc, fcols=fcols: e.tensor_copy(
                    out=vtm[:, tc, fcols], in_=ps_v[:]), "dve")
                P.wait("tensor", "dve")

        P.pfree(ps_k, ps_v)
        if s == 0:
            dbg_dump(P, "phiq_s0", phiq[:])
            dbg_dump(P, "phik_s0", phik[:])
            dbg_dump(P, "vtm_s0", vtm[:])
        ps_kT = P.palloc("p1_ps_kT", [64, 128], BF16)
        ps_A = P.palloc("p1_ps_A", [128, 128], F32)
        ps_y = P.palloc("p1_ps_y", [128, 64], F32)
        ps_den = P.palloc("p1_ps_den", [128, 1], F32)
        ps_kv = P.palloc("p1_ps_kv", [64, 64], F32)
        ps_zk = P.palloc("p1_ps_zk", [64, 1], F32)
        ps_yT = P.palloc("p1_ps_yT", [64, 128], BF16)
        for h in range(16):
            hoff = (h % 2) * 64
            hch = h // 2
            hf = slice(h * 64, h * 64 + 64)
            P.op("vector", lambda e: e.memset(S_sb[:], 0.0))
            P.iop("vector", lambda e: e.memset(S_bf[:], 0.0), "dve")
            P.wait("tensor", "dve")
            for n in range(16):
                ncs = slice(n * 128, n * 128 + 128)
                qblk = phiq[hoff:hoff + 64, hch, ncs]
                kblk = phik[:, n, hf]
                vblk = vtm[:, n, hf]
                P.iop("tensor", lambda e, kblk=kblk: e.transpose(
                    ps_kT[:], kblk, cs["ident"][:]), "pe")
                P.wait("vector", "pe")
                P.op("vector", lambda e, qblk=qblk: e.tensor_copy(out=q_s[:], in_=qblk))
                P.iop("vector", lambda e: e.tensor_copy(out=kT_sb[:], in_=ps_kT[:]), "dve")
                P.wait("tensor", "dve")
                if s == 0 and h == 0 and n <= 1:
                    dbg_dump(P, f"kT_n{n}", kT_sb[:], hard=True)
                    dbg_dump(P, f"qs_n{n}", q_s[:], hard=True)
                P.iop("tensor", lambda e: e.matmul(
                    ps_A[:], kT_sb[:], q_s[:], start=True, stop=True), "pe")
                P.wait("vector", "pe")
                P.iop("vector", lambda e: e.tensor_tensor(
                    out=A_sb[:], in0=ps_A[:], in1=cs["triu"][:], op=ALU.mult), "dve")
                P.wait("tensor", "dve")
                if s == 0 and h == 0 and n <= 1:
                    dbg_dump(P, f"A_n{n}", A_sb[:], hard=True)
                P.op("tensor", lambda e: e.matmul(
                    ps_y[:], q_s[:], S_bf[:, 0:64], start=True, stop=False))
                P.op("tensor", lambda e, vblk=vblk: e.matmul(
                    ps_y[:], A_sb[:], vblk, start=False, stop=True))
                P.op("tensor", lambda e: e.matmul(
                    ps_den[:], q_s[:], S_bf[:, 64:65], start=True, stop=False))
                P.iop("tensor", lambda e: e.matmul(
                    ps_den[:], A_sb[:], cs["ones_bf"][:], start=False, stop=True), "pe")
                P.op("tensor", lambda e, kblk=kblk, vblk=vblk: e.matmul(
                    ps_kv[:], kblk, vblk, start=True, stop=True))
                P.iop("tensor", lambda e, kblk=kblk: e.matmul(
                    ps_zk[:], kblk, cs["ones_bf"][:], start=True, stop=True), "pe")
                P.wait("vector", "pe")
                P.op("vector", lambda e: e.tensor_scalar(
                    out=den_sb[:], in0=ps_den[:], scalar1=EPS_ATTN, scalar2=None,
                    op0=ALU.add))
                P.flush_dve()
                P.iop("vector", lambda e: e.reciprocal(out=rec_sb[:], in_=den_sb[:]), "dve")
                P.flush_dve()
                P.iop("vector", lambda e: e.tensor_tensor(
                    out=ytm_sb[:], in0=ps_y[:], in1=rec_sb[:].broadcast_to([128, 64]),
                    op=ALU.mult), "dve")
                if s == 0 and h == 0 and n <= 1:
                    dbg_dump(P, f"den_n{n}", den_sb[:], hard=True)
                    dbg_dump(P, f"ytm_n{n}", ytm_sb[:], hard=True)
                P.op("vector", lambda e: e.tensor_tensor(
                    out=S_sb[:, 0:64], in0=S_sb[:, 0:64], in1=ps_kv[:], op=ALU.add))
                P.op("vector", lambda e: e.tensor_tensor(
                    out=S_sb[:, 64:65], in0=S_sb[:, 64:65], in1=ps_zk[:], op=ALU.add))
                P.op("vector", lambda e: e.drain())
                P.iop("vector", lambda e: e.tensor_copy(out=S_bf[:], in_=S_sb[:]), "dve")
                P.wait("tensor", "dve")
                if s == 0 and h == 0 and n <= 1:
                    dbg_dump(P, f"S_n{n}", S_sb[:], hard=True)
                P.iop("tensor", lambda e: e.transpose(
                    ps_yT[:], ytm_sb[:], cs["ident"][:]), "pe")
                P.wait("vector", "pe")
                P.iop("vector", lambda e, hoff=hoff, hch=hch, ncs=ncs: e.tensor_copy(
                    out=xln[hoff:hoff + 64, hch, ncs], in_=ps_yT[:]), "dve")
                P.wait("tensor", "dve")

        P.pfree(ps_kT, ps_A, ps_y, ps_den, ps_kv, ps_zk, ps_yT)
        if s == 0:
            dbg_dump(P, "Y_s0", xln[:])
        ps_q = P.palloc("p1_ps_o", [128, 512], F32)
        for f in range(8):
            P.wait("gpsimd", "pe")
            P.dma("gpsimd", wblk[:],
                  io["wpT"].rearrange("(c p) f -> p c f", p=128)[:, :, f * 128:(f + 1) * 128],
                  "w")
            P.wait("tensor", "w")
            for g in range(4):
                gc = slice(g * 512, (g + 1) * 512)
                xcols = slice(sbase + g * 512, sbase + g * 512 + 512)
                P.wait("sync", "dve")
                P.dma("sync", res_sb[:], X[f * 128:(f + 1) * 128, xcols], "xr")
                for k in range(8):
                    fn = lambda e, k=k, gc=gc: e.matmul(
                        ps_q[:], wblk[:, k, :], xln[:, k, gc],
                        start=(k == 0), stop=(k == 7))
                    if k == 7:
                        P.iop("tensor", fn, "pe")
                    else:
                        P.op("tensor", fn)
                P.wait("scalar", "pe")
                P.iop("scalar", lambda e, f=f: e.activation(
                    out=t1[:], in_=ps_q[:], func=AF.Identity, bias=bp_sb[:, f:f + 1]), "act")
                P.wait("vector", "act")
                P.wait("vector", "xr")
                P.iop("vector", lambda e: e.tensor_tensor(
                    out=out_sb[:], in0=t1[:], in1=res_sb[:], op=ALU.add), "dve")
                P.wait("gpsimd", "dve")
                P.dma("gpsimd", Xo[f * 128:(f + 1) * 128, xcols], out_sb[:], "st")
                P.wait("vector", "st")
                P.wait("tensor", "dve")

        P.pfree(ps_q)
    P.wait("sync", "st")
    P.wait("gpsimd", "st")




def phase6_dram(nc):
    io = {}
    io["gateT"] = nc.dram_tensor("p6_gateT", [D, 8], F32, kind="ExternalInput").ap()
    io["wgT"] = nc.dram_tensor("p6_wgT", [8, D, 4096], F32, kind="ExternalInput").ap()
    io["wuT"] = nc.dram_tensor("p6_wuT", [8, D, 4096], F32, kind="ExternalInput").ap()
    io["wdT"] = nc.dram_tensor("p6_wdT", [8, 4096, D], F32, kind="ExternalInput").ap()
    return io


def phase6_host(inputs):
    h = {}
    h["p6_gateT"] = np.ascontiguousarray(inputs["L1_ffn_gate_w"].T)
    h["p6_wgT"] = np.ascontiguousarray(np.asarray(inputs["L1_ffn_wg"]).transpose(0, 2, 1))
    h["p6_wuT"] = np.ascontiguousarray(np.asarray(inputs["L1_ffn_wu"]).transpose(0, 2, 1))
    h["p6_wdT"] = np.ascontiguousarray(np.asarray(inputs["L1_ffn_wd"]).transpose(0, 2, 1))
    return h


def emit_phase6(P, cs, io):
    X, Xo = io["X"], io["Xo"]
    L = LNState(P)
    xln = P.sbuf("p6_xln", [128, 8, NT], BF16)
    h_e = P.sbuf("p6_h", [128, 32, 512], BF16)
    acc = P.sbuf("p6_acc", [128, 8, 512], F32)
    wgu = P.sbuf("p6_wgu", [128, 8, 1024], BF16)
    wdb = P.sbuf("p6_wdb", [128, 32, 128], BF16)
    gsb = P.sbuf("p6_gate", [128, 8, 8], BF16)
    logit = P.sbuf("p6_logit", [128, 32, 8], F32)
    m1 = P.sbuf("p6_m1", [128, 32], F32)
    m2 = P.sbuf("p6_m2", [128, 32], F32)
    dd = P.sbuf("p6_dd", [128, 32], F32)
    w1 = P.sbuf("p6_w1", [128, 32], F32)
    w2 = P.sbuf("p6_w2", [128, 32], F32)
    sel1 = P.sbuf("p6_sel1", [128, 8], F32)
    sel2 = P.sbuf("p6_sel2", [128, 8], F32)
    mskd = P.sbuf("p6_mskd", [128, 32, 8], F32)
    c1 = P.sbuf("p6_c1", [128, 8], F32)
    coefb = P.sbuf("p6_coefb", [128, 8], BF16)
    coef_fm = P.sbuf("p6_coef_fm", [8, NT], BF16)
    bc = P.sbuf("p6_bc", [128, 512], BF16)
    t1 = P.sbuf("p6_t1", [128, 512], F32)
    t2 = P.sbuf("p6_t2", [128, 512], F32)
    res = P.sbuf("p6_res", [128, 512], F32)
    osb = P.sbuf("p6_osb", [128, 512], F32)

    emit_ln(P, L, cs, X, xln, (0, NT))
    P.wait("tensor", "dve")

    P.dma("gpsimd", gsb[:], io["gateT"].rearrange("(c p) e -> p c e", p=128), "w")
    P.wait("tensor", "w")
    ps_l = P.palloc("p6_ps_l", [128, 8], F32)
    for tc in range(32):
        tcs = slice(tc * 128, tc * 128 + 128)
        for k in range(8):
            fn = lambda e, k=k, tcs=tcs: e.matmul(
                ps_l[:], xln[:, k, tcs], gsb[:, k, :], start=(k == 0), stop=(k == 7))
            if k == 7:
                P.iop("tensor", fn, "pe")
            else:
                P.op("tensor", fn)
        P.wait("vector", "pe")
        P.iop("vector", lambda e, tc=tc: e.tensor_copy(
            out=logit[:, tc, :], in_=ps_l[:]), "dve")
        P.wait("tensor", "dve")
    P.pfree(ps_l)

    dbg_dump(P, "p6_xln", xln[:], hard=True)
    dbg_dump(P, "p6_logit", logit[:], hard=True)
    P.iop("vector", lambda e: e.reduce_max(m1[:], logit[:], axis=AX), "dve")
    P.flush_dve_big()
    P.wait("vector", "dve")
    for tc in range(32):
        P.op("vector", lambda e, tc=tc: e.tensor_scalar(
            out=sel1[:], in0=logit[:, tc, :], scalar1=m1[:, tc:tc + 1], scalar2=None,
            op0=ALU.is_ge))
        P.op("vector", lambda e: e.drain())
        P.op("vector", lambda e, tc=tc: e.scalar_tensor_tensor(
            out=mskd[:, tc, :], in0=sel1[:], scalar=-1e9, in1=logit[:, tc, :],
            op0=ALU.mult, op1=ALU.add))
        P.iop("vector", lambda e, tc=tc: e.tensor_copy(
            out=logit[:, tc, :], in_=sel1[:]), "dve")
    dbg_dump(P, "p6_sel1", logit[:], hard=True)
    dbg_dump(P, "p6_mskd", mskd[:], hard=True)
    P.iop("vector", lambda e: e.reduce_max(m2[:], mskd[:], axis=AX), "dve")
    P.flush_dve_big()
    P.wait("vector", "dve")
    P.op("vector", lambda e: e.tensor_tensor(out=dd[:], in0=m1[:], in1=m2[:],
                                             op=ALU.subtract))
    P.op("vector", lambda e: e.drain())
    P.iop("vector", lambda e: e.tensor_scalar(
        out=w2[:], in0=dd[:], scalar1=-1.0, scalar2=None, op0=ALU.mult), "dve")
    P.wait("scalar", "dve")
    P.iop("scalar", lambda e: e.activation(out=w1[:], in_=dd[:], func=AF.Sigmoid), "act")
    P.iop("scalar", lambda e: e.activation(out=w2[:], in_=w2[:], func=AF.Sigmoid), "act")
    P.wait("vector", "act")
    ps_ct = P.palloc("p6_ps_ct", [8, 128], BF16)
    for tc in range(32):
        P.op("vector", lambda e, tc=tc: e.tensor_scalar(
            out=sel2[:], in0=mskd[:, tc, :], scalar1=m2[:, tc:tc + 1], scalar2=None,
            op0=ALU.is_ge))
        P.op("vector", lambda e, tc=tc: e.tensor_scalar(
            out=c1[:], in0=logit[:, tc, :], scalar1=w1[:, tc:tc + 1], scalar2=None,
            op0=ALU.mult))
        P.op("vector", lambda e: e.drain())
        P.iop("vector", lambda e, tc=tc: e.scalar_tensor_tensor(
            out=coefb[:], in0=sel2[:], scalar=w2[:, tc:tc + 1], in1=c1[:],
            op0=ALU.mult, op1=ALU.add), "dve")
        P.wait("tensor", "dve")
        P.iop("tensor", lambda e: e.transpose(ps_ct[:], coefb[:], cs["ident"][:]), "pe")
        P.wait("vector", "pe")
        P.iop("vector", lambda e, tc=tc: e.tensor_copy(
            out=coef_fm[:, tc * 128:tc * 128 + 128], in_=ps_ct[:]), "dve")
        P.wait("tensor", "dve")
    P.pfree(ps_ct)

    ps_g = [P.palloc("p6_ps_g0", [128, 512], F32), P.palloc("p6_ps_g1", [128, 512], F32)]
    ps_u = [P.palloc("p6_ps_u0", [128, 512], F32), P.palloc("p6_ps_u1", [128, 512], F32)]
    ps_d = [P.palloc("p6_ps_d0", [128, 512], F32), P.palloc("p6_ps_d1", [128, 512], F32)]
    dbg_dump(P, "p6_coef", coef_fm[:])
    dbg_dump(P, "p6_m1", m1[:])
    dbg_dump(P, "p6_m2", m2[:])
    dbg_dump(P, "p6_w1", w1[:])
    dbg_dump(P, "p6_w2", w2[:])
    ps_bc = P.palloc("p6_ps_bc", [128, 512], F32)
    for ti in range(8):
        tcols = slice(ti * 512, ti * 512 + 512)
        for e_i in range(8):
            P.wait("tensor", "act")
            P.iop("tensor", lambda e, e_i=e_i, tcols=tcols: e.matmul(
                ps_bc[:], cs["esel"][:, e_i * 128:(e_i + 1) * 128],
                coef_fm[:, tcols], start=True, stop=True), "pe")
            P.wait("scalar", "pe")
            P.wait("scalar", "dve")
            P.iop("scalar", lambda e: e.activation(
                out=bc[:], in_=ps_bc[:], func=AF.Copy), "act")
            for fb in range(8):
                P.wait("gpsimd", "pe")
                P.wait("gpsimd", "act")
                P.dma("gpsimd", wgu[:, :, 0:512],
                      io["wgT"][e_i].rearrange("(c p) f -> p c f", p=128)[
                          :, :, fb * 512:(fb + 1) * 512], "w")
                P.dma("gpsimd", wgu[:, :, 512:1024],
                      io["wuT"][e_i].rearrange("(c p) f -> p c f", p=128)[
                          :, :, fb * 512:(fb + 1) * 512], "w")
                P.wait("tensor", "w")
                for fc in range(4):
                    pg = ps_g[fc % 2]
                    pu = ps_u[fc % 2]
                    fpc = fb * 4 + fc
                    if fb + fc > 0:
                        P.wait("tensor", "dve")
                    for k in range(8):
                        P.op("tensor", lambda e, k=k, fc=fc, tcols=tcols, pg=pg: e.matmul(
                            pg[:], wgu[:, k, fc * 128:fc * 128 + 128], xln[:, k, tcols],
                            start=(k == 0), stop=(k == 7)))
                    for k in range(8):
                        fn = lambda e, k=k, fc=fc, tcols=tcols, pu=pu: e.matmul(
                            pu[:], wgu[:, k, 512 + fc * 128:512 + fc * 128 + 128],
                            xln[:, k, tcols], start=(k == 0), stop=(k == 7))
                        if k == 7:
                            P.iop("tensor", fn, "pe")
                        else:
                            P.op("tensor", fn)
                    P.wait("scalar", "pe")
                    P.wait("scalar", "dve")
                    P.iop("scalar", lambda e, pg=pg: e.activation(
                        out=t1[:], in_=pg[:], func=AF.Silu), "act")
                    P.wait("vector", "act")
                    P.op("vector", lambda e, pu=pu: e.tensor_tensor(
                        out=t2[:], in0=pu[:], in1=bc[:], op=ALU.mult))
                    P.op("vector", lambda e: e.drain())
                    P.iop("vector", lambda e, fpc=fpc: e.tensor_tensor(
                        out=h_e[:, fpc, :], in0=t1[:], in1=t2[:], op=ALU.mult), "dve")
            for fo in range(8):
                pd = ps_d[fo % 2]
                P.wait("gpsimd", "pe")
                P.dma("gpsimd", wdb[:],
                      io["wdT"][e_i].rearrange("(c p) f -> p c f", p=128)[
                          :, :, fo * 128:(fo + 1) * 128], "w")
                P.wait("tensor", "w")
                if fo > 1:
                    P.wait("tensor", "dve")
                for k in range(32):
                    fn = lambda e, k=k, pd=pd: e.matmul(
                        pd[:], wdb[:, k, :], h_e[:, k, :], start=(k == 0), stop=(k == 31))
                    if k == 31:
                        P.iop("tensor", fn, "pe")
                    else:
                        P.op("tensor", fn)
                P.wait("vector", "pe")
                if e_i == 0:
                    P.iop("vector", lambda e, fo=fo, pd=pd: e.tensor_copy(
                        out=acc[:, fo, :], in_=pd[:]), "dve")
                else:
                    P.iop("vector", lambda e, fo=fo, pd=pd: e.tensor_tensor(
                        out=acc[:, fo, :], in0=acc[:, fo, :], in1=pd[:], op=ALU.add), "dve")
        for fo in range(8):
            P.wait("sync", "dve")
            P.dma("sync", res[:], X[fo * 128:(fo + 1) * 128, tcols], "xr")
            P.wait("vector", "xr")
            P.wait("vector", "st")
            P.iop("vector", lambda e, fo=fo: e.tensor_tensor(
                out=osb[:], in0=acc[:, fo, :], in1=res[:], op=ALU.add), "dve")
            P.wait("gpsimd", "dve")
            P.dma("gpsimd", Xo[fo * 128:(fo + 1) * 128, tcols], osb[:], "st")
    P.pfree(ps_g[0], ps_g[1], ps_u[0], ps_u[1], ps_d[0], ps_d[1], ps_bc)
    P.wait("sync", "st")
    P.wait("gpsimd", "st")






def phase7_dram(nc):
    io = {}
    io["winT"] = nc.dram_tensor("p7_winT", [D, 4096], F32, kind="ExternalInput").ap()
    io["woT"] = nc.dram_tensor("p7_woT", [D, D], F32, kind="ExternalInput").ap()
    io["b_in"] = nc.dram_tensor("p7_b_in", [128, 32], F32, kind="ExternalInput").ap()
    io["b_out"] = nc.dram_tensor("p7_b_out", [128, 8], F32, kind="ExternalInput").ap()
    io["ones_row"] = nc.dram_tensor("p7_ones_row", [1, T], BF16, kind="ExternalInput").ap()
    return io


def phase7_host(inputs):
    h = {}
    h["p7_winT"] = np.ascontiguousarray(np.asarray(inputs["L1_attn_b_W_in"], np.float32).T)
    h["p7_woT"] = np.ascontiguousarray(np.asarray(inputs["L1_attn_b_W_out"], np.float32).T)
    h["p7_b_in"] = np.ascontiguousarray(
        np.asarray(inputs["L1_attn_b_b_in"], np.float32).reshape(32, 128).T)
    h["p7_b_out"] = np.ascontiguousarray(
        np.asarray(inputs["L1_attn_b_b_out"], np.float32).reshape(8, 128).T)
    h["p7_ones_row"] = np.ones((1, T), BF16NP)
    return h


def emit_phase7(P, cs, io):
    X, Xo = io["X"], io["Xo"]
    SHEP_EPS_COL = 1
    L = LNState(P)
    xln = P.sbuf("p7_xln", [128, 8, T], BF16)
    Xn = P.sbuf("p7_Xn", [128, 8, T], BF16)
    yfm = P.sbuf("p7_yfm", [128, 8, T], BF16)
    qk = P.sbuf("p7_qk", [128, T], BF16)
    vg = P.sbuf("p7_vg", [128, T], BF16)
    n2K = P.sbuf("p7_n2K", [64, T], BF16)
    Ksq = P.sbuf("p7_Ksq", [64, T], BF16)
    k2row = P.sbuf("p7_k2row", [1, T], BF16)
    q2row = P.sbuf("p7_q2row", [1, T], BF16)
    onesr = P.sbuf("p7_onesr", [1, T], BF16)
    Wk_sb = [P.sbuf("p7_Wk0", [128, T], BF16), P.sbuf("p7_Wk1", [128, T], BF16)]
    vtm = P.sbuf("p7_vtm", [128, 16, 65], BF16)
    s1b = [P.sbuf("p7_s1a", [128, 512], F32), P.sbuf("p7_s1b", [128, 512], F32)]
    s2b = [P.sbuf("p7_s2a", [128, 512], F32), P.sbuf("p7_s2b", [128, 512], F32)]
    lr = P.sbuf("p7_lr", [1, T], F32)
    recr = P.sbuf("p7_recr", [1, T], BF16)
    bc = P.sbuf("p7_bc", [64, 512], BF16)
    tsc = P.sbuf("p7_tsc", [128, 512], F32)
    wblk = P.sbuf("p7_wblk", [128, 8, 128], BF16)
    b_in = P.sbuf("p7_b_in_sb", [128, 32], F32)
    b_out = P.sbuf("p7_b_out_sb", [128, 8], F32)
    t1 = P.sbuf("p7_t1", [128, 512], F32)
    res = P.sbuf("p7_res", [128, 512], F32)
    osb = P.sbuf("p7_osb", [128, 512], F32)

    P.dma("gpsimd", b_in[:], io["b_in"], "w")
    P.dma("gpsimd", b_out[:], io["b_out"], "w")
    P.dma("gpsimd", onesr[:], io["ones_row"], "w")
    P.wait("vector", "w")
    P.iop("vector", lambda e: e.memset(vtm[:, :, 64:65], 1.0), "dve")

    for s in range(BC):
        sbase = s * T
        emit_ln(P, L, cs, X, xln, (sbase, T), xln_base=sbase)
        P.wait("tensor", "dve")
        for tt in range(4):
            emit_ln_sbuf(P, L, cs,
                         xln[:, :, tt * 512:(tt + 1) * 512],
                         Xn[:, :, tt * 512:(tt + 1) * 512])
        P.wait("tensor", "dve")

        for h in range(16):
            hb = (h % 2) * 64
            ps_p = [P.palloc("p7_ps_p0", [128, 512], F32),
                    P.palloc("p7_ps_p1", [128, 512], F32)]
            for half, dst in ((0, qk), (1, vg)):
                fch = 2 * h + half
                P.wait("gpsimd", "pe")
                P.dma("gpsimd", wblk[:],
                      io["winT"].rearrange("(c p) f -> p c f", p=128)[
                          :, :, fch * 128:(fch + 1) * 128], "w")
                P.wait("tensor", "w")
                for g in range(4):
                    gc = slice(g * 512, (g + 1) * 512)
                    pin = ps_p[g % 2]
                    if not (half == 0 and g == 0):
                        P.wait("tensor", "act")
                    for k in range(8):
                        fn = lambda e, k=k, gc=gc, pin=pin: e.matmul(
                            pin[:], wblk[:, k, :], Xn[:, k, gc],
                            start=(k == 0), stop=(k == 7))
                        if k == 7:
                            P.iop("tensor", fn, "pe")
                        else:
                            P.op("tensor", fn)
                    P.wait("scalar", "pe")
                    P.iop("scalar", lambda e, gc=gc, fch=fch, dst=dst, pin=pin: e.activation(
                        out=dst[:, gc], in_=pin[:], func=AF.Identity,
                        bias=b_in[:, fch:fch + 1]), "act")
            P.wait("vector", "act")
            P.pfree(ps_p[0], ps_p[1])

            P.iop("vector", lambda e: e.tensor_scalar(
                out=n2K[:], in0=qk[64:128, :], scalar1=-2.0, scalar2=None,
                op0=ALU.mult), "dve")
            P.wait("scalar", "dve")
            P.iop("scalar", lambda e: e.activation(
                out=Ksq[:], in_=qk[64:128, :], func=AF.Square), "act")
            ps_r = P.palloc("p7_ps_r", [1, 512], F32)
            P.wait("tensor", "act")
            for g in range(4):
                gc = slice(g * 512, (g + 1) * 512)
                P.iop("tensor", lambda e, gc=gc: e.matmul(
                    ps_r[:], cs["ones_bf"][0:64, :], Ksq[:, gc],
                    start=True, stop=True), "pe")
                P.wait("vector", "pe")
                P.iop("vector", lambda e, gc=gc: e.tensor_copy(
                    out=k2row[0:1, gc], in_=ps_r[:]), "dve")
                P.wait("tensor", "dve")
            P.wait("scalar", "dve")
            P.iop("scalar", lambda e: e.activation(
                out=Ksq[:], in_=qk[0:64, :], func=AF.Square), "act")
            P.wait("tensor", "act")
            for g in range(4):
                gc = slice(g * 512, (g + 1) * 512)
                P.iop("tensor", lambda e, gc=gc: e.matmul(
                    ps_r[:], cs["ones_bf"][0:64, :], Ksq[:, gc],
                    start=True, stop=True), "pe")
                P.wait("vector", "pe")
                P.iop("vector", lambda e, gc=gc: e.tensor_copy(
                    out=q2row[0:1, gc], in_=ps_r[:]), "dve")
                P.wait("tensor", "dve")
            P.pfree(ps_r)

            ps_t = P.palloc("p7_ps_t", [128, 64], BF16)
            for tb in range(16):
                P.iop("tensor", lambda e, tb=tb: e.transpose(
                    ps_t[:], vg[0:64, tb * 128:(tb + 1) * 128],
                    cs["ident"][0:64, 0:64]), "pe")
                P.wait("vector", "pe")
                P.iop("vector", lambda e, tb=tb: e.tensor_copy(
                    out=vtm[:, tb, 0:64], in_=ps_t[:]), "dve")
                P.wait("tensor", "dve")
            P.pfree(ps_t)

            ps_pv = [P.palloc(f"p7_ps_pv{g}", [65, 512], F32) for g in range(4)]
            ps_s = [P.palloc("p7_ps_s0", [128, 512], F32),
                    P.palloc("p7_ps_s1", [128, 512], F32)]
            dve_relu = {}
            act_w = {}
            pe_mm = {}
            pe_pv = {}

            def scores(kcb):
                wcur = Wk_sb[kcb % 2]
                kcs = slice(kcb * 128, kcb * 128 + 128)
                if kcb >= 2:
                    P.wait("scalar", "pe", pe_pv.get(kcb - 2))
                for g in range(4):
                    gc = slice(g * 512, (g + 1) * 512)
                    pss = ps_s[g % 2]
                    prev = dve_relu.get((kcb, g - 2)) or dve_relu.get((kcb - 1, g + 2))
                    if prev:
                        P.wait("tensor", "dve", prev)
                    P.op("tensor", lambda e, kcs=kcs, gc=gc, pss=pss: e.matmul(
                        pss[:], n2K[:, kcs], qk[0:64, gc], start=True, stop=False))
                    P.op("tensor", lambda e, kcs=kcs, gc=gc, pss=pss: e.matmul(
                        pss[:], k2row[0:1, kcs], onesr[0:1, gc], start=False, stop=False))
                    P.iop("tensor", lambda e, kcs=kcs, gc=gc, pss=pss: e.matmul(
                        pss[:], cs["ones1x128"][:], q2row[0:1, gc],
                        start=False, stop=True), "pe")
                    pe_mm[(kcb, g)] = P.cnt["pe"]
                    P.wait("vector", "pe", pe_mm[(kcb, g)])
                    sb1 = s1b[g % 2]
                    sb2 = s2b[g % 2]
                    pa = act_w.get((kcb, g - 2)) or act_w.get((kcb - 1, g + 2))
                    if pa:
                        P.wait("vector", "act", pa)
                    P.iop("vector", lambda e, pss=pss, sb1=sb1: e.tensor_scalar(
                        out=sb1[:], in0=pss[:], scalar1=0.0, scalar2=None,
                        op0=ALU.max), "dve")
                    dve_relu[(kcb, g)] = P.cnt["dve"]
                    P.wait("scalar", "dve", dve_relu[(kcb, g)])
                    P.iop("scalar", lambda e, sb1=sb1, sb2=sb2: e.activation(
                        out=sb2[:], in_=sb1[:], func=AF.Sqrt), "act")
                    P.op("scalar", lambda e: e.drain())
                    P.iop("scalar", lambda e, sb2=sb2: e.activation(
                        out=sb2[:], in_=sb2[:], func=AF.Ln,
                        bias=cs["cvals"][:, 1:2]), "act")
                    P.op("scalar", lambda e: e.drain())
                    P.iop("scalar", lambda e, wcur=wcur, gc=gc, sb2=sb2: e.activation(
                        out=wcur[:, gc], in_=sb2[:], func=AF.Exp, scale=-2.0), "act")
                    act_w[(kcb, g)] = P.cnt["act"]

            def pv(kcb):
                wcur = Wk_sb[kcb % 2]
                P.wait("tensor", "act", act_w[(kcb, 3)])
                for g in range(4):
                    gc = slice(g * 512, (g + 1) * 512)
                    fn = lambda e, kcb=kcb, g=g, gc=gc, wcur=wcur: e.matmul(
                        ps_pv[g][:], vtm[:, kcb, :], wcur[:, gc],
                        start=(kcb == 0), stop=(kcb == 15))
                    if g == 3:
                        P.iop("tensor", fn, "pe")
                    else:
                        P.op("tensor", fn)
                pe_pv[kcb] = P.cnt["pe"]

            scores(0)
            for kcb in range(1, 16):
                scores(kcb)
                pv(kcb - 1)
            pv(15)
            P.pfree(ps_s[0], ps_s[1])
            ps_bc = P.palloc("p7_ps_bc", [64, 512], F32)
            P.wait("vector", "pe")
            for g in range(4):
                gc = slice(g * 512, (g + 1) * 512)
                P.iop("vector", lambda e, g=g, gc=gc: e.tensor_scalar(
                    out=lr[0:1, gc], in0=ps_pv[g][64:65, :],
                    scalar1=cs["cvals"][0:1, 1:2], scalar2=None, op0=ALU.add), "dve")
            P.wait("scalar", "dve")
            P.iop("scalar", lambda e: e.activation(
                out=lr[:], in_=lr[:], func=AF.Ln), "act")
            P.op("scalar", lambda e: e.drain())
            P.iop("scalar", lambda e: e.activation(
                out=recr[:], in_=lr[:], func=AF.Exp, scale=-1.0), "act")
            P.wait("tensor", "act")
            for g in range(4):
                gc = slice(g * 512, (g + 1) * 512)
                P.iop("tensor", lambda e, gc=gc: e.matmul(
                    ps_bc[:], cs["ones1x128"][0:1, 0:64], recr[0:1, gc],
                    start=True, stop=True), "pe")
                P.wait("scalar", "pe")
                P.iop("scalar", lambda e: e.activation(
                    out=bc[:], in_=ps_bc[:], func=AF.Copy), "act")
                P.wait("vector", "act")
                P.op("vector", lambda e, g=g: e.tensor_tensor(
                    out=tsc[64:128, :], in0=ps_pv[g][0:64, :], in1=bc[:], op=ALU.mult))
                P.op("vector", lambda e: e.drain())
                P.iop("vector", lambda e, g=g, gc=gc, h=h, hb=hb: e.tensor_tensor(
                    out=yfm[hb:hb + 64, h // 2, gc], in0=tsc[64:128, :],
                    in1=vg[64:128, gc], op=ALU.mult), "dve")
                P.wait("tensor", "dve")
            P.pfree(ps_pv[0], ps_pv[1], ps_pv[2], ps_pv[3], ps_bc)

        ps_op = P.palloc("p7_ps_op", [128, 512], F32)
        for f in range(8):
            P.wait("gpsimd", "pe")
            P.dma("gpsimd", wblk[:],
                  io["woT"].rearrange("(c p) f -> p c f", p=128)[:, :, f * 128:(f + 1) * 128],
                  "w")
            P.wait("tensor", "w")
            for g in range(4):
                gc = slice(g * 512, (g + 1) * 512)
                xcols = slice(sbase + g * 512, sbase + g * 512 + 512)
                P.wait("sync", "dve")
                P.dma("sync", res[:], X[f * 128:(f + 1) * 128, xcols], "xr")
                for k in range(8):
                    fn = lambda e, k=k, gc=gc: e.matmul(
                        ps_op[:], wblk[:, k, :], yfm[:, k, gc], start=(k == 0), stop=(k == 7))
                    if k == 7:
                        P.iop("tensor", fn, "pe")
                    else:
                        P.op("tensor", fn)
                P.wait("scalar", "pe")
                P.iop("scalar", lambda e, f=f: e.activation(
                    out=t1[:], in_=ps_op[:], func=AF.Identity,
                    bias=b_out[:, f:f + 1]), "act")
                P.wait("vector", "act")
                P.wait("vector", "xr")
                P.wait("vector", "st")
                P.op("vector", lambda e, f=f, gc=gc: e.tensor_tensor(
                    out=osb[:], in0=t1[:], in1=xln[:, f, gc], op=ALU.add))
                P.op("vector", lambda e: e.drain())
                P.iop("vector", lambda e: e.tensor_tensor(
                    out=osb[:], in0=osb[:], in1=res[:], op=ALU.add), "dve")
                P.wait("gpsimd", "dve")
                P.dma("gpsimd", Xo[f * 128:(f + 1) * 128, xcols], osb[:], "st")
                P.wait("tensor", "dve")
        P.pfree(ps_op)
    P.wait("sync", "st")
    P.wait("gpsimd", "st")


def phase5_dram(nc):
    io = {}
    io["wq"] = nc.dram_tensor("p5_wq", [D, D], F32, kind="ExternalInput").ap()
    io["wk"] = nc.dram_tensor("p5_wk", [D, 192], F32, kind="ExternalInput").ap()
    io["wv"] = nc.dram_tensor("p5_wv", [D, 192], F32, kind="ExternalInput").ap()
    io["wproj"] = nc.dram_tensor("p5_wproj", [D, D], F32, kind="ExternalInput").ap()
    io["mscat"] = nc.dram_tensor("p5_mscat", [192, D], F32, kind="ExternalInput").ap()
    io["msign"] = nc.dram_tensor("p5_msign", [192, D], F32, kind="ExternalInput").ap()
    io["vdflat"] = nc.dram_tensor("p5_vdflat", [192, D], F32, kind="ExternalInput").ap()
    io["cos4"] = nc.dram_tensor("p5_cos4", [128, T], F32, kind="ExternalInput").ap()
    io["sinP"] = nc.dram_tensor("p5_sinP", [128, T], F32, kind="ExternalInput").ap()
    io["sinN"] = nc.dram_tensor("p5_sinN", [128, T], F32, kind="ExternalInput").ap()
    io["causal"] = nc.dram_tensor("p5_causal", [128, 128], F32, kind="ExternalInput").ap()
    return io


def phase5_host(inputs):
    h = {}
    perm = np.zeros(D, np.int64)
    for hh in range(16):
        base = hh * 64
        perm[base:base + 32] = base + np.arange(32) * 2
        perm[base + 32:base + 64] = base + np.arange(32) * 2 + 1
    h["p5_wq"] = np.ascontiguousarray(np.asarray(inputs["L1_attn_a_Wq"], np.float32)[:, perm])
    h["p5_wk"] = np.ascontiguousarray(np.asarray(inputs["L1_attn_a_Wk"], np.float32))
    h["p5_wv"] = np.ascontiguousarray(np.asarray(inputs["L1_attn_a_Wv"], np.float32))
    h["p5_wproj"] = np.ascontiguousarray(np.asarray(inputs["L1_attn_a_Wproj"], np.float32))
    kd = np.asarray(inputs["L1_attn_a_key_decoder"], np.float32)
    mscat = np.zeros((192, D), np.float32)
    msign = np.zeros((192, D), np.float32)
    for hh in range(16):
        for f in range(32):
            for r in range(3):
                zr_row = f * 6 + 2 * r
                zi_row = f * 6 + 2 * r + 1
                col_e = hh * 64 + f
                col_o = hh * 64 + 32 + f
                mscat[zr_row, col_e] = kd[f, hh, r, 0]; msign[zr_row, col_e] = 1.0
                mscat[zi_row, col_e] = kd[f, hh, r, 1]; msign[zi_row, col_e] = -1.0
                mscat[zr_row, col_o] = kd[f, hh, r, 1]; msign[zr_row, col_o] = 1.0
                mscat[zi_row, col_o] = kd[f, hh, r, 0]; msign[zi_row, col_o] = 1.0
    h["p5_mscat"] = mscat
    h["p5_msign"] = msign
    vd = np.asarray(inputs["L1_attn_a_value_decoder"], np.float32)
    h["p5_vdflat"] = np.ascontiguousarray(vd.transpose(1, 0, 2).reshape(192, D))
    half = 32
    freq = 1.0 / 10000.0 ** (np.arange(half, dtype=np.float32) / half)
    freqs = np.outer(np.arange(T, dtype=np.float32), freq)
    cosT_ = np.cos(freqs).T.astype(np.float32)
    sinT_ = np.sin(freqs).T.astype(np.float32)
    h["p5_cos4"] = np.ascontiguousarray(np.tile(cosT_, (4, 1)))
    h["p5_sinP"] = np.ascontiguousarray(np.tile(sinT_, (4, 1)))
    h["p5_sinN"] = np.ascontiguousarray(np.tile(-sinT_, (4, 1)))
    ca = np.zeros((128, 128), np.float32)
    ca[np.triu_indices(128, 1)] = -1e30
    h["p5_causal"] = ca
    return h


def emit_phase5(P, cs, io):
    X, Xo = io["X"], io["Xo"]
    L = LNState(P)
    xln = P.sbuf("p5_xln", [128, 8, T], BF16)
    yfm = P.sbuf("p5_yfm", [128, 8, T], BF16)
    z_hi = P.sbuf("p5_zhi", [128, T], BF16)
    z_lo = P.sbuf("p5_zlo", [64, T], BF16)
    v_hi = P.sbuf("p5_vhi", [128, T], BF16)
    v_lo = P.sbuf("p5_vlo", [64, T], BF16)
    mhi = P.sbuf("p5_mhi", [128, D], BF16)
    mlo = P.sbuf("p5_mlo", [64, D], BF16)
    vd_hi = P.sbuf("p5_vdhi", [128, D], BF16)
    vd_lo = P.sbuf("p5_vdlo", [64, D], BF16)
    mtmp = P.sbuf("p5_mtmp", [128, D], F32)
    mtmp2 = P.sbuf("p5_mtmp2", [128, D], F32)
    rope_s = P.sbuf("p5_rope_s", [128, T], F32)
    qc = P.sbuf("p5_qc", [128, T], BF16)
    kc = P.sbuf("p5_kc", [128, T], BF16)
    vc = P.sbuf("p5_vc", [128, 16, 128], BF16)
    cos4 = P.sbuf("p5_cos_sb", [128, T], F32)
    sinP = P.sbuf("p5_sinP_sb", [128, T], F32)
    sinN = P.sbuf("p5_sinN_sb", [128, T], F32)
    rs1 = P.sbuf("p5_rs1", [128, T], F32)
    wblk = P.sbuf("p5_wblk", [128, 8, 128], BF16)
    causal = P.sbuf("p5_causal_sb", [128, 128], F32)
    Psb = P.sbuf("p5_P", [128, T], BF16)
    PT = P.sbuf("p5_PT", [128, 128], BF16)
    mg = P.sbuf("p5_mg", [128, 4], F32)
    lparts = P.sbuf("p5_lp", [128, 4], F32)
    mrow = P.sbuf("p5_m", [128, 1], F32)
    negm8 = P.sbuf("p5_negm8", [128, 1], F32)
    lsum = P.sbuf("p5_l", [128, 1], F32)
    linv = P.sbuf("p5_linv", [128, 1], F32)
    ytm = P.sbuf("p5_ytm", [128, 64], BF16)
    t1 = P.sbuf("p5_t1", [128, 512], F32)
    res = P.sbuf("p5_res", [128, 512], F32)
    osb = P.sbuf("p5_osb", [128, 512], F32)

    P.dma("gpsimd", cos4[:], io["cos4"], "w")
    P.dma("gpsimd", sinP[:], io["sinP"], "w")
    P.dma("gpsimd", sinN[:], io["sinN"], "w")
    P.dma("gpsimd", causal[:], io["causal"], "w")
    P.dma("gpsimd", mtmp[:], io["mscat"][0:128, :], "w")
    P.dma("gpsimd", mtmp2[:], io["msign"][0:128, :], "w")
    P.dma("gpsimd", vd_hi[:], io["vdflat"][0:128, :], "w")
    P.dma("gpsimd", vd_lo[:], io["vdflat"][128:192, :], "w")
    P.wait("vector", "w")
    P.iop("vector", lambda e: e.tensor_tensor(
        out=mhi[:], in0=mtmp[:], in1=mtmp2[:], op=ALU.mult), "dve")
    P.wait("gpsimd", "dve")
    P.dma("gpsimd", mtmp[0:64, :], io["mscat"][128:192, :], "w")
    P.dma("gpsimd", mtmp2[0:64, :], io["msign"][128:192, :], "w")
    P.wait("vector", "w")
    P.iop("vector", lambda e: e.tensor_tensor(
        out=mlo[:], in0=mtmp[0:64, :], in1=mtmp2[0:64, :], op=ALU.mult), "dve")

    def rope(buf, scaled):
        P.op("vector", lambda e, buf=buf: e.tensor_tensor(
            out=rs1[:], in0=buf[:], in1=cos4[:], op=ALU.mult))
        for blk in range(4):
            srcb = blk ^ 1
            tbl = sinN if blk % 2 == 0 else sinP
            P.op("vector", lambda e, buf=buf, blk=blk, srcb=srcb, tbl=tbl: e.tensor_tensor(
                out=rope_s[blk * 32:blk * 32 + 32, :],
                in0=buf[srcb * 32:srcb * 32 + 32, :],
                in1=tbl[srcb * 32:srcb * 32 + 32, :], op=ALU.mult))
        P.op("vector", lambda e: e.drain())
        P.iop("vector", lambda e, buf=buf: e.tensor_tensor(
            out=buf[:], in0=rs1[:], in1=rope_s[:], op=ALU.add), "dve")

    for s in range(BC):
        sbase = s * T
        emit_ln(P, L, cs, X, xln, (sbase, T), xln_base=sbase)
        P.wait("tensor", "dve")
        ps_p = [P.palloc("p5_ps_p0", [128, 512], F32), P.palloc("p5_ps_p1", [128, 512], F32)]
        for (wname, dsth, dstl) in (("wk", z_hi, z_lo), ("wv", v_hi, v_lo)):
            for fg, (f0, fw) in enumerate(((0, 128), (128, 64))):
                P.wait("gpsimd", "pe")
                P.dma("gpsimd", wblk[:, :, 0:fw],
                      io[wname].rearrange("(c p) f -> p c f", p=128)[:, :, f0:f0 + fw], "w")
                P.wait("tensor", "w")
                for g in range(4):
                    gc = slice(g * 512, (g + 1) * 512)
                    pin = ps_p[g % 2]
                    if not (fg == 0 and g == 0):
                        P.wait("tensor", "dve")
                    for k in range(8):
                        fn = lambda e, k=k, fw=fw, gc=gc, pin=pin: e.matmul(
                            pin[0:fw, :], wblk[:, k, 0:fw], xln[:, k, gc],
                            start=(k == 0), stop=(k == 7))
                        if k == 7:
                            P.iop("tensor", fn, "pe")
                        else:
                            P.op("tensor", fn)
                    P.wait("vector", "pe")
                    dst = dsth if fw == 128 else dstl
                    P.iop("vector", lambda e, dst=dst, fw=fw, gc=gc, pin=pin: e.tensor_copy(
                        out=dst[0:fw, gc], in_=pin[0:fw, :]), "dve")
        P.wait("tensor", "dve")
        P.pfree(ps_p[0], ps_p[1])

        ps_q = P.palloc("p5_ps_q", [128, 512], F32)
        ps_s = [P.palloc(f"p5_ps_s{i}", [128, 512], F32) for i in range(4)]
        ps_pt = P.palloc("p5_ps_pt", [128, 128], BF16)
        ps_o = P.palloc("p5_ps_o", [128, 64], F32)
        ps_yt = P.palloc("p5_ps_yt", [64, 128], BF16)
        for c in range(8):
            P.wait("gpsimd", "pe")
            P.dma("gpsimd", wblk[:],
                  io["wq"].rearrange("(c p) f -> p c f", p=128)[:, :, c * 128:(c + 1) * 128],
                  "w")
            P.wait("tensor", "w")
            for g in range(4):
                gc = slice(g * 512, (g + 1) * 512)
                if not (c == 0 and g == 0):
                    P.wait("tensor", "dve")
                for k in range(8):
                    fn = lambda e, k=k, gc=gc: e.matmul(
                        ps_q[:], wblk[:, k, :], xln[:, k, gc], start=(k == 0), stop=(k == 7))
                    if k == 7:
                        P.iop("tensor", fn, "pe")
                    else:
                        P.op("tensor", fn)
                P.wait("vector", "pe")
                P.iop("vector", lambda e, gc=gc: e.tensor_copy(
                    out=qc[:, gc], in_=ps_q[:]), "dve")
            for g in range(4):
                gc = slice(g * 512, (g + 1) * 512)
                P.wait("tensor", "dve")
                P.op("tensor", lambda e, c=c, gc=gc: e.matmul(
                    ps_q[:], mhi[:, c * 128:(c + 1) * 128], z_hi[:, gc],
                    start=True, stop=False))
                P.iop("tensor", lambda e, c=c, gc=gc: e.matmul(
                    ps_q[:], mlo[:, c * 128:(c + 1) * 128], z_lo[:, gc],
                    start=False, stop=True), "pe")
                P.wait("vector", "pe")
                P.iop("vector", lambda e, gc=gc: e.tensor_copy(
                    out=kc[:, gc], in_=ps_q[:]), "dve")
            for tb in range(16):
                tbs = slice(tb * 128, tb * 128 + 128)
                for half in range(2):
                    f0 = c * 128 + half * 64
                    P.wait("tensor", "dve")
                    P.op("tensor", lambda e, tbs=tbs, f0=f0: e.matmul(
                        ps_o[:], v_hi[:, tbs], vd_hi[:, f0:f0 + 64],
                        start=True, stop=False))
                    P.iop("tensor", lambda e, tbs=tbs, f0=f0: e.matmul(
                        ps_o[:], v_lo[:, tbs], vd_lo[:, f0:f0 + 64],
                        start=False, stop=True), "pe")
                    P.wait("vector", "pe")
                    P.iop("vector", lambda e, tb=tb, half=half: e.tensor_copy(
                        out=vc[:, tb, half * 64:half * 64 + 64], in_=ps_o[:]), "dve")
            P.wait("tensor", "dve")
            rope(qc, True)
            rope(kc, False)
            P.iop("vector", lambda e: e.drain(), "dve")
            P.wait("tensor", "dve")
            for hh in range(2):
                b = hh * 64
                for qi in range(16):
                    ngroups = qi // 4 + 1
                    qs = slice(qi * 128, qi * 128 + 128)
                    P.wait("tensor", "act")
                    for g in range(ngroups):
                        fn = lambda e, g=g, b=b, qs=qs: e.matmul(
                            ps_s[g][:], qc[b:b + 64, qs],
                            kc[b:b + 64, g * 512:(g + 1) * 512], start=True, stop=True)
                        if g == ngroups - 1:
                            P.iop("tensor", fn, "pe")
                        else:
                            P.op("tensor", fn)
                    P.wait("vector", "pe")
                    dcol = (qi % 4) * 128
                    P.op("vector", lambda e, ngroups=ngroups, dcol=dcol: e.tensor_tensor(
                        out=ps_s[ngroups - 1][:, dcol:dcol + 128],
                        in0=ps_s[ngroups - 1][:, dcol:dcol + 128],
                        in1=causal[:], op=ALU.add))
                    if dcol + 128 < 512:
                        P.op("vector", lambda e, ngroups=ngroups, dcol=dcol: e.memset(
                            ps_s[ngroups - 1][:, dcol + 128:512], -1e30))
                    P.op("vector", lambda e: e.drain())
                    for g in range(ngroups):
                        P.op("vector", lambda e, g=g: e.reduce_max(
                            mg[:, g:g + 1], ps_s[g][:], axis=AX))
                    P.op("vector", lambda e: e.drain())
                    P.op("vector", lambda e, ngroups=ngroups: e.reduce_max(
                        mrow[:], mg[:, 0:ngroups], axis=AX))
                    P.op("vector", lambda e: e.drain())
                    P.op("vector", lambda e: e.tensor_scalar(
                        out=negm8[:], in0=mrow[:], scalar1=-0.125, scalar2=None,
                        op0=ALU.mult))
                    P.iop("vector", lambda e: e.drain(), "dve")
                    P.wait("scalar", "dve")
                    for g in range(ngroups):
                        P.iop("scalar", lambda e, g=g: e.activation(
                            out=Psb[:, g * 512:(g + 1) * 512], in_=ps_s[g][:],
                            func=AF.Exp, bias=negm8[:], scale=0.125,
                            accum_out=lparts[:, g:g + 1]), "act")
                    P.wait("vector", "act")
                    P.op("vector", lambda e, ngroups=ngroups: e.reduce_sum(
                        lsum[:], lparts[:, 0:ngroups], axis=AX))
                    P.op("vector", lambda e: e.drain())
                    P.op("vector", lambda e: e.reciprocal(out=linv[:], in_=lsum[:]))
                    P.iop("vector", lambda e: e.drain(), "dve")
                    P.wait("tensor", "act")
                    for kb in range(qi + 1):
                        P.iop("tensor", lambda e, kb=kb: e.transpose(
                            ps_pt[:], Psb[:, kb * 128:kb * 128 + 128], cs["ident"][:]), "pe")
                        P.wait("scalar", "pe")
                        P.iop("scalar", lambda e: e.activation(
                            out=PT[:], in_=ps_pt[:], func=AF.Copy), "act")
                        P.wait("tensor", "act")
                        fn = lambda e, kb=kb, b=b, qi=qi: e.matmul(
                            ps_o[:], PT[:], vc[:, kb, b:b + 64],
                            start=(kb == 0), stop=(kb == qi))
                        if kb == qi:
                            P.iop("tensor", fn, "pe")
                        else:
                            P.op("tensor", fn)
                    P.wait("vector", "pe")
                    P.wait("vector", "dve")
                    P.iop("vector", lambda e: e.tensor_tensor(
                        out=ytm[:], in0=ps_o[:], in1=linv[:].broadcast_to([128, 64]),
                        op=ALU.mult), "dve")
                    P.wait("tensor", "dve")
                    P.iop("tensor", lambda e: e.transpose(
                        ps_yt[:], ytm[:], cs["ident"][:]), "pe")
                    P.wait("vector", "pe")
                    P.iop("vector", lambda e, c=c, hh=hh, qs=qs: e.tensor_copy(
                        out=yfm[hh * 64:hh * 64 + 64, c, qs],
                        in_=ps_yt[:]), "dve")
                    P.wait("tensor", "dve")
        P.pfree(ps_q, ps_s[0], ps_s[1], ps_s[2], ps_s[3], ps_pt, ps_o, ps_yt)

        ps_op = P.palloc("p5_ps_op", [128, 512], F32)
        for f in range(8):
            P.wait("gpsimd", "pe")
            P.dma("gpsimd", wblk[:],
                  io["wproj"].rearrange("(c p) f -> p c f", p=128)[:, :, f * 128:(f + 1) * 128],
                  "w")
            P.wait("tensor", "w")
            for g in range(4):
                gc = slice(g * 512, (g + 1) * 512)
                xcols = slice(sbase + g * 512, sbase + g * 512 + 512)
                P.wait("sync", "dve")
                P.dma("sync", res[:], X[f * 128:(f + 1) * 128, xcols], "xr")
                for k in range(8):
                    fn = lambda e, k=k, gc=gc: e.matmul(
                        ps_op[:], wblk[:, k, :], yfm[:, k, gc], start=(k == 0), stop=(k == 7))
                    if k == 7:
                        P.iop("tensor", fn, "pe")
                    else:
                        P.op("tensor", fn)
                P.wait("vector", "pe")
                P.wait("vector", "xr")
                P.wait("vector", "st")
                P.iop("vector", lambda e: e.tensor_tensor(
                    out=osb[:], in0=ps_op[:], in1=res[:], op=ALU.add), "dve")
                P.wait("gpsimd", "dve")
                P.dma("gpsimd", Xo[f * 128:(f + 1) * 128, xcols], osb[:], "st")
                P.wait("tensor", "dve")
        P.pfree(ps_op)
    P.wait("sync", "st")
    P.wait("gpsimd", "st")


def emit_ln_sbuf(P, L, cs, src, out, gcol=None, bcol=None, ncols=512):
    ps_sum = P.palloc("lns_sum", [1, ncols], F32)
    ps_sumsq = P.palloc("lns_sumsq", [1, ncols], F32)
    ps_bc0 = P.palloc("lns_bc0", [128, ncols], F32)
    ps_bc1 = P.palloc("lns_bc1", [128, ncols], F32)
    P.wait("scalar", "pe")
    P.wait("scalar", "dve")
    for c in range(8):
        P.iop("scalar", lambda e, c=c: e.activation(
            out=L.xsq[:, c, :ncols], in_=src[:, c, :], func=AF.Square), "act")
    P.wait("tensor", "act")
    P.wait("tensor", "dve")
    for c in range(8):
        P.op("tensor", lambda e, c=c: e.matmul(
            ps_sum[:], cs["ones_bf"][:], src[:, c, :], start=(c == 0), stop=(c == 7)))
    for c in range(8):
        fn = lambda e, c=c: e.matmul(
            ps_sumsq[:], cs["ones_bf"][:], L.xsq[:, c, :ncols],
            start=(c == 0), stop=(c == 7))
        if c == 7:
            P.iop("tensor", fn, "pe")
        else:
            P.op("tensor", fn)
    P.wait("vector", "pe")
    P.op("vector", lambda e: e.tensor_scalar(
        out=L.mu[0:1, :ncols], in0=ps_sum[:], scalar1=1.0 / 1024, scalar2=None, op0=ALU.mult))
    P.op("vector", lambda e: e.tensor_scalar(
        out=L.e2[0:1, :ncols], in0=ps_sumsq[:], scalar1=1.0 / 1024, scalar2=None, op0=ALU.mult))
    P.op("vector", lambda e: e.tensor_tensor(
        out=L.musq[0:1, :ncols], in0=L.mu[0:1, :ncols], in1=L.mu[0:1, :ncols], op=ALU.mult))
    P.op("vector", lambda e: e.drain())
    P.iop("vector", lambda e: e.tensor_tensor(
        out=L.var[0:1, :ncols], in0=L.e2[0:1, :ncols], in1=L.musq[0:1, :ncols],
        op=ALU.subtract), "dve")
    P.wait("scalar", "dve")
    P.iop("scalar", lambda e: e.activation(
        out=L.srt[0:1, :ncols], in_=L.var[0:1, :ncols], func=AF.Sqrt,
        bias=cs["cvals"][0:1, 0:1]), "act")
    P.wait("vector", "act")
    P.op("vector", lambda e: e.drain())
    P.iop("vector", lambda e: e.reciprocal(out=L.rstd[0:1, :ncols], in_=L.srt[0:1, :ncols]), "dve")
    P.op("vector", lambda e: e.drain())
    P.op("vector", lambda e: e.scalar_tensor_tensor(
        out=L.nb[0:1, :ncols], in0=L.mu[0:1, :ncols], scalar=-1.0, in1=L.rstd[0:1, :ncols],
        op0=ALU.mult, op1=ALU.mult))
    P.op("vector", lambda e: e.drain())
    P.op("vector", lambda e: e.tensor_copy(out=L.rp[0:1, 0:ncols], in_=L.rstd[0:1, :ncols]))
    P.iop("vector", lambda e: e.tensor_copy(out=L.rp[0:1, 512:512 + ncols], in_=L.nb[0:1, :ncols]), "dve")
    P.wait("tensor", "dve")
    P.op("tensor", lambda e: e.matmul(
        ps_bc0[:], cs["ones1x128"][:], L.rp[0:1, 0:ncols], start=True, stop=True))
    P.iop("tensor", lambda e: e.matmul(
        ps_bc1[:], cs["ones1x128"][:], L.rp[0:1, 512:512 + ncols],
        start=True, stop=True), "pe")
    P.wait("vector", "pe")
    for c in range(8):
        P.op("vector", lambda e, c=c: e.tensor_tensor(
            out=L.tmp[:, :ncols], in0=src[:, c, :], in1=ps_bc0[:], op=ALU.mult))
        if gcol is None:
            P.iop("vector", lambda e, c=c: e.tensor_tensor(
                out=out[:, c, :], in0=L.tmp[:, :ncols], in1=ps_bc1[:], op=ALU.add), "dve")
        else:
            P.op("vector", lambda e, c=c: e.tensor_tensor(
                out=L.tmp[:, :ncols], in0=L.tmp[:, :ncols], in1=ps_bc1[:], op=ALU.add))
            P.iop("vector", lambda e, c=c: e.tensor_scalar(
                out=out[:, c, :], in0=L.tmp[:, :ncols], scalar1=gcol[:, c:c + 1],
                scalar2=bcol[:, c:c + 1], op0=ALU.mult, op1=ALU.add), "dve")
    P.wait("tensor", "dve")
    P.wait("scalar", "dve")
    P.pfree(ps_sum, ps_sumsq, ps_bc0, ps_bc1)


def phase3_dram(nc):
    io = {}
    io["winT"] = nc.dram_tensor("p3_winT", [D, 4096], F32, kind="ExternalInput").ap()
    io["wcT"] = nc.dram_tensor("p3_wcT", [2048, 2048], F32, kind="ExternalInput").ap()
    io["wdT"] = nc.dram_tensor("p3_wdT", [2048, 2048], F32, kind="ExternalInput").ap()
    io["woT"] = nc.dram_tensor("p3_woT", [2048, D], F32, kind="ExternalInput").ap()
    io["b_in"] = nc.dram_tensor("p3_b_in", [128, 32], F32, kind="ExternalInput").ap()
    io["b_c"] = nc.dram_tensor("p3_b_c", [128, 16], F32, kind="ExternalInput").ap()
    io["b_d"] = nc.dram_tensor("p3_b_d", [128, 16], F32, kind="ExternalInput").ap()
    io["b_out"] = nc.dram_tensor("p3_b_out", [128, 8], F32, kind="ExternalInput").ap()
    io["convw"] = nc.dram_tensor("p3_convw", [128, 16, 4], F32, kind="ExternalInput").ap()
    io["convb"] = nc.dram_tensor("p3_convb", [128, 16], F32, kind="ExternalInput").ap()
    io["logdelta"] = nc.dram_tensor("p3_logdelta", [128, 16], F32, kind="ExternalInput").ap()
    io["Amat"] = nc.dram_tensor("p3_Amat", [128, 16, 16], F32, kind="ExternalInput").ap()
    io["Bmat"] = nc.dram_tensor("p3_Bmat", [128, 16, 16], F32, kind="ExternalInput").ap()
    io["ln_g"] = nc.dram_tensor("p3_ln_g", [128, 8], F32, kind="ExternalInput").ap()
    io["ln_b"] = nc.dram_tensor("p3_ln_b", [128, 8], F32, kind="ExternalInput").ap()
    return io


def phase3_host(inputs):
    def cols(v, nch):
        return np.ascontiguousarray(np.asarray(v, np.float32).reshape(nch, 128).T)
    h = {}
    h["p3_winT"] = np.ascontiguousarray(inputs["L0_attn_b_W_in"].T)
    h["p3_wcT"] = np.ascontiguousarray(inputs["L0_attn_b_W_c"].T)
    h["p3_wdT"] = np.ascontiguousarray(inputs["L0_attn_b_W_d"].T)
    h["p3_woT"] = np.ascontiguousarray(inputs["L0_attn_b_W_out"].T)
    h["p3_b_in"] = cols(inputs["L0_attn_b_b_in"], 32)
    h["p3_b_c"] = cols(inputs["L0_attn_b_b_c"], 16)
    h["p3_b_d"] = cols(inputs["L0_attn_b_b_d"], 16)
    h["p3_b_out"] = cols(inputs["L0_attn_b_b_out"], 8)
    h["p3_convw"] = np.ascontiguousarray(
        np.asarray(inputs["L0_attn_b_conv_w"], np.float32)[:, 0, :].reshape(16, 128, 4)
        .transpose(1, 0, 2))
    h["p3_convb"] = cols(inputs["L0_attn_b_conv_b"], 16)
    h["p3_logdelta"] = cols(inputs["L0_attn_b_log_delta"], 16)
    h["p3_Amat"] = np.ascontiguousarray(
        np.asarray(inputs["L0_attn_b_A"], np.float32).reshape(16, 128, 16).transpose(1, 0, 2))
    h["p3_Bmat"] = np.ascontiguousarray(
        np.asarray(inputs["L0_attn_b_Bm"], np.float32).reshape(16, 128, 16).transpose(1, 0, 2))
    h["p3_ln_g"] = cols(inputs["L0_attn_b_ln_g"], 8)
    h["p3_ln_b"] = cols(inputs["L0_attn_b_ln_b"], 8)
    return h


def emit_phase3(P, cs, io):
    X, Xo = io["X"], io["Xo"]
    L = LNState(P)
    xln = P.sbuf("p3_xln", [128, 8, 512], BF16)
    xla = P.sbuf("p3_xla", [128, 8, 512], BF16)
    xp = P.sbuf("p3_xp", [128, 16, 515], BF16)
    halo = P.sbuf("p3_halo", [128, 16, 3], BF16)
    gsig = P.sbuf("p3_gsig", [128, 16, 512], BF16)
    xc = P.sbuf("p3_xc", [128, 16, 512], BF16)
    u = P.sbuf("p3_u", [128, 16, 512], BF16)
    g_all = P.sbuf("p3_g", [128, 16, 512], F32)
    xg = P.sbuf("p3_xg", [128, 16, 512], BF16)
    carry = P.sbuf("p3_carry", [128, 16], F32)
    adec = P.sbuf("p3_adec", [128, 16], F32)
    Cvec = P.sbuf("p3_C", [128, 16], F32)
    wblk = P.sbuf("p3_wblk", [128, 8, 128], BF16)
    wcb = P.sbuf("p3_wcb", [128, 16, 128], BF16)
    wdb = P.sbuf("p3_wdb", [128, 16, 128], BF16)
    wob = P.sbuf("p3_wob", [128, 16, 128], BF16)
    biasb = {}
    for nm, shape in (("b_in", [128, 32]), ("b_c", [128, 16]), ("b_d", [128, 16]),
                      ("b_out", [128, 8]), ("convw", [128, 16, 4]), ("convb", [128, 16]),
                      ("logdelta", [128, 16]), ("ln_g", [128, 8]), ("ln_b", [128, 8])):
        biasb[nm] = P.sbuf("p3s_" + nm, shape, F32)
        P.dma("gpsimd", biasb[nm][:], io[nm], "w")
    Abuf = P.sbuf("p3s_Amat", [128, 16, 16], F32)
    Bbuf = P.sbuf("p3s_Bmat", [128, 16, 16], F32)
    P.dma("gpsimd", Abuf[:], io["Amat"], "w")
    P.dma("gpsimd", Bbuf[:], io["Bmat"], "w")
    ca = P.sbuf("p3_ca", [128, 512], F32)
    cb = P.sbuf("p3_cb", [128, 512], F32)
    t1 = P.sbuf("p3_t1", [128, 512], F32)
    sg = P.sbuf("p3_sg", [128, 512], F32)
    AB = P.sbuf("p3_AB", [128, 16, 16], F32)
    res = P.sbuf("p3_res", [128, 512], F32)
    osb = P.sbuf("p3_osb", [128, 512], F32)

    P.wait("scalar", "w")
    P.wait("vector", "w")
    ld_ = biasb["logdelta"]
    P.iop("scalar", lambda e: e.activation(out=adec[:], in_=ld_[:], func=AF.Abs), "act")
    P.iop("scalar", lambda e: e.activation(out=adec[:], in_=adec[:], func=AF.Exp,
                                           scale=-1.0), "act")
    P.iop("scalar", lambda e: e.activation(out=adec[:], in_=adec[:], func=AF.Ln,
                                           bias=cs["cvals"][:, 3:4]), "act")
    P.wait("vector", "act")
    P.op("vector", lambda e: e.tensor_scalar(
        out=Cvec[:], in0=ld_[:], scalar1=0.0, scalar2=None, op0=ALU.max))
    P.op("vector", lambda e: e.drain())
    P.iop("vector", lambda e: e.tensor_tensor(
        out=adec[:], in0=adec[:], in1=Cvec[:], op=ALU.add), "dve")
    P.wait("scalar", "dve")
    P.iop("scalar", lambda e: e.activation(out=adec[:], in_=adec[:], func=AF.Exp,
                                           scale=-1.0), "act")
    P.wait("vector", "act")
    P.op("vector", lambda e: e.tensor_tensor(
        out=AB[:], in0=Abuf[:], in1=Bbuf[:], op=ALU.mult))
    P.op("vector", lambda e: e.drain())
    P.op("vector", lambda e: e.reduce_sum(Cvec[:], AB[:], axis=AX))
    P.iop("vector", lambda e: e.drain(), "dve")

    ps_in = [P.palloc("p3_ps_in0", [128, 512], F32), P.palloc("p3_ps_in1", [128, 512], F32)]
    ps_c = P.palloc("p3_ps_c", [128, 512], F32)
    ps_dd = P.palloc("p3_ps_dd", [128, 512], F32)

    for tile in range(8):
        s = tile // 4
        j = tile % 4
        x0 = s * T + j * 512
        tcol = slice(x0, x0 + 512)
        emit_ln(P, L, cs, X, xln, (x0, 512), xln_base=x0)
        P.wait("tensor", "dve")
        emit_ln_sbuf(P, L, cs, xln, xla, biasb["ln_g"], biasb["ln_b"])
        for f in range(32):
            P.wait("gpsimd", "pe")
            P.wait("gpsimd", "act")
            P.dma("gpsimd", wblk[:],
                  io["winT"].rearrange("(c p) f -> p c f", p=128)[:, :, f * 128:(f + 1) * 128],
                  "w")
            P.wait("tensor", "w")
            pin = ps_in[f % 2]
            if f > 1:
                P.wait("tensor", "act")
            for k in range(8):
                fn = lambda e, k=k, pin=pin: e.matmul(
                    pin[:], wblk[:, k, :], xla[:, k, :], start=(k == 0), stop=(k == 7))
                if k == 7:
                    P.iop("tensor", fn, "pe")
                else:
                    P.op("tensor", fn)
            P.wait("scalar", "pe")
            if f < 16:
                P.iop("scalar", lambda e, f=f, pin=pin: e.activation(
                    out=xp[:, f, 3:515], in_=pin[:], func=AF.Identity,
                    bias=io and biasb["b_in"][:, f:f + 1]), "act")
            else:
                P.iop("scalar", lambda e, f=f, pin=pin: e.activation(
                    out=gsig[:, f - 16, :], in_=pin[:], func=AF.Sigmoid,
                    bias=biasb["b_in"][:, f:f + 1]), "act")
        P.wait("vector", "act")
        if j == 0:
            P.iop("vector", lambda e: e.memset(xp[:, :, 0:3], 0.0), "dve")
        else:
            P.iop("vector", lambda e: e.tensor_copy(out=xp[:, :, 0:3], in_=halo[:]), "dve")
        P.op("vector", lambda e: e.drain())
        P.iop("vector", lambda e: e.tensor_copy(out=halo[:], in_=xp[:, :, 512:515]), "dve")
        P.op("vector", lambda e: e.drain())
        cw = biasb["convw"]
        for fc in range(16):
            P.op("vector", lambda e, fc=fc: e.tensor_scalar(
                out=ca[:], in0=xp[:, fc, 0:512], scalar1=cw[:, fc, 0:1], scalar2=None,
                op0=ALU.mult))
            P.op("vector", lambda e, fc=fc: e.tensor_scalar(
                out=cb[:], in0=xp[:, fc, 1:513], scalar1=cw[:, fc, 1:2], scalar2=None,
                op0=ALU.mult))
            P.op("vector", lambda e: e.drain())
            P.op("vector", lambda e, fc=fc: e.scalar_tensor_tensor(
                out=ca[:], in0=xp[:, fc, 2:514], scalar=cw[:, fc, 2:3], in1=ca[:],
                op0=ALU.mult, op1=ALU.add))
            P.op("vector", lambda e, fc=fc: e.scalar_tensor_tensor(
                out=cb[:], in0=xp[:, fc, 3:515], scalar=cw[:, fc, 3:4], in1=cb[:],
                op0=ALU.mult, op1=ALU.add))
            P.op("vector", lambda e: e.drain())
            P.iop("vector", lambda e: e.tensor_tensor(
                out=ca[:], in0=ca[:], in1=cb[:], op=ALU.add), "dve")
            P.wait("scalar", "dve")
            P.iop("scalar", lambda e, fc=fc: e.activation(
                out=xc[:, fc, :], in_=ca[:], func=AF.Silu,
                bias=biasb["convb"][:, fc:fc + 1]), "act")
            P.wait("vector", "act")
        for f in range(16):
            P.wait("gpsimd", "pe")
            P.dma("gpsimd", wcb[:],
                  io["wcT"].rearrange("(c p) f -> p c f", p=128)[:, :, f * 128:(f + 1) * 128],
                  "w")
            P.dma("gpsimd", wdb[:],
                  io["wdT"].rearrange("(c p) f -> p c f", p=128)[:, :, f * 128:(f + 1) * 128],
                  "w")
            P.wait("tensor", "w")
            P.wait("tensor", "dve")
            P.wait("tensor", "act")
            for k in range(16):
                P.op("tensor", lambda e, k=k: e.matmul(
                    ps_c[:], wcb[:, k, :], xc[:, k, :], start=(k == 0), stop=(k == 15)))
            for k in range(16):
                fn = lambda e, k=k: e.matmul(
                    ps_dd[:], wdb[:, k, :], xc[:, k, :], start=(k == 0), stop=(k == 15))
                if k == 15:
                    P.iop("tensor", fn, "pe")
                else:
                    P.op("tensor", fn)
            P.wait("scalar", "pe")
            P.iop("scalar", lambda e, f=f: e.activation(
                out=sg[:], in_=ps_dd[:], func=AF.Sigmoid,
                bias=biasb["b_d"][:, f:f + 1]), "act")
            P.wait("vector", "pe")
            P.op("vector", lambda e, f=f: e.tensor_scalar(
                out=t1[:], in0=ps_c[:], scalar1=biasb["b_c"][:, f:f + 1], scalar2=None,
                op0=ALU.add))
            P.wait("vector", "act")
            P.op("vector", lambda e: e.drain())
            P.iop("vector", lambda e, f=f: e.tensor_tensor(
                out=u[:, f, :], in0=t1[:], in1=sg[:], op=ALU.mult), "dve")
        if j == 0:
            P.iop("vector", lambda e: e.memset(carry[:], 0.0), "dve")
            P.op("vector", lambda e: e.drain())
        for fc in range(16):
            P.op("vector", lambda e, fc=fc: e.tensor_tensor_scan(
                out=g_all[:, fc, :], data0=adec[:, fc:fc + 1].broadcast_to([128, 512]),
                data1=u[:, fc, :], initial=carry[:, fc:fc + 1],
                op0=ALU.mult, op1=ALU.add))
        P.op("vector", lambda e: e.drain())
        P.iop("vector", lambda e: e.tensor_copy(out=carry[:], in_=g_all[:, :, 511]), "dve")
        P.op("vector", lambda e: e.drain())
        for fc in range(16):
            P.iop("vector", lambda e, fc=fc: e.scalar_tensor_tensor(
                out=xg[:, fc, :], in0=g_all[:, fc, :], scalar=Cvec[:, fc:fc + 1],
                in1=gsig[:, fc, :], op0=ALU.mult, op1=ALU.mult), "dve")
        for fo in range(8):
            P.wait("gpsimd", "pe")
            P.dma("gpsimd", wob[:],
                  io["woT"].rearrange("(c p) f -> p c f", p=128)[:, :, fo * 128:(fo + 1) * 128],
                  "w")
            P.wait("sync", "dve")
            P.dma("sync", res[:], X[fo * 128:(fo + 1) * 128, tcol], "xr")
            P.wait("tensor", "w")
            P.wait("tensor", "dve")
            for k in range(16):
                fn = lambda e, k=k: e.matmul(
                    ps_c[:], wob[:, k, :], xg[:, k, :], start=(k == 0), stop=(k == 15))
                if k == 15:
                    P.iop("tensor", fn, "pe")
                else:
                    P.op("tensor", fn)
            P.wait("scalar", "pe")
            P.iop("scalar", lambda e, fo=fo: e.activation(
                out=t1[:], in_=ps_c[:], func=AF.Identity,
                bias=biasb["b_out"][:, fo:fo + 1]), "act")
            P.wait("vector", "act")
            P.wait("vector", "xr")
            P.wait("vector", "st")
            P.op("vector", lambda e, fo=fo: e.tensor_tensor(
                out=osb[:], in0=t1[:], in1=xln[:, fo, :], op=ALU.add))
            P.op("vector", lambda e: e.drain())
            P.iop("vector", lambda e: e.tensor_tensor(
                out=osb[:], in0=osb[:], in1=res[:], op=ALU.add), "dve")
            P.wait("gpsimd", "dve")
            P.dma("gpsimd", Xo[fo * 128:(fo + 1) * 128, tcol], osb[:], "st")
    P.pfree(ps_in[0], ps_in[1], ps_c, ps_dd)
    P.wait("sync", "st")
    P.wait("gpsimd", "st")


def phase2_dram(nc):
    io = {}
    io["wgate"] = nc.dram_tensor("p2_wgate", [D, 8], F32, kind="ExternalInput").ap()
    io["cfc"] = nc.dram_tensor("p2_cfc", [8, D, 3072], F32, kind="ExternalInput").ap()
    io["cproj"] = nc.dram_tensor("p2_cproj", [8, 1536, D], F32, kind="ExternalInput").ap()
    io["prevmask"] = nc.dram_tensor("p2_prevmask", [8, 1], F32, kind="ExternalInput").ap()
    io["cc_in"] = nc.dram_tensor("p2_cc_in", [8, 1], F32).ap()
    io["cc_out"] = nc.dram_tensor("p2_cc_out", [64, 1], F32).ap()
    return io


def phase2_host(inputs):
    h = {}
    h["p2_wgate"] = np.ascontiguousarray(np.asarray(inputs["L0_ffn_w_gate"], np.float32))
    h["p2_cfc"] = np.ascontiguousarray(np.asarray(inputs["L0_ffn_c_fc"], np.float32))
    h["p2_cproj"] = np.ascontiguousarray(np.asarray(inputs["L0_ffn_c_proj"], np.float32))
    return h


def emit_phase2(P, cs, io):
    X, Xo = io["X"], io["Xo"]
    CAP = 4096.0
    L = LNState(P)
    xln = P.sbuf("p2_xln", [128, 8, NT], BF16)
    h_e = P.sbuf("p2_h", [128, 12, 512], BF16)
    acc = P.sbuf("p2_acc", [128, 8, 512], F32)
    wfa = P.sbuf("p2_wfa", [128, 8, 512], BF16)
    wfg = P.sbuf("p2_wfg", [128, 8, 512], BF16)
    wpr = P.sbuf("p2_wpr", [128, 12, 128], BF16)
    gsb = P.sbuf("p2_gate", [128, 8, 8], BF16)
    logit = P.sbuf("p2_logit", [128, 32, 8], F32)
    shifted = P.sbuf("p2_shift", [128, 32, 8], F32)
    EX = P.sbuf("p2_EX", [128, 32, 8], F32)
    mskd = P.sbuf("p2_mskd", [128, 32, 8], F32)
    m1 = P.sbuf("p2_m1", [128, 32], F32)
    m2 = P.sbuf("p2_m2", [128, 32], F32)
    Z = P.sbuf("p2_Z", [128, 32], F32)
    w1 = P.sbuf("p2_w1", [128, 32], F32)
    w2 = P.sbuf("p2_w2", [128, 32], F32)
    dd2 = P.sbuf("p2_dd2", [128, 32], F32)
    e2w = P.sbuf("p2_e2w", [128, 32], F32)
    sel1 = P.sbuf("p2_sel1", [128, 8], F32)
    sel2 = P.sbuf("p2_sel2", [128, 8], F32)
    c1 = P.sbuf("p2_c1", [128, 8], F32)
    sboth = P.sbuf("p2_sboth", [128, 32, 8], BF16)
    coefb = P.sbuf("p2_coefb", [128, 32, 8], BF16)
    cum_bf = P.sbuf("p2_cumbf", [128, 8], BF16)
    tot = P.sbuf("p2_tot", [8, 32], F32)
    off = P.sbuf("p2_off", [8, 33], F32)
    gath = P.sbuf("p2_gath", [8, 8], F32)
    pmsk = P.sbuf("p2_pmsk", [8, 1], F32)
    thrb = P.sbuf("p2_thrb", [8, 1], F32)
    thr = P.sbuf("p2_thr", [8, 32], F32)
    vT = P.sbuf("p2_vT", [8, 128], F32)
    coef_fm = P.sbuf("p2_coef_fm", [8, NT], BF16)
    bc = P.sbuf("p2_bc", [128, 512], BF16)
    t1 = P.sbuf("p2_t1", [128, 512], F32)
    t2 = P.sbuf("p2_t2", [128, 512], F32)
    res = P.sbuf("p2_res", [128, 512], F32)
    osb = P.sbuf("p2_osb", [128, 512], F32)

    emit_ln(P, L, cs, X, xln, (0, NT))
    P.wait("tensor", "dve")

    P.dma("gpsimd", gsb[:], io["wgate"].rearrange("(c p) e -> p c e", p=128), "w")
    P.dma("gpsimd", pmsk[:], io["prevmask"], "w")
    P.wait("tensor", "w")
    ps_l = P.palloc("p2_ps_l", [128, 8], F32)
    for tc in range(32):
        tcs = slice(tc * 128, tc * 128 + 128)
        for k in range(8):
            fn = lambda e, k=k, tcs=tcs: e.matmul(
                ps_l[:], xln[:, k, tcs], gsb[:, k, :], start=(k == 0), stop=(k == 7))
            if k == 7:
                P.iop("tensor", fn, "pe")
            else:
                P.op("tensor", fn)
        P.wait("vector", "pe")
        P.iop("vector", lambda e, tc=tc: e.tensor_copy(
            out=logit[:, tc, :], in_=ps_l[:]), "dve")
        P.wait("tensor", "dve")
    P.pfree(ps_l)

    P.iop("vector", lambda e: e.reduce_max(m1[:], logit[:], axis=AX), "dve")
    P.op("vector", lambda e: e.drain())
    for tc in range(32):
        P.op("vector", lambda e, tc=tc: e.tensor_scalar(
            out=sel1[:], in0=logit[:, tc, :], scalar1=m1[:, tc:tc + 1], scalar2=None,
            op0=ALU.is_ge))
        P.op("vector", lambda e: e.drain())
        P.op("vector", lambda e, tc=tc: e.scalar_tensor_tensor(
            out=mskd[:, tc, :], in0=sel1[:], scalar=-1e9, in1=logit[:, tc, :],
            op0=ALU.mult, op1=ALU.add))
        P.op("vector", lambda e, tc=tc: e.tensor_scalar(
            out=shifted[:, tc, :], in0=logit[:, tc, :], scalar1=m1[:, tc:tc + 1],
            scalar2=None, op0=ALU.subtract))
        P.op("vector", lambda e: e.drain())
        P.iop("vector", lambda e, tc=tc: e.tensor_copy(
            out=logit[:, tc, :], in_=sel1[:]), "dve")
    P.iop("vector", lambda e: e.reduce_max(m2[:], mskd[:], axis=AX), "dve")
    P.op("vector", lambda e: e.drain())
    P.op("vector", lambda e: e.tensor_tensor(
        out=dd2[:], in0=m2[:], in1=m1[:], op=ALU.subtract))
    P.iop("vector", lambda e: e.drain(), "dve")
    P.wait("scalar", "dve")
    P.iop("scalar", lambda e: e.activation(out=EX[:], in_=shifted[:], func=AF.Exp), "act")
    P.iop("scalar", lambda e: e.activation(out=e2w[:], in_=dd2[:], func=AF.Exp), "act")
    P.wait("vector", "act")
    P.iop("vector", lambda e: e.reduce_sum(Z[:], EX[:], axis=AX), "dve")
    P.op("vector", lambda e: e.drain())
    P.op("vector", lambda e: e.reciprocal(out=w1[:], in_=Z[:]))
    P.op("vector", lambda e: e.drain())
    P.op("vector", lambda e: e.tensor_tensor(out=w2[:], in0=e2w[:], in1=w1[:],
                                             op=ALU.mult))
    P.iop("vector", lambda e: e.drain(), "dve")

    for tc in range(32):
        P.op("vector", lambda e, tc=tc: e.tensor_scalar(
            out=sel2[:], in0=mskd[:, tc, :], scalar1=m2[:, tc:tc + 1], scalar2=None,
            op0=ALU.is_ge))
        P.op("vector", lambda e, tc=tc: e.tensor_scalar(
            out=c1[:], in0=logit[:, tc, :], scalar1=w1[:, tc:tc + 1], scalar2=None,
            op0=ALU.mult))
        P.op("vector", lambda e: e.drain())
        P.op("vector", lambda e, tc=tc: e.tensor_tensor(
            out=sboth[:, tc, :], in0=logit[:, tc, :], in1=sel2[:], op=ALU.add))
        P.iop("vector", lambda e, tc=tc: e.scalar_tensor_tensor(
            out=coefb[:, tc, :], in0=sel2[:], scalar=w2[:, tc:tc + 1], in1=c1[:],
            op0=ALU.mult, op1=ALU.add), "dve")
    ps_t = P.palloc("p2_ps_t", [8, 1], F32)
    P.wait("tensor", "dve")
    for tc in range(32):
        P.iop("tensor", lambda e, tc=tc: e.matmul(
            ps_t[:], sboth[:, tc, :], cs["ones_bf"][:], start=True, stop=True), "pe")
        P.wait("vector", "pe")
        P.iop("vector", lambda e, tc=tc: e.tensor_copy(
            out=tot[:, tc:tc + 1], in_=ps_t[:]), "dve")
        P.wait("tensor", "dve")
    P.pfree(ps_t)
    P.op("vector", lambda e: e.memset(off[:, 0:1], 0.0))
    P.op("vector", lambda e: e.drain())
    for n in range(32):
        P.op("vector", lambda e, n=n: e.tensor_tensor(
            out=off[:, n + 1:n + 2], in0=off[:, n:n + 1], in1=tot[:, n:n + 1],
            op=ALU.add))
        P.iop("vector", lambda e: e.drain(), "dve")
    P.wait("gpsimd", "dve")
    P.dma("gpsimd", io["cc_in"], off[:, 32:33], "cc")
    P.wait("gpsimd", "cc")
    P.iop("gpsimd", lambda e: e.collective_compute(
        "AllGather", ALU.bypass, replica_groups=[list(range(NCORES))],
        ins=[io["cc_in"].opt()], outs=[io["cc_out"].opt()]), "ccd")
    P.wait("gpsimd", "ccd")
    P.dma("gpsimd", gath[:], io["cc_out"].rearrange("(c e) one -> c (e one)", e=8), "cc")
    P.wait("tensor", "cc")
    ps_og = P.palloc("p2_ps_og", [8, 1], F32)
    P.iop("tensor", lambda e: e.matmul(
        ps_og[:], gath[:], pmsk[:], start=True, stop=True), "pe")
    P.wait("vector", "pe")
    P.op("vector", lambda e: e.tensor_scalar(
        out=thrb[:], in0=ps_og[:], scalar1=-1.0, scalar2=CAP, op0=ALU.mult, op1=ALU.add))
    P.op("vector", lambda e: e.drain())
    for n in range(32):
        P.op("vector", lambda e, n=n: e.tensor_tensor(
            out=thr[:, n:n + 1], in0=thrb[:], in1=off[:, n:n + 1], op=ALU.subtract))
    P.iop("vector", lambda e: e.drain(), "dve")
    P.pfree(ps_og)

    ps_cum = P.palloc("p2_ps_cum", [128, 8], F32)
    ps_cT = P.palloc("p2_ps_cT", [8, 128], BF16)
    ps_fT = P.palloc("p2_ps_fT", [8, 128], BF16)
    P.wait("tensor", "dve")
    for tc in range(32):
        P.iop("tensor", lambda e, tc=tc: e.matmul(
            ps_cum[:], cs["strictU"][:], sboth[:, tc, :], start=True, stop=True), "pe")
        P.wait("vector", "pe")
        P.iop("vector", lambda e: e.tensor_copy(out=cum_bf[:], in_=ps_cum[:]), "dve")
        P.wait("tensor", "dve")
        P.op("tensor", lambda e: e.transpose(ps_cT[:], cum_bf[:], cs["ident"][:]))
        P.iop("tensor", lambda e, tc=tc: e.transpose(
            ps_fT[:], coefb[:, tc, :], cs["ident"][:]), "pe")
        P.wait("vector", "pe")
        P.op("vector", lambda e, tc=tc: e.tensor_scalar(
            out=vT[:], in0=ps_cT[:], scalar1=thr[:, tc:tc + 1], scalar2=None,
            op0=ALU.is_lt))
        P.op("vector", lambda e: e.drain())
        P.iop("vector", lambda e, tc=tc: e.tensor_tensor(
            out=coef_fm[:, tc * 128:tc * 128 + 128], in0=ps_fT[:], in1=vT[:],
            op=ALU.mult), "dve")
        P.wait("tensor", "dve")
    P.pfree(ps_cum, ps_cT, ps_fT)

    ps_a = [P.palloc("p2_ps_a0", [128, 512], F32), P.palloc("p2_ps_a1", [128, 512], F32)]
    ps_g = [P.palloc("p2_ps_g0", [128, 512], F32), P.palloc("p2_ps_g1", [128, 512], F32)]
    ps_d = [P.palloc("p2_ps_d0", [128, 512], F32), P.palloc("p2_ps_d1", [128, 512], F32)]
    ps_bc = P.palloc("p2_ps_bc", [128, 512], F32)
    for ti in range(8):
        tcols = slice(ti * 512, ti * 512 + 512)
        for e_i in range(8):
            P.wait("tensor", "act")
            P.iop("tensor", lambda e, e_i=e_i, tcols=tcols: e.matmul(
                ps_bc[:], cs["esel"][:, e_i * 128:(e_i + 1) * 128],
                coef_fm[:, tcols], start=True, stop=True), "pe")
            P.wait("scalar", "pe")
            P.wait("scalar", "dve")
            P.iop("scalar", lambda e: e.activation(
                out=bc[:], in_=ps_bc[:], func=AF.Copy), "act")
            for fb in range(3):
                P.wait("gpsimd", "pe")
                P.dma("gpsimd", wfa[:],
                      io["cfc"][e_i].rearrange("(c p) f -> p c f", p=128)[
                          :, :, fb * 512:(fb + 1) * 512], "w")
                P.dma("gpsimd", wfg[:],
                      io["cfc"][e_i].rearrange("(c p) f -> p c f", p=128)[
                          :, :, 1536 + fb * 512:1536 + (fb + 1) * 512], "w")
                P.wait("tensor", "w")
                for fc in range(4):
                    pa = ps_a[fc % 2]
                    pg = ps_g[fc % 2]
                    j = fb * 4 + fc
                    if fb + fc > 0:
                        P.wait("tensor", "dve")
                    for k in range(8):
                        P.op("tensor", lambda e, k=k, fc=fc, tcols=tcols, pa=pa: e.matmul(
                            pa[:], wfa[:, k, fc * 128:fc * 128 + 128], xln[:, k, tcols],
                            start=(k == 0), stop=(k == 7)))
                    for k in range(8):
                        fn = lambda e, k=k, fc=fc, tcols=tcols, pg=pg: e.matmul(
                            pg[:], wfg[:, k, fc * 128:fc * 128 + 128], xln[:, k, tcols],
                            start=(k == 0), stop=(k == 7))
                        if k == 7:
                            P.iop("tensor", fn, "pe")
                        else:
                            P.op("tensor", fn)
                    P.wait("scalar", "pe")
                    P.wait("scalar", "dve")
                    P.iop("scalar", lambda e, pa=pa: e.activation(
                        out=t1[:], in_=pa[:], func=AF.Silu), "act")
                    P.wait("vector", "act")
                    P.op("vector", lambda e, pg=pg: e.tensor_tensor(
                        out=t2[:], in0=pg[:], in1=bc[:], op=ALU.mult))
                    P.op("vector", lambda e: e.drain())
                    P.iop("vector", lambda e, j=j: e.tensor_tensor(
                        out=h_e[:, j, :], in0=t1[:], in1=t2[:], op=ALU.mult), "dve")
            for fo in range(8):
                pd = ps_d[fo % 2]
                P.wait("gpsimd", "pe")
                P.dma("gpsimd", wpr[:],
                      io["cproj"][e_i].rearrange("(c p) f -> p c f", p=128)[
                          :, :, fo * 128:(fo + 1) * 128], "w")
                P.wait("tensor", "w")
                if fo > 1:
                    P.wait("tensor", "dve")
                for k in range(12):
                    fn = lambda e, k=k, pd=pd: e.matmul(
                        pd[:], wpr[:, k, :], h_e[:, k, :], start=(k == 0), stop=(k == 11))
                    if k == 11:
                        P.iop("tensor", fn, "pe")
                    else:
                        P.op("tensor", fn)
                P.wait("vector", "pe")
                if e_i == 0:
                    P.iop("vector", lambda e, fo=fo, pd=pd: e.tensor_copy(
                        out=acc[:, fo, :], in_=pd[:]), "dve")
                else:
                    P.iop("vector", lambda e, fo=fo, pd=pd: e.tensor_tensor(
                        out=acc[:, fo, :], in0=acc[:, fo, :], in1=pd[:], op=ALU.add), "dve")
        for fo in range(8):
            P.wait("sync", "dve")
            P.dma("sync", res[:], X[fo * 128:(fo + 1) * 128, tcols], "xr")
            P.wait("vector", "xr")
            P.wait("vector", "st")
            P.iop("vector", lambda e, fo=fo: e.tensor_tensor(
                out=osb[:], in0=acc[:, fo, :], in1=res[:], op=ALU.add), "dve")
            P.wait("gpsimd", "dve")
            P.dma("gpsimd", Xo[fo * 128:(fo + 1) * 128, tcols], osb[:], "st")
    P.pfree(ps_a[0], ps_a[1], ps_g[0], ps_g[1], ps_d[0], ps_d[1], ps_bc)
    P.wait("sync", "st")
    P.wait("gpsimd", "st")



N_TILES = 24
SLOTS = N_TILES * 512
I32 = mybir.dt.int32


def phase12_dram(nc):
    io = phase2_dram(nc)
    io["cfcB"] = nc.dram_tensor("p12_cfcB", [8 * D, 3072], BF16).ap()
    io["cprojB"] = nc.dram_tensor("p12_cprojB", [8 * 1536, D], BF16).ap()
    io["gidx"] = nc.dram_tensor("p12_gidx", [SLOTS, 1], I32).ap()
    io["wslot"] = nc.dram_tensor("p12_wslot", [SLOTS, 1], F32).ap()
    io["accm"] = nc.dram_tensor("p12_accm", [NT, D], F32).ap()
    io["xlntm"] = nc.dram_tensor("p12_xlntm", [NT, D], BF16).ap()
    io["cc_in2"] = nc.dram_tensor("p12_cc_in", [1, 8], F32).ap()
    io["cc_out2"] = nc.dram_tensor("p12_cc_out", [8, 8], F32).ap()
    io["_conv_done"] = [False]
    return io


def phase16_dram(nc):
    io = phase6_dram(nc)
    io["wgB"] = nc.dram_tensor("p16_wgB", [8 * D, 4096], BF16).ap()
    io["wuB"] = nc.dram_tensor("p16_wuB", [8 * D, 4096], BF16).ap()
    io["wdB"] = nc.dram_tensor("p16_wdB", [8 * 4096, D], BF16).ap()
    io["gidx"] = nc.dram_tensor("p16_gidx", [SLOTS, 1], I32).ap()
    io["wslot"] = nc.dram_tensor("p16_wslot", [SLOTS, 1], F32).ap()
    io["accm"] = nc.dram_tensor("p16_accm", [NT, D], F32).ap()
    io["xlntm"] = nc.dram_tensor("p16_xlntm", [NT, D], BF16).ap()
    io["_conv_done"] = [False]
    return io


def _emit_cast_cache(P, dst_flat, src_ap, rows, width):
    buf = P.sbuf("cvt", [128, width], BF16)
    sap = src_ap.rearrange("e r w -> (e r) w") if len(src_ap.shape) == 3 else src_ap
    for r0 in range(0, rows, 128):
        P.wait("gpsimd", "cvo")
        P.dma("gpsimd", buf[:], sap[r0:r0 + 128, :], "cvi")
        P.wait("gpsimd", "cvi")
        P.dma("gpsimd", dst_flat[r0:r0 + 128, :], buf[:], "cvo")


def emit_moe_sparse(P, cs, io, variant):
    X, Xo = io["X"], io["Xo"]
    l0 = (variant == 0)
    IOTA = cs["iota"]

    if not io["_conv_done"][0]:
        io["_conv_done"][0] = True
        P.push_scope()
        if l0:
            _emit_cast_cache(P, io["cfcB"], io["cfc"], 8 * D, 3072)
            _emit_cast_cache(P, io["cprojB"], io["cproj"], 8 * 1536, D)
        else:
            _emit_cast_cache(P, io["wgB"], io["wgT"], 8 * D, 4096)
            _emit_cast_cache(P, io["wuB"], io["wuT"], 8 * D, 4096)
            _emit_cast_cache(P, io["wdB"], io["wdT"], 8 * 4096, D)
        P.pop_scope()

    P.push_scope()
    L = LNState(P)
    xln = P.sbuf("sm_xln", [128, 8, NT], BF16)
    logit = P.sbuf("sm_logit", [128, 32, 8], F32)
    mskd = P.sbuf("sm_mskd", [128, 32, 8], F32)
    shifted = P.sbuf("sm_shift", [128, 32, 8], F32)
    EX = P.sbuf("sm_EX", [128, 32, 8], F32)
    m1 = P.sbuf("sm_m1", [128, 32], F32)
    m2 = P.sbuf("sm_m2", [128, 32], F32)
    Z = P.sbuf("sm_Z", [128, 32], F32)
    w1 = P.sbuf("sm_w1", [128, 32], F32)
    w2 = P.sbuf("sm_w2", [128, 32], F32)
    dd2 = P.sbuf("sm_dd2", [128, 32], F32)
    e2w = P.sbuf("sm_e2w", [128, 32], F32)
    sel1 = P.sbuf("sm_sel1", [128, 8], F32)
    sel2 = P.sbuf("sm_sel2", [128, 8], F32)
    sboth = P.sbuf("sm_sboth", [128, 32, 8], BF16)
    svalid = P.sbuf("sm_svalid", [128, 32, 8], BF16)
    vtmb = P.sbuf("sm_vtm", [128, 8], F32)
    tots = P.sbuf("sm_tots", [1, 32 * 8], F32)
    offr = P.sbuf("sm_offr", [1, 33 * 8], F32)
    obf = P.sbuf("sm_obf", [1, 32 * 8], F32)
    alr = P.sbuf("sm_alr", [1, 8], F32)
    bser = P.sbuf("sm_bser", [1, 8], F32)
    endr = P.sbuf("sm_endr", [1, 8], F32)
    ter = P.sbuf("sm_ter", [1, N_TILES], F32)
    tmp8 = P.sbuf("sm_tmp8", [1, 8], F32)
    gath = P.sbuf("sm_gath", [8, 8], F32)
    pmsk = P.sbuf("sm_pmsk", [8, 1], F32)
    offgr = P.sbuf("sm_offgr", [1, 8], F32)
    tokf = P.sbuf("sm_tokf", [128, 1], F32)
    toki = P.sbuf("sm_toki", [128, 1], I32)
    posf = P.sbuf("sm_posf", [128, 1], F32)
    posi = P.sbuf("sm_posi", [128, 1], I32)
    coefc = P.sbuf("sm_coefc", [128, 1], F32)
    vk = P.sbuf("sm_vk", [128, 1], F32)
    t8 = P.sbuf("sm_t8", [128, 8], F32)
    cumv = P.sbuf("sm_cumv", [128, 8], F32)
    zrow = P.sbuf("sm_zrow", [128, 2], F32)
    tpT = P.sbuf("sm_tpT", [128, 128], BF16)
    tecol = P.sbuf("sm_tecol", [128, N_TILES], F32)
    gsb = P.sbuf("sm_gate", [128, 8, 8], BF16)

    emit_ln(P, L, cs, X, xln, (0, NT))
    P.wait("tensor", "dve")

    ps_tp = P.palloc("sm_ps_tp", [128, 128], BF16)
    for tc in range(32):
        for fc in range(8):
            P.iop("tensor", lambda e, tc=tc, fc=fc: e.transpose(
                ps_tp[:], xln[:, fc, tc * 128:(tc + 1) * 128], cs["ident"][:]), "pe")
            P.wait("scalar", "pe")
            P.iop("scalar", lambda e: e.activation(
                out=tpT[:], in_=ps_tp[:], func=AF.Copy), "act")
            P.wait("gpsimd", "act")
            P.dma("gpsimd", io["xlntm"][tc * 128:(tc + 1) * 128, fc * 128:(fc + 1) * 128],
                  tpT[:], "xtm")
            P.wait("tensor", "xtm")
    P.pfree(ps_tp)

    gname = "wgate" if l0 else "gateT"
    P.dma("gpsimd", gsb[:], io[gname].rearrange("(c p) e -> p c e", p=128), "w")
    if l0:
        P.dma("gpsimd", pmsk[:], io["prevmask"], "w")
    P.wait("tensor", "w")
    ps_l = P.palloc("sm_ps_l", [128, 8], F32)
    for tc in range(32):
        tcs = slice(tc * 128, tc * 128 + 128)
        for k in range(8):
            fn = lambda e, k=k, tcs=tcs: e.matmul(
                ps_l[:], xln[:, k, tcs], gsb[:, k, :], start=(k == 0), stop=(k == 7))
            if k == 7:
                P.iop("tensor", fn, "pe")
            else:
                P.op("tensor", fn)
        P.wait("vector", "pe")
        P.iop("vector", lambda e, tc=tc: e.tensor_copy(
            out=logit[:, tc, :], in_=ps_l[:]), "dve")
        P.wait("tensor", "dve")
    P.pfree(ps_l)

    P.iop("vector", lambda e: e.reduce_max(m1[:], logit[:], axis=AX), "dve")
    P.op("vector", lambda e: e.drain())
    for tc in range(32):
        P.op("vector", lambda e, tc=tc: e.tensor_scalar(
            out=sel1[:], in0=logit[:, tc, :], scalar1=m1[:, tc:tc + 1], scalar2=None,
            op0=ALU.is_ge))
        P.op("vector", lambda e: e.drain())
        P.op("vector", lambda e, tc=tc: e.scalar_tensor_tensor(
            out=mskd[:, tc, :], in0=sel1[:], scalar=-1e9, in1=logit[:, tc, :],
            op0=ALU.mult, op1=ALU.add))
        if l0:
            P.op("vector", lambda e, tc=tc: e.tensor_scalar(
                out=shifted[:, tc, :], in0=logit[:, tc, :], scalar1=m1[:, tc:tc + 1],
                scalar2=None, op0=ALU.subtract))
        P.op("vector", lambda e: e.drain())
        P.iop("vector", lambda e, tc=tc: e.tensor_copy(
            out=logit[:, tc, :], in_=sel1[:]), "dve")
    P.iop("vector", lambda e: e.reduce_max(m2[:], mskd[:], axis=AX), "dve")
    P.op("vector", lambda e: e.drain())
    P.op("vector", lambda e: e.tensor_tensor(
        out=dd2[:], in0=m2[:], in1=m1[:], op=ALU.subtract))
    P.iop("vector", lambda e: e.drain(), "dve")
    P.wait("scalar", "dve")
    if l0:
        P.iop("scalar", lambda e: e.activation(out=EX[:], in_=shifted[:], func=AF.Exp), "act")
        P.iop("scalar", lambda e: e.activation(out=e2w[:], in_=dd2[:], func=AF.Exp), "act")
        P.wait("vector", "act")
        P.iop("vector", lambda e: e.reduce_sum(Z[:], EX[:], axis=AX), "dve")
        P.op("vector", lambda e: e.drain())
        P.op("vector", lambda e: e.reciprocal(out=w1[:], in_=Z[:]))
        P.op("vector", lambda e: e.drain())
        P.op("vector", lambda e: e.tensor_tensor(out=w2[:], in0=e2w[:], in1=w1[:],
                                                 op=ALU.mult))
        P.iop("vector", lambda e: e.drain(), "dve")
    else:
        P.iop("vector", lambda e: e.tensor_scalar(
            out=e2w[:], in0=dd2[:], scalar1=-1.0, scalar2=None, op0=ALU.mult), "dve")
        P.wait("scalar", "dve")
        P.iop("scalar", lambda e: e.activation(out=w1[:], in_=e2w[:], func=AF.Sigmoid), "act")
        P.iop("scalar", lambda e: e.activation(out=w2[:], in_=dd2[:], func=AF.Sigmoid), "act")
        P.wait("vector", "act")

    for tc in range(32):
        P.op("vector", lambda e, tc=tc: e.tensor_scalar(
            out=sel2[:], in0=mskd[:, tc, :], scalar1=m2[:, tc:tc + 1], scalar2=None,
            op0=ALU.is_ge))
        P.op("vector", lambda e: e.drain())
        P.iop("vector", lambda e, tc=tc: e.tensor_tensor(
            out=sboth[:, tc, :], in0=logit[:, tc, :], in1=sel2[:], op=ALU.add), "dve")

    ps_c = P.palloc("sm_ps_c", [1, 8], F32)
    ps_cum = P.palloc("sm_ps_cum", [128, 8], F32)
    if l0:
        P.wait("tensor", "dve")
        for tc in range(32):
            P.iop("tensor", lambda e, tc=tc: e.matmul(
                ps_c[:], cs["ones_bf"][:], sboth[:, tc, :], start=True, stop=True), "pe")
            P.wait("vector", "pe")
            P.iop("vector", lambda e, tc=tc: e.tensor_copy(
                out=tots[0:1, tc * 8:(tc + 1) * 8], in_=ps_c[:]), "dve")
            P.wait("tensor", "dve")
        P.op("vector", lambda e: e.memset(offr[0:1, 0:8], 0.0))
        P.op("vector", lambda e: e.drain())
        for n in range(32):
            P.op("vector", lambda e, n=n: e.tensor_tensor(
                out=offr[0:1, (n + 1) * 8:(n + 2) * 8], in0=offr[0:1, n * 8:(n + 1) * 8],
                in1=tots[0:1, n * 8:(n + 1) * 8], op=ALU.add))
            P.op("vector", lambda e: e.drain())
        P.iop("vector", lambda e: e.drain(), "dve")
        P.wait("gpsimd", "dve")
        P.dma("gpsimd", io["cc_in2"], offr[0:1, 32 * 8:33 * 8], "cc")
        P.wait("gpsimd", "cc")
        P.iop("gpsimd", lambda e: e.collective_compute(
            "AllGather", ALU.bypass, replica_groups=[list(range(NCORES))],
            ins=[io["cc_in2"].opt()], outs=[io["cc_out2"].opt()]), "ccd")
        P.wait("gpsimd", "ccd")
        P.dma("gpsimd", gath[:], io["cc_out2"], "cc")
        P.wait("tensor", "cc")
        ps_og = P.palloc("sm_ps_og", [1, 8], F32)
        P.iop("tensor", lambda e: e.matmul(
            ps_og[:], pmsk[:], gath[:], start=True, stop=True), "pe")
        P.wait("vector", "pe")
        P.iop("vector", lambda e: e.tensor_copy(out=offgr[:], in_=ps_og[:]), "dve")
        P.wait("tensor", "dve")
        P.pfree(ps_og)
        for tc in range(32):
            P.op("vector", lambda e, tc=tc: e.tensor_tensor(
                out=tmp8[:], in0=offr[0:1, tc * 8:(tc + 1) * 8], in1=offgr[:],
                op=ALU.add))
            P.iop("vector", lambda e: e.drain(), "dve")
            P.wait("tensor", "dve")
            P.op("tensor", lambda e, tc=tc: e.matmul(
                ps_cum[:], cs["strictU"][:], sboth[:, tc, :], start=True, stop=False))
            P.iop("tensor", lambda e: e.matmul(
                ps_cum[:], cs["ones1x128_f32"][:], tmp8[:], start=False, stop=True), "pe")
            P.wait("vector", "pe")
            P.iop("vector", lambda e: e.tensor_scalar(
                out=vtmb[:], in0=ps_cum[:], scalar1=4096.0, scalar2=None,
                op0=ALU.is_lt), "dve")
            P.op("vector", lambda e: e.drain())
            P.op("vector", lambda e, tc=tc: e.tensor_tensor(
                out=svalid[:, tc, :], in0=sboth[:, tc, :], in1=vtmb[:], op=ALU.mult))
            P.iop("vector", lambda e, tc=tc: e.tensor_copy(
                out=EX[:, tc, :], in_=vtmb[:]), "dve")
            P.wait("tensor", "dve")
    else:
        for tc in range(32):
            P.op("vector", lambda e, tc=tc: e.tensor_copy(
                out=svalid[:, tc, :], in_=sboth[:, tc, :]))
        P.iop("vector", lambda e: e.memset(EX[:], 1.0), "dve")
        P.wait("tensor", "dve")

    for tc in range(32):
        P.iop("tensor", lambda e, tc=tc: e.matmul(
            ps_c[:], cs["ones_bf"][:], svalid[:, tc, :], start=True, stop=True), "pe")
        P.wait("vector", "pe")
        P.iop("vector", lambda e, tc=tc: e.tensor_copy(
            out=tots[0:1, tc * 8:(tc + 1) * 8], in_=ps_c[:]), "dve")
        P.wait("tensor", "dve")
    P.op("vector", lambda e: e.memset(offr[0:1, 0:8], 0.0))
    P.op("vector", lambda e: e.drain())
    for n in range(32):
        P.op("vector", lambda e, n=n: e.tensor_tensor(
            out=offr[0:1, (n + 1) * 8:(n + 2) * 8], in0=offr[0:1, n * 8:(n + 1) * 8],
            in1=tots[0:1, n * 8:(n + 1) * 8], op=ALU.add))
        P.op("vector", lambda e: e.drain())
    P.op("vector", lambda e: e.memset(alr[:], 0.0))
    P.op("vector", lambda e: e.drain())
    for jj in range(8):
        P.op("vector", lambda e, jj=jj: e.tensor_scalar(
            out=tmp8[:], in0=offr[0:1, 32 * 8:33 * 8], scalar1=float(512 * jj),
            scalar2=None, op0=ALU.is_gt))
        P.op("vector", lambda e: e.drain())
        P.op("vector", lambda e: e.scalar_tensor_tensor(
            out=alr[:], in0=tmp8[:], scalar=512.0, in1=alr[:],
            op0=ALU.mult, op1=ALU.add))
        P.op("vector", lambda e: e.drain())
    P.op("vector", lambda e: e.memset(bser[0:1, 0:1], 0.0))
    P.op("vector", lambda e: e.drain())
    for e_i in range(7):
        P.op("vector", lambda e, e_i=e_i: e.tensor_tensor(
            out=bser[0:1, e_i + 1:e_i + 2], in0=bser[0:1, e_i:e_i + 1],
            in1=alr[0:1, e_i:e_i + 1], op=ALU.add))
        P.op("vector", lambda e: e.drain())
    P.op("vector", lambda e: e.tensor_tensor(
        out=endr[:], in0=bser[:], in1=alr[:], op=ALU.add))
    P.op("vector", lambda e: e.drain())
    for jj in range(N_TILES):
        P.op("vector", lambda e, jj=jj: e.tensor_scalar(
            out=tmp8[:], in0=endr[:], scalar1=float(512 * jj), scalar2=None,
            op0=ALU.is_le))
        P.op("vector", lambda e: e.drain())
        P.op("vector", lambda e, jj=jj: e.reduce_sum(
            ter[0:1, jj:jj + 1], tmp8[:], axis=AX))
        P.op("vector", lambda e: e.drain())
        P.op("vector", lambda e, jj=jj: e.tensor_scalar(
            out=ter[0:1, jj:jj + 1], in0=ter[0:1, jj:jj + 1], scalar1=7.0,
            scalar2=None, op0=ALU.min))
        P.op("vector", lambda e: e.drain())
    for tc in range(32):
        P.op("vector", lambda e, tc=tc: e.tensor_tensor(
            out=obf[0:1, tc * 8:(tc + 1) * 8], in0=offr[0:1, tc * 8:(tc + 1) * 8],
            in1=bser[:], op=ALU.add))
    P.iop("vector", lambda e: e.drain(), "dve")

    P.op("vector", lambda e: e.memset(zrow[:], 0.0))
    P.iop("vector", lambda e: e.drain(), "dve")
    P.wait("gpsimd", "dve")
    for b in range(N_TILES * 4):
        P.dma("gpsimd", io["gidx"][b * 128:(b + 1) * 128, :], zrow[:, 0:1], "gz")
        P.dma("gpsimd", io["wslot"][b * 128:(b + 1) * 128, :], zrow[:, 1:2], "gz")
    P.wait("gpsimd", "gz")
    for tc in range(32):
        P.op("vector", lambda e, tc=tc: e.tensor_scalar(
            out=tokf[:], in0=IOTA[:, 0:1], scalar1=float(tc * 128), scalar2=None,
            op0=ALU.add))
        P.op("vector", lambda e: e.drain())
        P.iop("vector", lambda e: e.tensor_copy(out=toki[:], in_=tokf[:]), "dve")
        P.wait("tensor", "dve")
        P.op("tensor", lambda e, tc=tc: e.matmul(
            ps_cum[:], cs["strictU"][:], svalid[:, tc, :], start=True, stop=False))
        P.iop("tensor", lambda e, tc=tc: e.matmul(
            ps_cum[:], cs["ones1x128_f32"][:], obf[0:1, tc * 8:(tc + 1) * 8],
            start=False, stop=True), "pe")
        P.wait("vector", "pe")
        P.iop("vector", lambda e: e.tensor_copy(out=cumv[:], in_=ps_cum[:]), "dve")
        P.op("vector", lambda e: e.drain())
        P.wait("tensor", "dve")
        for kk in range(2):
            wk = w1 if kk == 0 else w2
            sap = (lambda tc=tc: logit[:, tc, :]) if kk == 0 else (lambda: sel2[:])
            if kk == 1:
                P.op("vector", lambda e, tc=tc: e.tensor_scalar(
                    out=sel2[:], in0=mskd[:, tc, :], scalar1=m2[:, tc:tc + 1],
                    scalar2=None, op0=ALU.is_ge))
                P.op("vector", lambda e: e.drain())
            P.op("vector", lambda e, tc=tc, sap=sap: e.tensor_tensor(
                out=t8[:], in0=sap(), in1=EX[:, tc, :], op=ALU.mult))
            P.op("vector", lambda e: e.drain())
            P.op("vector", lambda e: e.reduce_sum(vk[:], t8[:], axis=AX))
            P.op("vector", lambda e, sap=sap: e.tensor_tensor(
                out=t8[:], in0=sap(), in1=cumv[:], op=ALU.mult))
            P.op("vector", lambda e: e.drain())
            P.op("vector", lambda e: e.reduce_sum(posf[:], t8[:], axis=AX))
            P.op("vector", lambda e: e.drain())
            P.op("vector", lambda e, tc=tc, wk=wk: e.tensor_scalar(
                out=coefc[:], in0=vk[:], scalar1=wk[:, tc:tc + 1], scalar2=None,
                op0=ALU.mult))
            P.op("vector", lambda e: e.tensor_scalar(
                out=vk[:], in0=vk[:], scalar1=-1.0, scalar2=1.0,
                op0=ALU.mult, op1=ALU.add))
            P.op("vector", lambda e: e.drain())
            P.op("vector", lambda e: e.scalar_tensor_tensor(
                out=posf[:], in0=vk[:], scalar=1e6, in1=posf[:],
                op0=ALU.mult, op1=ALU.add))
            P.op("vector", lambda e: e.drain())
            P.iop("vector", lambda e: e.tensor_copy(out=posi[:], in_=posf[:]), "dve")
            P.op("vector", lambda e: e.drain())
            P.wait("gpsimd", "dve")
            P.iop("gpsimd", lambda e: e.indirect_dma_start(
                out=io["gidx"], out_offset=IndirectOffsetOnAxis(posi[:], 0),
                in_=toki[:], in_offset=None), "gsc", 16)
            P.iop("gpsimd", lambda e: e.indirect_dma_start(
                out=io["wslot"], out_offset=IndirectOffsetOnAxis(posi[:], 0),
                in_=coefc[:], in_offset=None), "gsc", 16)
            P.wait("vector", "gsc")
    if DEBUG:
        dbg_dump(P, "smx_xln", xln[:], hard=True)
        dbg_dump(P, "smx_alr", alr[:], hard=True)
        dbg_dump(P, "smx_bser", bser[:], hard=True)
        dbg_dump(P, "smx_endr", endr[:], hard=True)
        dbg_dump(P, "smx_ter", ter[:], hard=True)
        dbg_dump(P, "smx_offr", offr[:], hard=True)
        dbg_dump(P, "smx_w1", w1[:], hard=True)
        dbg_dump(P, "smx_w2", w2[:], hard=True)
        gdump = P.sbuf("smx_gd", [128, SLOTS // 128], F32)
        wdump = P.sbuf("smx_wd", [128, SLOTS // 128], F32)
        gdi = P.sbuf("smx_gdi", [128, 1], I32)
        P.wait("gpsimd", "gsc")
        for b in range(SLOTS // 128):
            P.dma("gpsimd", gdi[:], io["gidx"][b * 128:(b + 1) * 128, :], "gdd")
            P.wait("vector", "gdd")
            P.iop("vector", lambda e, b=b: e.tensor_copy(
                out=gdump[:, b:b + 1], in_=gdi[:]), "dve")
            P.wait("gpsimd", "dve")
            P.dma("gpsimd", wdump[:, b:b + 1].rearrange("p one -> p one") if False else wdump[:, b:b + 1],
                  io["wslot"][b * 128:(b + 1) * 128, :], "gdd")
        P.wait("gpsimd", "gdd")
        dbg_dump(P, "smx_gidx", gdump[:], hard=True)
        dbg_dump(P, "smx_wslot", wdump[:], hard=True)
    ps_te = P.palloc("sm_ps_te", [128, N_TILES], F32)
    P.wait("tensor", "dve")
    P.iop("tensor", lambda e: e.matmul(
        ps_te[:], cs["ones1x128_f32"][:], ter[:], start=True, stop=True), "pe")
    P.wait("vector", "pe")
    P.iop("vector", lambda e: e.tensor_copy(out=tecol[:], in_=ps_te[:]), "dve")
    P.wait("tensor", "dve")
    P.pfree(ps_c, ps_cum, ps_te)
    P.pop_scope()

    P.push_scope()
    NKD = 12 if l0 else 32
    HF = 12 if l0 else 32
    upB = io["cfcB"] if l0 else io["wgB"]
    upB2 = io["cfcB"] if l0 else io["wuB"]
    up2_off = 1536 if l0 else 0
    dnB = io["cprojB"] if l0 else io["wdB"]
    DNROWS = 1536 if l0 else 4096
    gidx_sb = P.sbuf("sm_gidx_sb", [128, 4], I32)
    idxW = P.sbuf("sm_idxW", [128, 8], I32)
    idxD = P.sbuf("sm_idxD", [128, NKD], I32)
    idxf = P.sbuf("sm_idxf", [128, NKD], F32)
    ebc = P.sbuf("sm_ebc", [128, 1], F32)
    xgt = P.sbuf("sm_xgt", [128, D], BF16)
    xgf = P.sbuf("sm_xgf", [128, 8, 512], BF16)
    wgu = P.sbuf("sm_wgu", [128, 8, 1024], BF16)
    wdb = P.sbuf("sm_wdb", [128, D], BF16)
    h_e = P.sbuf("sm_h", [128, HF, 512], BF16)
    wsrow = P.sbuf("sm_wsrow", [1, 512], F32)
    wsbf = P.sbuf("sm_wsbf", [1, 512], BF16)
    bc = P.sbuf("sm_bc", [128, 512], BF16)
    t1 = P.sbuf("sm_t1", [128, 512], F32)
    t2 = P.sbuf("sm_t2", [128, 512], F32)
    osb_f = P.sbuf("sm_osbf", [128, 8, 512], F32)
    otm = P.sbuf("sm_otm", [128, D], F32)
    zb = P.sbuf("sm_zb", [128, D], F32)
    res = P.sbuf("sm_res", [128, 512], F32)
    osb = P.sbuf("sm_osb", [128, 128], F32)

    P.iop("vector", lambda e: e.memset(zb[:], 0.0), "dve")
    P.wait("gpsimd", "dve")
    for b in range(32):
        P.dma("gpsimd", io["accm"][b * 128:(b + 1) * 128, :], zb[:], "az")
    P.wait("gpsimd", "az")

    for j in range(N_TILES):
        ps_eb = P.palloc("sm_ps_eb", [128, 1], F32)
        P.wait("tensor", "dve")
        P.iop("tensor", lambda e, j=j: e.matmul(
            ps_eb[:], cs["ones1x128_f32"][:], tecol[0:1, j:j + 1], start=True, stop=True), "pe")
        P.wait("vector", "pe")
        P.iop("vector", lambda e: e.tensor_copy(out=ebc[:], in_=ps_eb[:]), "dve")
        P.op("vector", lambda e: e.drain())
        P.op("vector", lambda e: e.scalar_tensor_tensor(
            out=idxf[:, 0:8], in0=ebc[:].broadcast_to([128, 8]), scalar=float(D),
            in1=IOTA[:, 0:8], op0=ALU.mult, op1=ALU.add))
        P.op("vector", lambda e: e.drain())
        P.iop("vector", lambda e: e.tensor_copy(out=idxW[:], in_=idxf[:, 0:8]), "dve")
        P.op("vector", lambda e: e.scalar_tensor_tensor(
            out=idxf[:, 0:NKD], in0=ebc[:].broadcast_to([128, NKD]),
            scalar=float(DNROWS), in1=IOTA[:, 0:NKD], op0=ALU.mult, op1=ALU.add))
        P.op("vector", lambda e: e.drain())
        P.iop("vector", lambda e: e.tensor_copy(out=idxD[:], in_=idxf[:, 0:NKD]), "dve")
        P.op("vector", lambda e: e.drain())
        P.pfree(ps_eb)
        ps_tp2 = [P.palloc("sm_ps_t2a", [128, 128], BF16),
                  P.palloc("sm_ps_t2b", [128, 128], BF16)]
        P.wait("gpsimd", "dve")
        P.wait("gpsimd", "gsc")
        P.wait("gpsimd", "xtm")
        for a in range(4):
            P.dma("gpsimd", gidx_sb[:, a:a + 1],
                  io["gidx"][j * 512 + a * 128:j * 512 + (a + 1) * 128, :], "gi")
        P.wait("gpsimd", "gi")
        for sub in range(4):
            P.iop("gpsimd", lambda e, sub=sub: e.indirect_dma_start(
                out=xgt[:], out_offset=None,
                in_=io["xlntm"], in_offset=IndirectOffsetOnAxis(gidx_sb[:, sub:sub + 1], 0)),
                "gi", 16)
            P.wait("tensor", "gi")
            for fc in range(8):
                pt = ps_tp2[fc % 2]
                if sub + fc > 0:
                    P.wait("tensor", "act")
                P.iop("tensor", lambda e, fc=fc, pt=pt: e.transpose(
                    pt[:], xgt[:, fc * 128:(fc + 1) * 128], cs["ident"][:]), "pe")
                P.wait("scalar", "pe")
                P.iop("scalar", lambda e, fc=fc, sub=sub, pt=pt: e.activation(
                    out=xgf[:, fc, sub * 128:sub * 128 + 128], in_=pt[:], func=AF.Copy),
                    "act")
            P.wait("gpsimd", "act")
        P.pfree(ps_tp2[0], ps_tp2[1])
        ps_bc2 = P.palloc("sm_ps_bc2", [128, 512], F32)
        P.dma("gpsimd", wsrow[:],
              io["wslot"].rearrange("s one -> (one) (s)")[0:1, j * 512:(j + 1) * 512], "gi")
        P.wait("vector", "gi")
        P.wait("vector", "act")
        P.iop("vector", lambda e: e.tensor_copy(out=wsbf[:], in_=wsrow[:]), "dve")
        P.wait("tensor", "dve")
        P.wait("tensor", "act")
        P.iop("tensor", lambda e: e.matmul(
            ps_bc2[:], cs["ones1x128"][:], wsbf[:], start=True, stop=True), "pe")
        P.wait("scalar", "pe")
        P.iop("scalar", lambda e: e.activation(out=bc[:], in_=ps_bc2[:], func=AF.Copy), "act")
        ps_g = [P.palloc("sm_ps_g0", [128, 512], F32), P.palloc("sm_ps_g1", [128, 512], F32)]
        ps_u = [P.palloc("sm_ps_u0", [128, 512], F32), P.palloc("sm_ps_u1", [128, 512], F32)]
        for fb in range(HF // 4):
            P.wait("gpsimd", "pe")
            P.wait("gpsimd", "act")
            for kb in range(8):
                P.iop("gpsimd", lambda e, kb=kb, fb=fb: e.indirect_dma_start(
                    out=wgu[:, kb, 0:512], out_offset=None,
                    in_=upB.rearrange("r (b w) -> r b w", w=512)[:, fb, :],
                    in_offset=IndirectOffsetOnAxis(idxW[:, kb:kb + 1], 0),
                    element_offset=fb * 512), "w", 16) if False else \
                P.iop("gpsimd", lambda e, kb=kb, fb=fb: e.indirect_dma_start(
                    out=wgu[:, kb, 0:512], out_offset=None,
                    in_=upB,
                    in_offset=IndirectOffsetOnAxis(idxW[:, kb:kb + 1], 0),
                    element_offset=fb * 512), "w", 16)
                P.iop("gpsimd", lambda e, kb=kb, fb=fb: e.indirect_dma_start(
                    out=wgu[:, kb, 512:1024], out_offset=None,
                    in_=upB2,
                    in_offset=IndirectOffsetOnAxis(idxW[:, kb:kb + 1], 0),
                    element_offset=up2_off + fb * 512), "w", 16)
            P.wait("tensor", "w")
            if j == 0 and fb == 1:
                dbg_dump(P, "smx_wgu0", wgu[:], hard=True)
                dbg_dump(P, "smx_xgf0", xgf[:], hard=True)
                dbg_dump(P, "smx_bc0", bc[:], hard=True)
            for fc in range(4):
                pg = ps_g[fc % 2]
                pu = ps_u[fc % 2]
                fpc = fb * 4 + fc
                if fb + fc > 0:
                    P.wait("tensor", "dve")
                for k in range(8):
                    P.op("tensor", lambda e, k=k, fc=fc, pg=pg: e.matmul(
                        pg[:], wgu[:, k, fc * 128:fc * 128 + 128], xgf[:, k, :],
                        start=(k == 0), stop=(k == 7)))
                for k in range(8):
                    fn = lambda e, k=k, fc=fc, pu=pu: e.matmul(
                        pu[:], wgu[:, k, 512 + fc * 128:512 + fc * 128 + 128],
                        xgf[:, k, :], start=(k == 0), stop=(k == 7))
                    if k == 7:
                        P.iop("tensor", fn, "pe")
                    else:
                        P.op("tensor", fn)
                P.wait("scalar", "pe")
                P.wait("scalar", "dve")
                P.iop("scalar", lambda e, pg=pg: e.activation(
                    out=t1[:], in_=pg[:], func=AF.Silu), "act")
                P.wait("vector", "act")
                P.op("vector", lambda e, pu=pu: e.tensor_tensor(
                    out=t2[:], in0=pu[:], in1=bc[:], op=ALU.mult))
                P.op("vector", lambda e: e.drain())
                P.iop("vector", lambda e, fpc=fpc: e.tensor_tensor(
                    out=h_e[:, fpc, :], in0=t1[:], in1=t2[:], op=ALU.mult), "dve")
        P.pfree(ps_g[0], ps_g[1], ps_u[0], ps_u[1])
        P.pfree(ps_bc2)
        if j == 0:
            dbg_dump(P, "smx_h0", h_e[:], hard=True)
        ps_d = [P.palloc(f"sm_ps_d{i}", [128, 512], F32) for i in range(8)]
        P.wait("tensor", "dve")
        for kb in range(NKD):
            P.wait("gpsimd", "pe")
            P.iop("gpsimd", lambda e, kb=kb: e.indirect_dma_start(
                out=wdb[:], out_offset=None,
                in_=dnB, in_offset=IndirectOffsetOnAxis(idxD[:, kb:kb + 1], 0)), "w", 16)
            P.wait("tensor", "w")
            for fo in range(8):
                fn = lambda e, kb=kb, fo=fo: e.matmul(
                    ps_d[fo][:], wdb[:, fo * 128:(fo + 1) * 128], h_e[:, kb, :],
                    start=(kb == 0), stop=(kb == NKD - 1))
                if fo == 7:
                    P.iop("tensor", fn, "pe")
                else:
                    P.op("tensor", fn)
        P.wait("vector", "pe")
        for fo in range(8):
            P.iop("vector", lambda e, fo=fo: e.tensor_copy(
                out=osb_f[:, fo, :], in_=ps_d[fo][:]), "dve")
        P.pfree(*ps_d)
        if j == 0:
            dbg_dump(P, "smx_osbf0", osb_f[:], hard=True)
        ps_ot = [P.palloc("sm_ps_ot0", [128, 128], F32),
                 P.palloc("sm_ps_ot1", [128, 128], F32)]
        P.wait("tensor", "dve")
        for sub in range(4):
            P.wait("gpsimd", "sca")
            P.wait("vector", "sca")
            for fc in range(8):
                po = ps_ot[fc % 2]
                if sub + fc > 0:
                    P.wait("tensor", "dve")
                P.iop("tensor", lambda e, fc=fc, sub=sub, po=po: e.transpose(
                    po[:], osb_f[:, fc, sub * 128:sub * 128 + 128],
                    cs["ident_f32"][:]), "pe")
                P.wait("vector", "pe")
                P.iop("vector", lambda e, fc=fc, po=po: e.tensor_copy(
                    out=otm[:, fc * 128:(fc + 1) * 128], in_=po[:]), "dve")
            P.wait("gpsimd", "dve")
            P.iop("gpsimd", lambda e, sub=sub: e.indirect_dma_start(
                out=io["accm"], out_offset=IndirectOffsetOnAxis(gidx_sb[:, sub:sub + 1], 0),
                in_=otm[:], in_offset=None, compute_op=ALU.add), "sca", 16)
        P.pfree(ps_ot[0], ps_ot[1])

    if DEBUG:
        accd = P.sbuf("smx_accd", [128, 32 * D // 128], F32)
        P.wait("gpsimd", "sca")
        for b in range(4):
            P.dma("gpsimd", accd[:, b * 8 * 8:(b + 1) * 8 * 8].rearrange("p x -> p x")
                  if False else accd[:, b * 64:(b + 1) * 64],
                  io["accm"][b * 128:(b + 1) * 128, 0:64], "fd")
        P.wait("gpsimd", "fd")
        dbg_dump(P, "smx_accm", accd[:, 0:256], hard=True)
    ps_f = [P.palloc("sm_ps_f0", [128, 128], F32), P.palloc("sm_ps_f1", [128, 128], F32)]
    acct = P.sbuf("sm_acct", [128, D], F32)
    P.wait("gpsimd", "sca")
    for tc in range(32):
        P.wait("gpsimd", "dve")
        P.dma("gpsimd", acct[:], io["accm"][tc * 128:(tc + 1) * 128, :], "fi")
        P.wait("tensor", "fi")
        for fc in range(8):
            pf = ps_f[fc % 2]
            P.wait("sync", "dve")
            P.dma("sync", res[:, 0:128], X[fc * 128:(fc + 1) * 128, tc * 128:(tc + 1) * 128],
                  "xr")
            if tc + fc > 0:
                P.wait("tensor", "dve")
            P.iop("tensor", lambda e, fc=fc, pf=pf: e.transpose(
                pf[:], acct[:, fc * 128:(fc + 1) * 128], cs["ident_f32"][:]), "pe")
            P.wait("vector", "pe")
            P.wait("vector", "xr")
            P.wait("vector", "st")
            P.iop("vector", lambda e, pf=pf: e.tensor_tensor(
                out=osb[:], in0=pf[:], in1=res[:, 0:128], op=ALU.add), "dve")
            P.wait("gpsimd", "dve")
            P.dma("gpsimd", Xo[fc * 128:(fc + 1) * 128, tc * 128:(tc + 1) * 128],
                  osb[:], "st")
    P.pfree(ps_f[0], ps_f[1])
    P.pop_scope()
    P.wait("sync", "st")
    P.wait("gpsimd", "st")


def emit_phase12(P, cs, io):
    emit_moe_sparse(P, cs, io, 0)


def emit_phase16(P, cs, io):
    emit_moe_sparse(P, cs, io, 1)


_PHASES = {
    1: ("phase1", phase1_dram, phase1_host, emit_phase1),
    3: ("phase3", phase3_dram, phase3_host, emit_phase3),
    5: ("phase5", phase5_dram, phase5_host, emit_phase5),
    7: ("phase7", phase7_dram, phase7_host, emit_phase7),
    2: ("phase2", phase2_dram, phase2_host, emit_phase2),
    6: ("phase6", phase6_dram, phase6_host, emit_phase6),
    12: ("phase12", phase12_dram, phase2_host, emit_phase12),
    16: ("phase16", phase16_dram, phase6_host, emit_phase16),
}


def build_phase(idx):
    nc = bass.Bass("TRN2", target_bir_lowering=False, debug=False)
    ctx = ExitStack()
    c = const_inputs(nc)
    P = Prog(nc, ctx)
    _, iofn, _, emitfn = _PHASES[idx]
    io = iofn(nc)
    io["X"] = nc.dram_tensor("X_in", [D, NT], F32, kind="ExternalInput").ap()
    io["Xo"] = nc.dram_tensor("X_out", [D, NT], F32, kind="ExternalOutput").ap()
    cs = load_consts(P, c)
    P.push_scope()
    emitfn(P, cs, io)
    P.pop_scope()
    P.finalize()
    ctx.close()
    return nc


def build_fused():
    nc = bass.Bass("TRN2", target_bir_lowering=False, debug=False)
    ctx = ExitStack()
    c = const_inputs(nc)
    P = Prog(nc, ctx)
    ios = {}
    for idx in sorted(set(PHASE_SEQ)):
        ios[idx] = _PHASES[idx][1](nc)
    Xin = nc.dram_tensor("X_in", [D, NT], F32, kind="ExternalInput").ap()
    Xout = nc.dram_tensor("X_out", [D, NT], F32, kind="ExternalOutput").ap()
    XA = nc.dram_tensor("X_scratchA", [D, NT], F32).ap()
    XB = nc.dram_tensor("X_scratchB", [D, NT], F32).ap()
    cs = load_consts(P, c)
    n = len(PHASE_SEQ)
    for i, idx in enumerate(PHASE_SEQ):
        src_ap = Xin if i == 0 else (XA if i % 2 == 1 else XB)
        dst_ap = Xout if i == n - 1 else (XA if i % 2 == 0 else XB)
        io = dict(ios[idx])
        io["X"] = src_ap
        io["Xo"] = dst_ap
        P.wait("gpsimd", "st")
        P.wait("sync", "st")
        P.push_scope()
        _PHASES[idx][3](P, cs, io)
        P.pop_scope()
    P.finalize()
    ctx.close()
    return nc


def shard_stream(x):
    out = []
    for cid in range(NCORES):
        xs = x[cid * BC:(cid + 1) * BC].reshape(NT, D)
        out.append(np.ascontiguousarray(xs.T))
    return out


def unshard_stream(Xs):
    return np.concatenate(
        [Xc.T.reshape(BC, T, D) for Xc in Xs], axis=0).astype(np.float32)


PHASE_SEQ = [1, 12, 3, 12, 5, 16, 7, 16]
FUSED = True


def kernel(**inputs):
    consts = const_arrays()
    Xs = shard_stream(np.asarray(inputs["L0_attn_a_x"], np.float32))
    if FUSED:
        nc = build_fused()
        hosts = {}
        for idx in set(PHASE_SEQ):
            hosts[idx] = _PHASES[idx][2](inputs)
        in_maps = []
        for cid in range(NCORES):
            m = dict(consts)
            for idx in set(PHASE_SEQ):
                m.update(hosts[idx])
            m["X_in"] = Xs[cid]
            if 2 in PHASE_SEQ or 12 in PHASE_SEQ:
                m["p2_prevmask"] = (np.arange(8) < cid).astype(np.float32)[:, None]
            in_maps.append(m)
        res = run_bass_kernel_spmd(nc, in_maps, core_ids=list(range(NCORES)))
        return unshard_stream([r["X_out"] for r in res.results])
    progs = {}
    hosts = {}
    for idx in PHASE_SEQ:
        if idx not in progs:
            progs[idx] = build_phase(idx)
            hosts[idx] = _PHASES[idx][2](inputs)
        in_maps = []
        for cid in range(NCORES):
            m = dict(consts)
            m.update(hosts[idx])
            m["X_in"] = Xs[cid]
            if idx in (2, 12):
                m["p2_prevmask"] = (np.arange(8) < cid).astype(np.float32)[:, None]
            in_maps.append(m)
        res = run_bass_kernel_spmd(progs[idx], in_maps, core_ids=list(range(NCORES)))
        Xs = [r["X_out"] for r in res.results]
    return unshard_stream(Xs)
